# version 02
# baseline (speedup 1.0000x reference)
import os
from contextlib import ExitStack

import numpy as np
import ml_dtypes

import concourse.bass as bass
import concourse.bass_isa as bass_isa
import concourse.mybir as mybir
from concourse.bass_utils import run_bass_kernel_spmd

F32 = mybir.dt.float32
BF16 = mybir.dt.bfloat16
I32 = mybir.dt.int32
ALU = mybir.AluOpType
ACTF = mybir.ActivationFunctionType
AX = mybir.AxisListType.X

B, S, D, H, DH = 4, 4096, 1024, 16, 64
FF0, FF1 = 4096, 2816
T = 2048
NC = 8
DC = D // 128
QB = 512
NQB = T // QB
KCN = S // 128
FH = FF0 // 128
GH = FF1 // 128
WAVES = [(0, 6), (6, 12), (12, 16)]
EPS_LN = 1e-5
TP = T + 2
MAIN_BLOCKS = [(1, QB), (1 + QB, QB), (1 + 2 * QB, QB), (1 + 3 * QB, QB)]
KPH = int(os.environ.get("KPHASES", "7"))


def _prep_inputs(inputs):
    x = np.asarray(inputs["L0_attn_x"], dtype=np.float32)
    tr = lambda a: np.ascontiguousarray(np.asarray(a, np.float32).T)
    nat = lambda a: np.ascontiguousarray(np.asarray(a, np.float32))
    shared = dict(
        w1T=tr(inputs["L0_ffn_w1"]), w2T=tr(inputs["L0_ffn_w2"]),
        cw=nat(np.asarray(inputs["L0_ffn_conv_w"], np.float32)[:, 0, :]),
        cb=nat(np.asarray(inputs["L0_ffn_conv_b"], np.float32)[:, None]),
        pq0=nat(inputs["L0_attn_Wq"]), pk0=nat(inputs["L0_attn_Wk"]),
        pv0=nat(inputs["L0_attn_Wv"]), po0=nat(inputs["L0_attn_Wout"]),
        pq1=tr(inputs["L1_attn_wq"]), pk1=tr(inputs["L1_attn_wk"]),
        pv1=tr(inputs["L1_attn_wv"]), po1=tr(inputs["L1_attn_wo"]),
        wgT=tr(inputs["L1_ffn_w_gate"]), wuT=tr(inputs["L1_ffn_w_up"]),
        wdT=tr(inputs["L1_ffn_w_down"]),
        ident=np.eye(128, dtype=np.float32),
        onesq=np.ones((128, 128), dtype=np.float32),
        e2=np.repeat(np.eye(2, dtype=np.float32), 64, axis=1),
    )
    in_maps = []
    for c in range(NC):
        b, half = c // 2, c % 2
        r0 = half * T
        xT = np.ascontiguousarray(x[b, r0:r0 + T, :].T)
        halo = np.zeros((2, D), np.float32)
        if r0 - 1 >= 0:
            halo[0, :] = x[b, r0 - 1, :]
        if r0 + T < S:
            halo[1, :] = x[b, r0 + T, :]
        sel = np.zeros((128, 2), np.float32)
        sel[:, 0] = 1.0 if half == 1 else 0.0
        sel[:, 1] = 1.0 if half == 0 else 0.0
        kidx = np.arange(S).reshape(KCN, 128)
        qidx = r0 + np.arange(T).reshape(NQB, QB)
        m = (kidx[None, :, :, None] <= qidx[:, None, None, :])
        m = np.ascontiguousarray(
            m.reshape(NQB * KCN * 128, QB).astype(ml_dtypes.bfloat16))
        d = dict(shared)
        d.update(xT=xT, xhalo=halo, sel=sel, lmask=m)
        in_maps.append(d)
    return in_maps


SEMS = ["DS", "DA", "DG", "CV", "CS", "CT", "CG", "PH", "LA", "LB", "SA", "SB"]
ENGS = ["sync", "vector", "scalar", "tensor", "gpsimd"]


class _APCM:
    def __init__(self, cm):
        self.cm = cm

    def __enter__(self):
        return self.cm.__enter__().ap()

    def __exit__(self, *a):
        return self.cm.__exit__(*a)


class Ctx:
    def __init__(self, nc, stack):
        self.nc = nc
        self.sem = {n: stack.enter_context(nc.semaphore(n)) for n in SEMS}
        self.cnt = {n: 0 for n in SEMS}
        self.ph_target = 0
        self.uid = 0

    def name(self, base):
        self.uid += 1
        return f"{base}_{self.uid}"

    def sbt(self, st, name, shape, dt=F32):
        return st.enter_context(_APCM(self.nc.sbuf_tensor(self.name(name), shape, dt)))

    def inc(self, name, k=1):
        self.cnt[name] += k
        return self.cnt[name]

    def ld(self, eng, dst, src, q="DS", slow=False):
        eng.dma_start(out=dst, in_=src, allow_slow_non_contiguous=slow)\
            .then_inc(self.sem[q], 16)
        return self.inc(q, 16)

    def phase(self, emitters, order=None):
        order = list(order or ENGS)
        for nme in ENGS:
            if nme not in order:
                order.append(nme)
        with self.nc.Block() as blk:
            start = self.ph_target
            for nme in order:
                fn = emitters.get(nme)
                dec = getattr(blk, nme)

                def wrap(e, fn=fn, start=start):
                    e.wait_ge(self.sem["PH"], start)
                    if fn is not None:
                        fn(e)
                    e.nop().then_inc(self.sem["PH"], 1)

                dec(wrap)
            self.ph_target = start + len(ENGS)


def fp4_qdq(e, xin, xout, scr, w, dly=None):
    a, t, m, r, v, q, u, s = (scr[k] for k in "atmrvqus")
    w16, w32 = w // 16, w // 32
    e.tensor_scalar_mul(a[:, :w], xin[:, :w], -1.0)
    e.tensor_max(a[:, :w], a[:, :w], xin[:, :w])
    e.transpose(t[:, :w], a[:, :w])
    e.tensor_reduce(m[:, :w16],
                    t[:, :w].rearrange("p (a b c) -> p a b c", b=2, c=16),
                    AX, ALU.max)
    if dly is not None:
        dly(e)
    e.tensor_scalar(m[:, :w16], m[:, :w16], 1.0 / 6.0, 1e-12, ALU.mult, ALU.max)
    if dly is not None:
        dly(e)
    e.reciprocal(m[:, w16:2 * w16], m[:, :w16])
    if dly is not None:
        dly(e)
    e.tensor_copy(
        r[:, :w].rearrange("p (a b c) -> p a b c", b=2, c=16),
        m[:, w16:2 * w16].rearrange("p (a b) -> p a b", b=2).unsqueeze(3)
        .broadcast_to([128, w32, 2, 16]))
    e.transpose(v[:, :w], r[:, :w])
    e.tensor_copy(
        r[:, :w].rearrange("p (a b c) -> p a b c", b=2, c=16),
        m[:, :w16].rearrange("p (a b) -> p a b", b=2).unsqueeze(3)
        .broadcast_to([128, w32, 2, 16]))
    e.transpose(s[:, :w], r[:, :w])
    e.tensor_tensor(v[:, :w], a[:, :w], v[:, :w], ALU.mult)
    e.tensor_single_scalar(q[:, :w], v[:, :w], 0.25, ALU.is_gt)
    for c in (0.75, 1.25, 1.75):
        e.scalar_tensor_tensor(q[:, :w], v[:, :w], c, q[:, :w], ALU.is_gt, ALU.add)
    e.tensor_single_scalar(u[:, :w], v[:, :w], 2.5, ALU.is_gt)
    e.scalar_tensor_tensor(u[:, :w], v[:, :w], 3.5, u[:, :w], ALU.is_gt, ALU.add)
    e.scalar_tensor_tensor(q[:, :w], q[:, :w], 0.5, u[:, :w], ALU.mult, ALU.add)
    e.tensor_scalar(u[:, :w], v[:, :w], 5.0, 2.0, ALU.is_gt, ALU.mult)
    e.tensor_tensor(q[:, :w], q[:, :w], u[:, :w], ALU.add)
    e.tensor_scalar(u[:, :w], xin[:, :w], 0.0, None, ALU.is_ge)
    e.tensor_scalar(u[:, :w], u[:, :w], 2.0, 1.0, ALU.mult, ALU.subtract)
    e.tensor_tensor(q[:, :w], q[:, :w], u[:, :w], ALU.mult)
    e.tensor_tensor(xout[:, :w], q[:, :w], s[:, :w], ALU.mult)


def round_int_bf16(e, v, y0, y0f, rr, out, w):
    e.tensor_copy(y0[:, :w], v[:, :w])
    e.tensor_copy(y0f[:, :w], y0[:, :w])
    e.tensor_tensor(rr[:, :w], v[:, :w], y0f[:, :w], ALU.subtract)
    e.scalar_tensor_tensor(y0f[:, :w], rr[:, :w], 0.5, y0f[:, :w], ALU.is_gt, ALU.add)
    e.scalar_tensor_tensor(y0f[:, :w], rr[:, :w], -0.5, y0f[:, :w], ALU.is_ge, ALU.add)
    return e.tensor_scalar(out[:, :w], y0f[:, :w], 1.0, None, ALU.subtract)


def build_program():
    nc = bass.Bass("TRN2", target_bir_lowering=False, detect_race_conditions=False)
    stack = ExitStack()
    ctx = Ctx(nc, stack)
    sem = ctx.sem

    inp = {}
    for name, shape in [
        ("xT", [D, T]), ("xhalo", [2, D]), ("sel", [128, 2]), ("ident", [128, 128]),
        ("w1T", [D, FF0]), ("w2T", [FF0, D]), ("cw", [FF0, 3]), ("cb", [FF0, 1]),
        ("pq0", [D, D]), ("pk0", [D, D]), ("pv0", [D, D]), ("po0", [D, D]),
        ("pq1", [D, D]), ("pk1", [D, D]), ("pv1", [D, D]), ("po1", [D, D]),
        ("wgT", [D, FF1]), ("wuT", [D, FF1]), ("wdT", [FF1, D]),
        ("e2", [2, 128]), ("onesq", [128, 128]),
    ]:
        inp[name] = nc.dram_tensor(name, shape, F32, kind="ExternalInput").ap()
    inp["lmask"] = nc.dram_tensor("lmask", [NQB * KCN * 128, QB], BF16,
                                  kind="ExternalInput").ap()
    yT = nc.dram_tensor("yT", [D, T], F32, kind="ExternalOutput").ap()

    def dram(name, shape, dt=F32):
        return nc.dram_tensor(name, shape, dt, kind="Internal").ap()

    Xs = dram("Xs", [D, T])
    HALO3 = dram("HALO3", [2, D])
    LNH = dram("LNH", [D, T], BF16)
    LNL = dram("LNL", [D, T], BF16)
    HST = dram("HST", [FF0, QB])
    w1q = dram("w1q", [D, FF0], BF16)
    w2q = dram("w2q", [FF0, D], BF16)
    pwq = {}
    for nm in ["pq0", "pk0", "pv0", "po0", "pq1", "pk1", "pv1", "po1"]:
        pwq[nm + "h"] = dram(nm + "h", [D, D], BF16)
        pwq[nm + "l"] = dram(nm + "l", [D, D], BF16)
    wgqh = dram("wgqh", [D, FF1], BF16); wgql = dram("wgql", [D, FF1], BF16)
    wuqh = dram("wuqh", [D, FF1], BF16); wuql = dram("wuql", [D, FF1], BF16)
    wdqh = dram("wdqh", [FF1, D], BF16); wdql = dram("wdql", [FF1, D], BF16)
    kpin = dram("kpin", [D, 2 * T], BF16)
    kpout = dram("kpout", [2 * D, 2 * T], BF16)
    vain = dram("vain", [T, H * 65])
    vaout = dram("vaout", [S, H * 65])
    bgin = dram("bgin", [D, 2])
    bgout = dram("bgout", [2 * D, 2])
    Qt = dram("Qt", [D, T])
    hqh = dram("hqh", [FF1, T], BF16)
    hql = dram("hql", [FF1, T], BF16)

    A = lambda n, s_, d=F32: nc.alloc_sbuf_tensor(n, s_, d).ap()
    sb_ident = A("sb_ident", [128, 128])
    sb_sel = A("sb_sel", [128, 2])
    sb_ones1 = A("sb_ones1", [1, 128])
    sb_onesq = A("sb_onesq", [128, 128])
    sb_onesc = A("sb_onesc", [128, 1])
    sb_e2 = A("sb_e2", [2, 128])
    sb_scal = A("sb_scal", [128, 8])
    cwsb = A("cwsb", [128, FH * 3])
    cbsb = A("cbsb", [128, FH])

    ps = [nc.alloc_psum_tensor(f"ps{i}", [128, 512], F32).ap() for i in range(8)]
    sb_dly = A("sb_dly", [1, 128])
    asb_dly = A("asb_dly", [1, 128])

    def dve_delay(e):
        e.tensor_copy(sb_dly[0:1, :], sb_ident[0:1, :])
        e.tensor_copy(sb_dly[0:1, :], sb_ident[0:1, :])

    def act_delay(e):
        e.activation(asb_dly[0:1, :], sb_ident[0:1, :], ACTF.Copy)
        e.activation(asb_dly[0:1, :], sb_ident[0:1, :], ACTF.Copy)
    replica = [[2 * i, 2 * i + 1] for i in range(NC // 2)]

    def p0_sync(e):
        ctx.ld(e, sb_ident[:], inp["ident"])
        ctx.ld(e, sb_sel[:], inp["sel"])
        ctx.ld(e, sb_onesq[:], inp["onesq"])
        ctx.ld(e, sb_ones1[:], inp["onesq"][0:1, :])
        ctx.ld(e, sb_onesc[:], inp["onesq"][:, 0:1], slow=True)
        ctx.ld(e, cwsb.rearrange("p (a k) -> p a k", k=3),
               inp["cw"].rearrange("(a p) k -> a p k", p=128).transpose([1, 0, 2]))
        ctx.ld(e, cbsb[:],
               inp["cb"].rearrange("(a p) k -> a p k", p=128)
               .transpose([1, 0, 2]).rearrange("p a k -> p (a k)"), slow=True)
        w = ctx.ld(e, sb_e2[:], inp["e2"])
        e.wait_ge(sem["DS"], w)

    def p0_vec(e):
        pass

    ctx.phase({"sync": p0_sync, "vector": p0_vec})

    def bitnet_quant(src, dst, rows, cols, scol):
        nch = rows // 128
        with ExitStack() as st:
            wb = [ctx.sbt(st, "bnw", [128, cols]) for _ in range(2)]
            qb_ = [ctx.sbt(st, "bnq", [128, cols], BF16) for _ in range(2)]
            ab = ctx.sbt(st, "bna", [128, cols])
            sg = ctx.sbt(st, "bns", [128, cols])
            partb = ctx.sbt(st, "bnpb", [128, 32])
            bc = ctx.sbt(st, "bnbc", [128, 1])
            scl = ctx.sbt(st, "bnscl", [128, 1])

            ds0, cv0 = ctx.cnt["DS"], ctx.cnt["CV"]

            la0, lb0 = ctx.cnt["LA"], ctx.cnt["LB"]

            def p_sync(e):
                for k in range(nch):
                    if k >= 2:
                        e.wait_ge(sem["CV"], cv0 + (k - 1))
                    ctx.ld(e, wb[k % 2][:], src[128 * k:128 * (k + 1), :],
                           q="LA" if k % 2 == 0 else "LB")
                e.wait_ge(sem["LA"], ctx.cnt["LA"])
                e.wait_ge(sem["LB"], ctx.cnt["LB"])

            def p_vec(e):
                for k in range(nch):
                    e.wait_ge(sem["LA" if k % 2 == 0 else "LB"],
                              (la0 if k % 2 == 0 else lb0) + 16 * (k // 2 + 1))
                    e.reduce_sum(partb[:, k:k + 1], wb[k % 2][:], AX,
                                 apply_absolute_value=True)\
                        .then_inc(sem["CV"], 1)
                    ctx.inc("CV")

            ct0 = ctx.cnt["CT"]

            def p_tens(e):
                e.wait_ge(sem["CV"], cv0 + nch)
                e.matmul(ps[6][0:1, 0:nch], sb_onesc[:], partb[:, 0:nch],
                         start=True, stop=True).then_inc(sem["CT"], 1)
                ctx.inc("CT")
                e.wait_ge(sem["CV"], cv0 + nch + 1)
                e.matmul(ps[7][0:128, 0:1], sb_onesq[:], bc[:],
                         start=True, stop=True).then_inc(sem["CT"], 1)
                ctx.inc("CT")

            def p_vec_b(e):
                e.wait_ge(sem["CT"], ct0 + 1)
                dve_delay(e)
                e.tensor_scalar_mul(bc[:], sb_ident[:, 0:1], 0.0)
                e.reduce_sum(bc[0:1, 0:1], ps[6][0:1, 0:nch], AX)\
                    .then_inc(sem["CV"], 1)
                ctx.inc("CV")
                e.wait_ge(sem["CT"], ct0 + 2)
                dve_delay(e)
                e.tensor_scalar(scl[:], ps[7][0:128, 0:1], 1.0 / (rows * cols),
                                1e-5, ALU.mult, ALU.max)
                dve_delay(e)
                e.tensor_scalar_mul(sb_scal[:, scol:scol + 1], scl[:], 0.5)
                e.tensor_scalar_mul(sb_scal[:, scol + 1:scol + 2], scl[:],
                                    1.0 / 127.0)
                dve_delay(e)
                e.nop().then_inc(sem["CV"], 1)
                ctx.inc("CV")

            def _pv(e):
                p_vec(e)
                p_vec_b(e)

            ctx.phase({"sync": p_sync, "vector": _pv, "tensor": p_tens},
                      order=["sync", "vector", "tensor"])

            if os.environ.get("DBGBN") == "1" and scol == 0:
                def pbn_sync(e):
                    ctx.ld(e, Xs[640:768, 0:nch], partb[:, 0:nch], slow=True)
                    ctx.ld(e, Xs[640:768, 8:9], bc[:], slow=True)
                    ctx.ld(e, Xs[640:768, 9:10], scl[:], slow=True)
                    ctx.ld(e, Xs[640:768, 10:12], sb_scal[:, 0:2], slow=True)
                    e.wait_ge(sem["DS"], ctx.cnt["DS"])

                ctx.phase({"sync": pbn_sync})

            cv1 = ctx.cnt["CV"]
            la1, lb1 = ctx.cnt["LA"], ctx.cnt["LB"]

            def p_sync2(e):
                for k in range(nch):
                    if k >= 2:
                        e.wait_ge(sem["CV"], cv1 + (k - 1))
                    ctx.ld(e, wb[k % 2][:], src[128 * k:128 * (k + 1), :],
                           q="LA" if k % 2 == 0 else "LB")
                e.wait_ge(sem["LA"], ctx.cnt["LA"])
                e.wait_ge(sem["LB"], ctx.cnt["LB"])

            def p_vec2(e):
                for k in range(nch):
                    i = k % 2
                    e.wait_ge(sem["LA" if k % 2 == 0 else "LB"],
                              (la1 if k % 2 == 0 else lb1) + 16 * (k // 2 + 1))
                    e.tensor_scalar_mul(ab[:], wb[i][:], -1.0)
                    e.tensor_max(ab[:], ab[:], wb[i][:])
                    e.tensor_scalar(sg[:], wb[i][:], 0.0, None, ALU.is_ge)
                    e.tensor_scalar(sg[:], sg[:], 2.0, 1.0, ALU.mult, ALU.subtract)
                    e.tensor_single_scalar(ab[:], ab[:], sb_scal[:, scol:scol + 1],
                                           ALU.is_gt)
                    e.tensor_tensor(qb_[i][:], ab[:], sg[:], ALU.mult)\
                        .then_inc(sem["CV"], 1)
                    ctx.inc("CV")

            def p_scal2(e):
                for k in range(nch):
                    e.wait_ge(sem["CV"], cv1 + (k + 1))
                    ctx.ld(e, dst[128 * k:128 * (k + 1), :], qb_[k % 2][:], q="DA")
                e.wait_ge(sem["DA"], ctx.cnt["DA"])

            ctx.phase({"sync": p_sync2, "vector": p_vec2, "scalar": p_scal2})

    def hilo_split(names):
        with ExitStack() as st:
            wb = [ctx.sbt(st, "sp", [128, D]) for _ in range(2)]
            hb = [ctx.sbt(st, "sph", [128, D], BF16) for _ in range(2)]
            lb = [ctx.sbt(st, "spl", [128, D], BF16) for _ in range(2)]
            cv0 = ctx.cnt["CV"]
            la0, lb0 = ctx.cnt["LA"], ctx.cnt["LB"]

            def p_sync(e):
                k = 0
                for nm in names:
                    for dc in range(DC):
                        if k >= 2:
                            e.wait_ge(sem["CV"], cv0 + (k - 1))
                        ctx.ld(e, wb[k % 2][:], inp[nm][128 * dc:128 * (dc + 1), :],
                               q="LA" if k % 2 == 0 else "LB")
                        k += 1
                e.wait_ge(sem["LA"], ctx.cnt["LA"])
                e.wait_ge(sem["LB"], ctx.cnt["LB"])

            def p_vec(e):
                for k in range(len(names) * DC):
                    i = k % 2
                    e.wait_ge(sem["LA" if k % 2 == 0 else "LB"],
                              (la0 if k % 2 == 0 else lb0) + 16 * (k // 2 + 1))
                    e.tensor_copy(hb[i][:], wb[i][:])
                    e.tensor_tensor(lb[i][:], wb[i][:], hb[i][:], ALU.subtract)\
                        .then_inc(sem["CV"], 1)
                    ctx.inc("CV")

            def p_scal(e):
                k = 0
                for nm in names:
                    for dc in range(DC):
                        e.wait_ge(sem["CV"], cv0 + (k + 1))
                        r_ = slice(128 * dc, 128 * (dc + 1))
                        ctx.ld(e, pwq[nm + "h"][r_, :], hb[k % 2][:], q="DA")
                        ctx.ld(e, pwq[nm + "l"][r_, :], lb[k % 2][:], q="DA")
                        k += 1
                e.wait_ge(sem["DA"], ctx.cnt["DA"])

            ctx.phase({"sync": p_sync, "vector": p_vec, "scalar": p_scal})

    def fp4_weights(src, dsth, dstl, rows, cols):
        nch = rows // 128
        with ExitStack() as st:
            wb = [ctx.sbt(st, "f4w", [128, cols]) for _ in range(2)]
            hb = [ctx.sbt(st, "f4h", [128, cols], BF16) for _ in range(2)]
            lb = [ctx.sbt(st, "f4l", [128, cols], BF16) for _ in range(2)]
            xq = ctx.sbt(st, "f4xq", [128, cols])
            scr = {k: ctx.sbt(st, "f4" + k, [128, cols]) for k in "atrvqus"}
            scr["m"] = ctx.sbt(st, "f4m", [128, 2 * (cols // 16)])
            cv0 = ctx.cnt["CV"]
            la0, lb0 = ctx.cnt["LA"], ctx.cnt["LB"]

            def p_sync(e):
                for k in range(nch):
                    if k >= 2:
                        e.wait_ge(sem["CV"], cv0 + (k - 1))
                    ctx.ld(e, wb[k % 2][:], src[128 * k:128 * (k + 1), :],
                           q="LA" if k % 2 == 0 else "LB")
                e.wait_ge(sem["LA"], ctx.cnt["LA"])
                e.wait_ge(sem["LB"], ctx.cnt["LB"])

            def p_vec(e):
                for k in range(nch):
                    i = k % 2
                    e.wait_ge(sem["LA" if k % 2 == 0 else "LB"],
                              (la0 if k % 2 == 0 else lb0) + 16 * (k // 2 + 1))
                    fp4_qdq(e, wb[i], xq, scr, cols, dve_delay)
                    e.tensor_copy(hb[i][:], xq[:])
                    e.tensor_tensor(lb[i][:], xq[:], hb[i][:], ALU.subtract)\
                        .then_inc(sem["CV"], 1)
                    ctx.inc("CV")

            def p_scal(e):
                for k in range(nch):
                    e.wait_ge(sem["CV"], cv0 + (k + 1))
                    r_ = slice(128 * k, 128 * (k + 1))
                    ctx.ld(e, dsth[r_, :], hb[k % 2][:], q="DA")
                    ctx.ld(e, dstl[r_, :], lb[k % 2][:], q="DA")
                e.wait_ge(sem["DA"], ctx.cnt["DA"])

            ctx.phase({"sync": p_sync, "vector": p_vec, "scalar": p_scal})

    def l0_ffn(xsrc, halosrc, mask_halo, xdst):
        with ExitStack() as st:
            row = ctx.sbt(st, "ffrow", [1, 2 * TP])
            rsb = ctx.sbt(st, "ffrs", [128, TP])
            vv = ctx.sbt(st, "ffvv", [128, QB + 2])
            y0 = ctx.sbt(st, "ffy0", [128, QB + 2], I32)
            y0f = ctx.sbt(st, "ffy0f", [128, QB + 2])
            rr = ctx.sbt(st, "ffrr", [128, QB + 2])
            xq = ctx.sbt(st, "ffxq", [128, DC * TP], BF16)
            tr1 = ctx.sbt(st, "fftr", [128, 16])
            s16 = ctx.sbt(st, "s16", [16, 128])
            rseg = [ctx.sbt(st, "rseg", [128, 512]) for _ in range(2)]

            def pZr_vec(e):
                for i in range(2):
                    for j in range(4):
                        e.tensor_scalar_mul(rseg[i][:, 128 * j:128 * (j + 1)],
                                            sb_ident[:], 0.0)

            ctx.phase({"vector": pZr_vec})
            st2 = ExitStack()
            _mbst = st if os.environ.get("CMODE") == "mbouter" else st2
            xc = ctx.sbt(st2, "ffx", [128, TP])
            x2 = ctx.sbt(st2, "ffx2", [128, TP])
            lnc = ctx.sbt(st2, "ffln", [128, TP])
            mxa = ctx.sbt(st2, "ffmx", [128, TP])
            mb = ctx.sbt(_mbst, "ffmb", [128, TP])
            rb = ctx.sbt(_mbst, "ffrb", [128, TP])
            sxb = ctx.sbt(st2, "ffsx", [128, TP])

            ds0, cs0, ct0 = ctx.cnt["DS"], ctx.cnt["CS"], ctx.cnt["CT"]

            def pA_sync(e):
                for dc in range(DC):
                    if dc >= 1:
                        e.wait_ge(sem["CT"], ct0 + dc)
                    chunk_load(e, xc, xsrc, halosrc, dc)

            def pA_scal(e):
                for dc in range(DC):
                    e.wait_ge(sem["DS"], ds0 + 48 * (dc + 1))
                    if mask_halo:
                        e.activation(xc[:, 0:1], xc[:, 0:1], ACTF.Copy,
                                     scale=sb_sel[:, 0:1])
                        e.activation(xc[:, TP - 1:TP], xc[:, TP - 1:TP], ACTF.Copy,
                                     scale=sb_sel[:, 1:2])
                    e.activation(x2[:], xc[:], ACTF.Square).then_inc(sem["CS"], 1)
                    ctx.inc("CS")

            def pA_tens(e):
                for dc in range(DC):
                    e.wait_ge(sem["CS"], cs0 + dc + 1)
                    stt, spp = (dc == 0), (dc == DC - 1)
                    for bi, (o, w) in enumerate(MAIN_BLOCKS):
                        e.matmul(ps[bi][0:128, 0:w], sb_onesq[:], xc[:, o:o + w],
                                 start=stt, stop=spp)
                        e.matmul(ps[4 + bi][0:128, 0:w], sb_onesq[:],
                                 x2[:, o:o + w], start=stt, stop=spp)\
                            .then_inc(sem["CT"], 1) if bi == 3 else \
                            e.matmul(ps[4 + bi][0:128, 0:w], sb_onesq[:],
                                     x2[:, o:o + w], start=stt, stop=spp)
                    ctx.inc("CT")

            ctx.phase({"sync": pA_sync, "scalar": pA_scal, "tensor": pA_tens},
                      order=["sync", "scalar", "tensor"])

            cvB, csB = ctx.cnt["CV"], ctx.cnt["CS"]

            def pB_vec(e):
                for bi, (o, w) in enumerate(MAIN_BLOCKS):
                    e.tensor_scalar_mul(mb[:, o:o + w], ps[bi][0:128, 0:w], 1.0 / D)
                    e.tensor_scalar_mul(rb[:, o:o + w], ps[4 + bi][0:128, 0:w],
                                        1.0 / D)
                e.tensor_tensor(lnc[:, 1:T + 1], mb[:, 1:T + 1], mb[:, 1:T + 1],
                                ALU.mult)
                e.tensor_tensor(rb[:, 1:T + 1], rb[:, 1:T + 1], lnc[:, 1:T + 1],
                                ALU.subtract)
                e.tensor_scalar_add(rb[:, 1:T + 1], rb[:, 1:T + 1], EPS_LN)\
                    .then_inc(sem["CV"], 1)
                ctx.inc("CV")

            def pB_scal(e):
                e.wait_ge(sem["CV"], cvB + 1)
                e.activation(rb[:, 1:T + 1], rb[:, 1:T + 1], ACTF.Sqrt)\
                    .then_inc(sem["CS"], 1)
                ctx.inc("CS")

            def pB_vec2(e):
                e.wait_ge(sem["CS"], csB + 1)
                for o, w in MAIN_BLOCKS:
                    e.reciprocal(rb[:, o:o + w], rb[:, o:o + w])
                e.nop().then_inc(sem["CV"], 1)
                ctx.inc("CV")

            def _pB_v(e):
                pB_vec(e)
                pB_vec2(e)

            ctx.phase({"vector": _pB_v, "scalar": pB_scal},
                      order=["vector", "scalar"])

            ds2, cs2, ct2, cv2 = (ctx.cnt[k] for k in ["DS", "CS", "CT", "CV"])

            def pA2_sync(e):
                for dc in range(DC):
                    if dc >= 1:
                        e.wait_ge(sem["CT"], ct2 + dc)
                    r_ = slice(128 * dc, 128 * (dc + 1))
                    ctx.ld(e, xc[:, 0:1],
                           halosrc[0:1, r_].rearrange("a (p b) -> (a p) b", b=1))
                    ctx.ld(e, xc[:, TP - 1:TP],
                           halosrc[1:2, r_].rearrange("a (p b) -> (a p) b", b=1))

            def pA2_scal(e):
                for dc in range(DC):
                    e.wait_ge(sem["DS"], ds2 + 32 * (dc + 1))
                    if mask_halo:
                        e.activation(xc[:, 0:1], xc[:, 0:1], ACTF.Copy,
                                     scale=sb_sel[:, 0:1])
                        e.activation(xc[:, TP - 1:TP], xc[:, TP - 1:TP], ACTF.Copy,
                                     scale=sb_sel[:, 1:2])
                    e.activation(x2[:, 0:1], xc[:, 0:1], ACTF.Square)
                    e.activation(x2[:, TP - 1:TP], xc[:, TP - 1:TP], ACTF.Square)\
                        .then_inc(sem["CS"], 1)
                    ctx.inc("CS")

            def pA2_tens(e):
                for dc in range(DC):
                    e.wait_ge(sem["CS"], cs2 + dc + 1)
                    stt, spp = (dc == 0), (dc == DC - 1)
                    e.matmul(ps[0][0:128, 0:1], sb_onesq[:], xc[:, 0:1],
                             start=stt, stop=spp)
                    e.matmul(ps[1][0:128, 0:1], sb_onesq[:], xc[:, TP - 1:TP],
                             start=stt, stop=spp)
                    e.matmul(ps[2][0:128, 0:1], sb_onesq[:], x2[:, 0:1],
                             start=stt, stop=spp)
                    e.matmul(ps[3][0:128, 0:1], sb_onesq[:], x2[:, TP - 1:TP],
                             start=stt, stop=spp).then_inc(sem["CT"], 1)
                    ctx.inc("CT")

            ctx.phase({"sync": pA2_sync, "scalar": pA2_scal, "tensor": pA2_tens},
                      order=["sync", "scalar", "tensor"])

            cvB2, csB2 = ctx.cnt["CV"], ctx.cnt["CS"]

            def pB2_vec(e):
                for j, col in enumerate((0, TP - 1)):
                    e.tensor_scalar_mul(mb[:, col:col + 1], ps[j][0:128, 0:1],
                                        1.0 / D)
                    e.tensor_scalar_mul(rb[:, col:col + 1], ps[2 + j][0:128, 0:1],
                                        1.0 / D)
                    e.tensor_tensor(lnc[:, col:col + 1], mb[:, col:col + 1],
                                    mb[:, col:col + 1], ALU.mult)
                    e.tensor_tensor(rb[:, col:col + 1], rb[:, col:col + 1],
                                    lnc[:, col:col + 1], ALU.subtract)
                    e.tensor_scalar_add(rb[:, col:col + 1], rb[:, col:col + 1],
                                        EPS_LN)
                e.nop().then_inc(sem["CV"], 1)
                ctx.inc("CV")

            def pB2_scal(e):
                e.wait_ge(sem["CV"], cvB2 + 1)
                e.activation(rb[:, 0:1], rb[:, 0:1], ACTF.Sqrt)
                e.activation(rb[:, TP - 1:TP], rb[:, TP - 1:TP], ACTF.Sqrt)\
                    .then_inc(sem["CS"], 1)
                ctx.inc("CS")

            def pB2_vec2(e):
                e.wait_ge(sem["CS"], csB2 + 1)
                e.reciprocal(rb[:, 0:1], rb[:, 0:1])
                e.reciprocal(rb[:, TP - 1:TP], rb[:, TP - 1:TP])
                e.nop().then_inc(sem["CV"], 1)
                ctx.inc("CV")

            def _pB2_v(e):
                pB2_vec(e)
                pB2_vec2(e)

            ctx.phase({"vector": _pB2_v, "scalar": pB2_scal},
                      order=["vector", "scalar"])
            if os.environ.get("FFCUT") == "C":
                st2.close()
                return

            dsD, csD, cvD = ctx.cnt["DS"], ctx.cnt["CS"], ctx.cnt["CV"]

            def pD_sync(e):
                for dc in range(DC):
                    if dc >= 1:
                        e.wait_ge(sem["CV"], cvD + 2 * dc)
                    chunk_load(e, xc, xsrc, halosrc, dc)

            def pD_scal(e):
                for dc in range(DC):
                    e.wait_ge(sem["DS"], dsD + 48 * (dc + 1))
                    if mask_halo:
                        e.activation(xc[:, 0:1], xc[:, 0:1], ACTF.Copy,
                                     scale=sb_sel[:, 0:1])
                        e.activation(xc[:, TP - 1:TP], xc[:, TP - 1:TP], ACTF.Copy,
                                     scale=sb_sel[:, 1:2]).then_inc(sem["CS"], 1)
                    else:
                        e.nop().then_inc(sem["CS"], 1)
                    ctx.inc("CS")

            def pD_vec(e):
                for dc in range(DC):
                    e.wait_ge(sem["CS"], csD + dc + 1)
                    e.tensor_tensor(lnc[:], xc[:], mb[:], ALU.subtract)
                    e.tensor_tensor(lnc[:], lnc[:], rb[:], ALU.mult)\
                        .then_inc(sem["CV"], 1)
                    ctx.inc("CV")
                    e.tensor_scalar_mul(x2[:], lnc[:], -1.0)
                    e.tensor_max(lnc[:], lnc[:], x2[:])
                    if dc == 0:
                        e.tensor_copy(mxa[:], lnc[:]).then_inc(sem["CV"], 1)
                    else:
                        e.tensor_max(mxa[:], mxa[:], lnc[:]).then_inc(sem["CV"], 1)
                    ctx.inc("CV")

            ctx.phase({"sync": pD_sync, "scalar": pD_scal, "vector": pD_vec},
                      order=["sync", "scalar", "vector"])
            if os.environ.get("FFCUT") == "D":
                st2.close()
                return

            ctE, cvE, cgE = ctx.cnt["CT"], ctx.cnt["CV"], ctx.cnt["CG"]

            def pE_tens(e):
                for tb in range(16):
                    cseg = slice(1 + 128 * tb, 1 + 128 * (tb + 1))
                    e.transpose(ps[tb % 2][:, 0:128], mxa[:, cseg], sb_ident[:]).then_inc(sem["CT"], 1)
                    ctx.inc("CT")
                    if tb >= 1:
                        e.wait_ge(sem["CV"], cvE + tb)
                e.transpose(ps[3][0:1, 0:128], mxa[:, 0:1], sb_ident[:])
                e.transpose(ps[4][0:1, 0:128], mxa[:, TP - 1:TP], sb_ident[:]).then_inc(sem["CT"], 1)
                ctx.inc("CT")

            def pE_gps(e):
                e.nop().then_inc(sem["CG"], 1)
                ctx.inc("CG")

            def pE_vec(e):
                for tb in range(16):
                    e.wait_ge(sem["CT"], ctE + tb + 1)
                    dve_delay(e)
                    e.tensor_reduce(tr1[:, tb:tb + 1], ps[tb % 2][:, 0:128],
                                    AX, ALU.max).then_inc(sem["CV"], 1)
                    ctx.inc("CV")
                e.wait_ge(sem["CT"], ctE + 17)
                dve_delay(e)
                e.tensor_reduce(vv[0:1, 0:1], ps[3][0:1, 0:128], AX, ALU.max)
                e.tensor_reduce(vv[0:1, 2:3], ps[4][0:1, 0:128], AX, ALU.max)\
                    .then_inc(sem["CV"], 1)
                ctx.inc("CV")

            ctx.phase({"tensor": pE_tens, "gpsimd": pE_gps, "vector": pE_vec},
                      order=["tensor", "gpsimd", "vector"])
            if os.environ.get("FFCUT") == "E":
                st2.close()
                return

            ctE2, cvE2, cgE2, dsE2 = (ctx.cnt[k] for k in ["CT", "CV", "CG", "DS"])

            def pE2_tens(e):
                e.transpose(ps[2][0:16, 0:128], tr1[:], sb_ident[:]).then_inc(sem["CT"], 1)
                ctx.inc("CT")

            def pE2_vec(e):
                e.wait_ge(sem["CT"], ctE2 + 1)
                dve_delay(e)
                e.tensor_copy(s16[:], ps[2][0:16, 0:128]).then_inc(sem["CV"], 1)
                ctx.inc("CV")
                e.wait_ge(sem["DS"], dsE2 + 16 * 16)
                e.wait_ge(sem["CG"], cgE2)
                e.tensor_copy(row[0:1, TP + T:TP + T + 1], vv[0:1, 0:1])
                e.tensor_copy(row[0:1, TP + T + 1:TP + T + 2], vv[0:1, 2:3])
                e.tensor_scalar(row[0:1, TP:TP + T + 2], row[0:1, TP:TP + T + 2],
                                1e-5, None, ALU.max)
                e.tensor_single_scalar(row[0:1, 0:T + 2], row[0:1, TP:TP + T + 2],
                                       sb_scal[0:1, 1:2], ALU.mult)
                e.reciprocal(row[0:1, TP:TP + T], row[0:1, TP:TP + T])
                e.reciprocal(row[0:1, TP + T:TP + T + 2], row[0:1, TP + T:TP + T + 2])
                e.tensor_scalar_mul(row[0:1, TP:TP + T + 2],
                                    row[0:1, TP:TP + T + 2], 127.0)\
                    .then_inc(sem["CV"], 1)
                ctx.inc("CV")

            def pE2_sync(e):
                e.wait_ge(sem["CV"], cvE2 + 1)
                for tb in range(16):
                    ctx.ld(e, row[0:1, TP + 128 * tb:TP + 128 * (tb + 1)],
                           s16[tb:tb + 1, :])

            ctx.phase({"tensor": pE2_tens, "vector": pE2_vec, "sync": pE2_sync},
                      order=["tensor", "vector", "sync"])
            if os.environ.get("FFCUT") == "E2":
                st2.close()
                return


            ctF = ctx.cnt["CT"]
            cvF = ctx.cnt["CV"]
            segsF = []
            for so_, dst in [(TP, sxb), (0, rsb)]:
                for c0 in range(0, T + 2, 512):
                    segsF.append((so_ + c0, min(512, T + 2 - c0), dst, c0))

            def pF_vec(e):
                for k, (so_, w_, dst, c0) in enumerate(segsF):
                    if k >= 2:
                        e.wait_ge(sem["CT"], ctF + k - 1)
                    e.tensor_copy(rseg[k % 2][0:1, 0:w_], row[0:1, so_:so_ + w_])\
                        .then_inc(sem["CV"], 1)
                    ctx.inc("CV")
                    e.wait_ge(sem["CT"], ctF + k + 1)
                    dve_delay(e)
                    if c0 + w_ <= T:
                        e.tensor_copy(dst[:, 1 + c0:1 + c0 + w_],
                                      ps[k % 4][0:128, 0:w_])
                    else:
                        e.tensor_copy(dst[:, 0:1], ps[k % 4][0:128, 0:1])
                        e.tensor_copy(dst[:, TP - 1:TP], ps[k % 4][0:128, 1:2])
                    e.nop().then_inc(sem["CV"], 1)
                    ctx.inc("CV")

            def pF_tens(e):
                for k, (so_, w_, dst, c0) in enumerate(segsF):
                    e.wait_ge(sem["CV"], cvF + 2 * k + 1)
                    e.matmul(ps[k % 4][0:128, 0:w_], sb_onesq[:],
                             rseg[k % 2][:, 0:w_], start=True, stop=True).then_inc(sem["CT"], 1)
                    ctx.inc("CT")

            ctx.phase({"vector": pF_vec, "tensor": pF_tens},
                      order=["vector", "tensor"])
            if os.environ.get("FFCUT") == "F":
                st2.close()
                return

            dsG, csG, cvG = ctx.cnt["DS"], ctx.cnt["CS"], ctx.cnt["CV"]

            def pG_sync(e):
                for dc in range(DC):
                    if dc >= 1:
                        e.wait_ge(sem["CV"], cvG + dc)
                    chunk_load(e, xc, xsrc, halosrc, dc)

            def pG_scal(e):
                for dc in range(DC):
                    e.wait_ge(sem["DS"], dsG + 48 * (dc + 1))
                    if mask_halo:
                        e.activation(xc[:, 0:1], xc[:, 0:1], ACTF.Copy,
                                     scale=sb_sel[:, 0:1])
                        e.activation(xc[:, TP - 1:TP], xc[:, TP - 1:TP], ACTF.Copy,
                                     scale=sb_sel[:, 1:2]).then_inc(sem["CS"], 1)
                    else:
                        e.nop().then_inc(sem["CS"], 1)
                    ctx.inc("CS")

            def pG_vec(e):
                gb = [(0, QB + 1)] + [(1 + QB * i, QB) for i in range(1, 4)]
                gb.append((1 + 3 * QB + QB, 1))
                for dc in range(DC):
                    e.wait_ge(sem["CS"], csG + dc + 1)
                    e.tensor_tensor(lnc[:], xc[:], mb[:], ALU.subtract)
                    e.tensor_tensor(lnc[:], lnc[:], rb[:], ALU.mult)
                    last = None
                    for o, w in gb:
                        e.tensor_tensor(vv[:, 0:w], lnc[:, o:o + w],
                                        sxb[:, o:o + w], ALU.mult)
                        last = round_int_bf16(e, vv, y0, y0f, rr,
                                              xq[:, TP * dc + o:TP * dc + o + w],
                                              w)
                    last.then_inc(sem["CV"], 1)
                    ctx.inc("CV")

            ctx.phase({"sync": pG_sync, "scalar": pG_scal, "vector": pG_vec},
                      order=["sync", "scalar", "vector"])

            if os.environ.get("DBGDUMP") == "1":
                def pDBG_vec(e):
                    e.tensor_copy(lnc[:, 0:T], mb[:, 1:T + 1])
                    e.nop().then_inc(sem["CV"], 1)
                    ctx.inc("CV")
                    e.tensor_copy(lnc[:, 0:T], rb[:, 1:T + 1])\
                        .then_inc(sem["CV"], 1)
                    ctx.inc("CV")

                def pDBG_gps(e):
                    cv0_ = ctx.cnt["CV"] - 2
                    e.wait_ge(sem["CV"], cv0_ + 1)
                    pass

                def pDBG_sync(e):
                    ctx.ld(e, Xs[0:128, :], mb[:, 1:T + 1])
                    ctx.ld(e, Xs[128:256, :], rb[:, 1:T + 1])
                    ctx.ld(e, Xs[256:384, :], sxb[:, 1:T + 1])
                    ctx.ld(e, Xs[384:512, :], rsb[:, 1:T + 1])
                    w = ctx.ld(e, Xs[512:640, 0:T // 2],
                               xq.bitcast(F32)[:, 0:T // 2])
                    e.wait_ge(sem["DS"], w)

                ctx.phase({"sync": pDBG_sync})
                st2.close()
                return

            st2.close()
            if os.environ.get("FFCUT") == "G":
                return

            w1sb = ctx.sbt(st, "w1sb", [128, DC * FF0], BF16)
            w2sb = [ctx.sbt(st, "w2sb", [128, D], BF16) for _ in range(2)]
            hdv = ctx.sbt(st, "hdv", [128, QB + 2])
            cvo = ctx.sbt(st, "cvo", [128, QB])
            slb = [ctx.sbt(st, "slb", [128, QB]) for _ in range(2)]
            h2q = ctx.sbt(st, "h2q", [128, FH * QB], BF16)
            mx2 = ctx.sbt(st, "mx2", [128, QB])
            sx2 = ctx.sbt(st, "sx2", [128, QB])
            rs2 = ctx.sbt(st, "rs2", [128, QB])
            xcb = ctx.sbt(st, "xcb", [128, QB])
            srow = ctx.sbt(st, "srow", [1, 2 * QB])

            dsW = ctx.cnt["DS"]

            def pW_sync(e):
                for dc in range(DC):
                    ctx.ld(e, w1sb[:, FF0 * dc:FF0 * (dc + 1)],
                           w1q[128 * dc:128 * (dc + 1), :])
                e.wait_ge(sem["DS"], dsW + 16 * DC)

            ctx.phase({"sync": pW_sync})

            for tbk in range(NQB):
                c0 = QB * tbk

                ctH, cvH, csH, daH = (ctx.cnt[k] for k in ["CT", "CV", "CS", "DA"])

                def pH_tens(e, c0=c0, cvH=cvH):
                    for hc in range(FH):
                        pb = hc % 2
                        if hc >= 2:
                            e.wait_ge(sem["CV"], cvH + 2 * (hc - 1))
                        for dc in range(DC):
                            lt = w1sb[:, FF0 * dc + 128 * hc:FF0 * dc + 128 * (hc + 1)]
                            e.matmul(ps[pb][:, 0:QB],
                                     lt, xq[:, TP * dc + c0 + 1:TP * dc + c0 + 1 + QB],
                                     start=(dc == 0), stop=(dc == DC - 1))
                        for dc in range(DC):
                            lt = w1sb[:, FF0 * dc + 128 * hc:FF0 * dc + 128 * (hc + 1)]
                            e.matmul(ps[2 + pb][:, 0:1], lt,
                                     xq[:, TP * dc + c0:TP * dc + c0 + 1],
                                     start=(dc == 0), stop=(dc == DC - 1))
                        for dc in range(DC):
                            lt = w1sb[:, FF0 * dc + 128 * hc:FF0 * dc + 128 * (hc + 1)]
                            mm = e.matmul(
                                ps[4 + pb][:, 0:1], lt,
                                xq[:, TP * dc + c0 + QB + 1:TP * dc + c0 + QB + 2],
                                start=(dc == 0), stop=(dc == DC - 1))
                            if dc == DC - 1:
                                mm.then_inc(sem["CT"], 1)
                        ctx.inc("CT")

                def pH_vec(e, c0=c0, ctH=ctH, csH=csH):
                    for hc in range(FH):
                        pb = hc % 2
                        e.wait_ge(sem["CT"], ctH + hc + 1)
                        if hc >= 1:
                            e.wait_ge(sem["CS"], csH + hc)
                        dve_delay(e)
                        e.tensor_tensor(hdv[:, 1:QB + 1], ps[pb][:, 0:QB],
                                        rsb[:, c0 + 1:c0 + 1 + QB], ALU.mult)
                        e.tensor_tensor(hdv[:, 0:1], ps[2 + pb][:, 0:1],
                                        rsb[:, c0:c0 + 1], ALU.mult)
                        e.tensor_tensor(hdv[:, QB + 1:QB + 2], ps[4 + pb][:, 0:1],
                                        rsb[:, c0 + QB + 1:c0 + QB + 2], ALU.mult)
                        e.tensor_scalar(cvo[:], hdv[:, 1:QB + 1],
                                        cwsb[:, 3 * hc + 1:3 * hc + 2],
                                        cbsb[:, hc:hc + 1], ALU.mult, ALU.add)
                        e.scalar_tensor_tensor(cvo[:], hdv[:, 0:QB],
                                               cwsb[:, 3 * hc:3 * hc + 1], cvo[:],
                                               ALU.mult, ALU.add)
                        e.scalar_tensor_tensor(cvo[:], hdv[:, 2:QB + 2],
                                               cwsb[:, 3 * hc + 2:3 * hc + 3], cvo[:],
                                               ALU.mult, ALU.add)\
                            .then_inc(sem["CV"], 1)
                        ctx.inc("CV")
                        e.wait_ge(sem["CS"], csH + hc + 1)
                        e.tensor_scalar_mul(hdv[:, 0:QB], slb[pb][:], -1.0)
                        e.tensor_max(hdv[:, 0:QB], hdv[:, 0:QB], slb[pb][:])
                        if hc == 0:
                            e.tensor_copy(mx2[:], hdv[:, 0:QB]).then_inc(sem["CV"], 1)
                        else:
                            e.tensor_max(mx2[:], mx2[:], hdv[:, 0:QB])\
                                .then_inc(sem["CV"], 1)
                        ctx.inc("CV")

                sa0, sb0 = ctx.cnt["SA"], ctx.cnt["SB"]

                def pH_scal(e, cvH=cvH, sa0=sa0, sb0=sb0):
                    for hc in range(FH):
                        e.wait_ge(sem["CV"], cvH + 2 * hc + 1)
                        if hc >= 2:
                            p_ = hc % 2
                            e.wait_ge(sem["SA" if p_ == 0 else "SB"],
                                      (sa0 if p_ == 0 else sb0)
                                      + 16 * ((hc - 2) // 2 + 1))
                        e.activation(slb[hc % 2][:], cvo[:], ACTF.Silu)\
                            .then_inc(sem["CS"], 1)
                        ctx.inc("CS")

                def pH_gps(e, csH=csH):
                    for hc in range(FH):
                        e.wait_ge(sem["CS"], csH + hc + 1)
                        ctx.ld(e, HST[128 * hc:128 * (hc + 1), :], slb[hc % 2][:],
                               q="SA" if hc % 2 == 0 else "SB")
                    e.wait_ge(sem["SA"], ctx.cnt["SA"])
                    e.wait_ge(sem["SB"], ctx.cnt["SB"])

                ctx.phase({"tensor": pH_tens, "vector": pH_vec, "scalar": pH_scal,
                           "gpsimd": pH_gps},
                          order=["tensor", "vector", "scalar", "gpsimd"])
                if os.environ.get("FFCUT") == "H":
                    return
                if os.environ.get("DBGH") == "1" and tbk == 0:
                    def pDH_sync(e):
                        for j in range(16):
                            ctx.ld(e, Xs[128 * (j % 8):128 * (j % 8 + 1),
                                         512 * (j // 8):512 * (j // 8) + 512],
                                   HST[128 * j:128 * (j + 1), :])
                        e.wait_ge(sem["DS"], ctx.cnt["DS"])

                    ctx.phase({"sync": pDH_sync})
                    return

                ctI, cvI = ctx.cnt["CT"], ctx.cnt["CV"]

                dsI = ctx.cnt["DS"]
                _imode = os.environ.get("IMODE", "")

                def pI_tens(e, ctI=ctI, cvI=cvI):
                    if _imode in ("not", "none"):
                        for _ in range(5):
                            e.nop().then_inc(sem["CT"], 1)
                            ctx.inc("CT")
                        return
                    for qt in range(QB // 128):
                        if qt >= 2:
                            e.wait_ge(sem["CV"], cvI + qt - 1)
                        e.transpose(ps[qt % 2][:, 0:128],
                                    mx2[:, 128 * qt:128 * (qt + 1)], sb_ident[:]).then_inc(sem["CT"], 1)
                        ctx.inc("CT")
                    e.wait_ge(sem["CV"], cvI + 5)
                    e.transpose(ps[2][0:16, 0:128], tr1[:], sb_ident[:]).then_inc(sem["CT"], 1)
                    ctx.inc("CT")

                def pI_vec(e, ctI=ctI, dsI=dsI):
                    if _imode in ("not", "none"):
                        for _ in range(6):
                            e.nop().then_inc(sem["CV"], 1)
                            ctx.inc("CV")
                        e.wait_ge(sem["DS"], dsI + 16 * 4)
                        e.tensor_scalar(srow[0:1, 0:QB], srow[0:1, 0:QB], 1e-5,
                                        None, ALU.max)
                        e.reciprocal(srow[0:1, QB:2 * QB], srow[0:1, 0:QB])
                        e.tensor_scalar_mul(srow[0:1, QB:2 * QB],
                                            srow[0:1, QB:2 * QB], 127.0)
                        e.tensor_single_scalar(srow[0:1, 0:QB], srow[0:1, 0:QB],
                                               sb_scal[0:1, 3:4], ALU.mult)\
                            .then_inc(sem["CV"], 1)
                        ctx.inc("CV")
                        return
                    for qt in range(QB // 128):
                        e.wait_ge(sem["CT"], ctI + qt + 1)
                        dve_delay(e)
                        e.tensor_reduce(tr1[:, qt:qt + 1], ps[qt % 2][:, 0:128],
                                        AX, ALU.max).then_inc(sem["CV"], 1)
                        ctx.inc("CV")
                    e.nop().then_inc(sem["CV"], 1)
                    ctx.inc("CV")
                    e.wait_ge(sem["CT"], ctI + 5)
                    dve_delay(e)
                    e.tensor_copy(s16[0:4, :], ps[2][0:4, 0:128])\
                        .then_inc(sem["CV"], 1)
                    ctx.inc("CV")
                    e.wait_ge(sem["DS"], dsI + (0 if _imode in ("nos", "none") else 16 * 4))
                    if _imode not in ("nos", "none"):
                        e.tensor_scalar(srow[0:1, 0:QB], srow[0:1, 0:QB], 1e-5, None,
                                        ALU.max)
                        e.reciprocal(srow[0:1, QB:2 * QB], srow[0:1, 0:QB])
                        e.tensor_scalar_mul(srow[0:1, QB:2 * QB],
                                            srow[0:1, QB:2 * QB], 127.0)
                        e.tensor_single_scalar(srow[0:1, 0:QB], srow[0:1, 0:QB],
                                               sb_scal[0:1, 3:4], ALU.mult)\
                            .then_inc(sem["CV"], 1)
                    else:
                        e.nop().then_inc(sem["CV"], 1)
                    ctx.inc("CV")

                def pI_sync(e, cvI=cvI):
                    e.wait_ge(sem["CV"], cvI + 6)
                    if _imode not in ("nos", "none"):
                        for qt in range(4):
                            ctx.ld(e, srow[0:1, 128 * qt:128 * (qt + 1)],
                                   s16[qt:qt + 1, :])

                ctx.phase({"tensor": pI_tens, "vector": pI_vec, "sync": pI_sync},
                          order=["tensor", "vector", "sync"])
                if os.environ.get("FFCUT") == "I":
                    return

                ctI2, cvI2 = ctx.cnt["CT"], ctx.cnt["CV"]

                def pI2_vec(e, cvI2=cvI2, ctI2=ctI2):
                    e.tensor_copy(rseg[0][0:1, 0:QB], srow[0:1, QB:2 * QB])
                    e.tensor_copy(rseg[1][0:1, 0:QB], srow[0:1, 0:QB])\
                        .then_inc(sem["CV"], 1)
                    ctx.inc("CV")
                    e.wait_ge(sem["CT"], ctI2 + 1)
                    dve_delay(e)
                    e.tensor_copy(sx2[:], ps[0][0:128, 0:QB])
                    e.tensor_copy(rs2[:], ps[1][0:128, 0:QB]).then_inc(sem["CV"], 1)
                    ctx.inc("CV")

                def pI2_tens(e, cvI2=cvI2):
                    e.wait_ge(sem["CV"], cvI2 + 1)
                    e.matmul(ps[0][0:128, 0:QB], sb_onesq[:], rseg[0][:, 0:QB],
                             start=True, stop=True)
                    e.matmul(ps[1][0:128, 0:QB], sb_onesq[:], rseg[1][:, 0:QB],
                             start=True, stop=True).then_inc(sem["CT"], 1)
                    ctx.inc("CT")

                ctx.phase({"vector": pI2_vec, "tensor": pI2_tens},
                          order=["vector", "tensor"])
                if os.environ.get("FFCUT") == "I2":
                    return
                if os.environ.get("DBGI") == "1" and tbk == 0:
                    def pDI_sync(e):
                        ctx.ld(e, Xs[0:128, 0:QB], sx2[:])
                        ctx.ld(e, Xs[128:256, 0:QB], rs2[:])
                        ctx.ld(e, Xs[256:257, 0:2 * QB], srow[0:1, :], slow=True)
                        ctx.ld(e, Xs[384:512, 0:QB], mx2[:])
                        ctx.ld(e, Xs[512:640, 0:16], tr1[:, 0:16], slow=True)
                        e.wait_ge(sem["DS"], ctx.cnt["DS"])

                    ctx.phase({"sync": pDI_sync})
                    return

                cvJ = ctx.cnt["CV"]
                laJ, lbJ = ctx.cnt["LA"], ctx.cnt["LB"]

                def pJ_sync(e, cvJ=cvJ):
                    for hc in range(FH):
                        if hc >= 2:
                            e.wait_ge(sem["CV"], cvJ + (hc - 1))
                        ctx.ld(e, slb[hc % 2][:], HST[128 * hc:128 * (hc + 1), :],
                               q="LA" if hc % 2 == 0 else "LB")
                    e.wait_ge(sem["LA"], ctx.cnt["LA"])
                    e.wait_ge(sem["LB"], ctx.cnt["LB"])

                _jmode = os.environ.get("JMODE", "")

                def pJ_vec(e, laJ=laJ, lbJ=lbJ):
                    for hc in range(FH):
                        e.wait_ge(sem["LA" if hc % 2 == 0 else "LB"],
                                  (laJ if hc % 2 == 0 else lbJ)
                                  + 16 * (hc // 2 + 1))
                        e.tensor_tensor(vv[:, 0:QB], slb[hc % 2][:], sx2[:], ALU.mult)
                        if _jmode != "norc":
                            round_int_bf16(e, vv, y0, y0f, rr,
                                           h2q[:, QB * hc:QB * (hc + 1)], QB)\
                                .then_inc(sem["CV"], 1)
                        else:
                            e.tensor_copy(h2q[:, QB * hc:QB * (hc + 1)],
                                          vv[:, 0:QB]).then_inc(sem["CV"], 1)
                        ctx.inc("CV")

                ctx.phase({"sync": pJ_sync, "vector": pJ_vec},
                          order=["sync", "vector"])
                if os.environ.get("FFCUT") == "J":
                    return
                if os.environ.get("DBGJ") == "1" and tbk == 0:
                    def pDJ_sync(e):
                        ctx.ld(e, Xs[0:128, 0:4 * QB // 2],
                               h2q.bitcast(F32)[:, 0:4 * QB // 2])
                        e.wait_ge(sem["DS"], ctx.cnt["DS"])

                    ctx.phase({"sync": pDJ_sync})
                    return

                dsK, dgK, ctK, cvK = (ctx.cnt[k] for k in ["DS", "DG", "CT", "CV"])
                ctx_saK = [ctx.cnt["SA"]]

                laK, lbK = ctx.cnt["LA"], ctx.cnt["LB"]

                def pK_sync(e, ctK=ctK):
                    for hc in range(FH):
                        ctx.ld(e, w2sb[hc % 2][:], w2q[128 * hc:128 * (hc + 1), :],
                               q="LA" if hc % 2 == 0 else "LB")
                        if hc >= 1:
                            e.wait_ge(sem["CT"], ctK + hc)
                    e.wait_ge(sem["LA"], ctx.cnt["LA"])
                    e.wait_ge(sem["LB"], ctx.cnt["LB"])

                dgK2 = ctx.cnt["DG"]

                def pK_gps(e, dgK2=dgK2, cvK=cvK, c0=c0):
                    for dc in range(DC):
                        if dc >= 1:
                            e.wait_ge(sem["SA"], ctx_saK[0] + 16 * dc)
                        ctx.ld(e, xcb[:], xsrc[128 * dc:128 * (dc + 1), c0:c0 + QB],
                               q="DG")
                        e.wait_ge(sem["CV"], cvK + dc + 1)
                        ctx.ld(e, xdst[128 * dc:128 * (dc + 1), c0:c0 + QB],
                               xcb[:], q="SA")
                    e.wait_ge(sem["SA"], ctx.cnt["SA"])

                def pK_tens(e, laK=laK, lbK=lbK):
                    for hc in range(FH):
                        e.wait_ge(sem["LA" if hc % 2 == 0 else "LB"],
                                  (laK if hc % 2 == 0 else lbK)
                                  + 16 * (hc // 2 + 1))
                        for dc in range(DC):
                            mm = e.matmul(ps[dc][:, 0:QB],
                                          w2sb[hc % 2][:, 128 * dc:128 * (dc + 1)],
                                          h2q[:, QB * hc:QB * (hc + 1)],
                                          start=(hc == 0), stop=(hc == FH - 1))
                            if dc == DC - 1:
                                mm.then_inc(sem["CT"], 1)
                        ctx.inc("CT")

                def pK_vec(e, ctK=ctK, dgK2=dgK2, c0=c0):
                    e.wait_ge(sem["CT"], ctK + FH)
                    dve_delay(e)
                    for dc in range(DC):
                        e.wait_ge(sem["DG"], dgK2 + 16 * (dc + 1))
                        e.tensor_tensor(cvo[:], ps[dc][:, 0:QB], rs2[:], ALU.mult)
                        e.scalar_tensor_tensor(xcb[:], cvo[:], 0.5, xcb[:],
                                               ALU.mult, ALU.add)\
                            .then_inc(sem["CV"], 1)
                        ctx.inc("CV")

                ctx.phase({"sync": pK_sync, "tensor": pK_tens, "gpsimd": pK_gps,
                           "vector": pK_vec},
                          order=["sync", "tensor", "gpsimd", "vector"])

    def chunk_load(e, xc, xsrc, halosrc, dc):
        r_ = slice(128 * dc, 128 * (dc + 1))
        ctx.ld(e, xc[:, 1:T + 1], xsrc[r_, :])
        ctx.ld(e, xc[:, 0:1],
               halosrc[0:1, r_].rearrange("a (p b) -> (a p) b", b=1))
        ctx.ld(e, xc[:, TP - 1:TP],
               halosrc[1:2, r_].rearrange("a (p b) -> (a p) b", b=1))

    def ln_stats(xsrc, mb, rb, xc, x2):
        ds0, cs0, ct0 = ctx.cnt["DS"], ctx.cnt["CS"], ctx.cnt["CT"]

        def pS_sync(e):
            for dc in range(DC):
                if dc >= 1:
                    e.wait_ge(sem["CT"], ct0 + dc)
                ctx.ld(e, xc[:], xsrc[128 * dc:128 * (dc + 1), :])

        def pS_scal(e):
            for dc in range(DC):
                e.wait_ge(sem["DS"], ds0 + 16 * (dc + 1))
                e.activation(x2[:], xc[:], ACTF.Square).then_inc(sem["CS"], 1)
                ctx.inc("CS")

        def pS_tens(e):
            for dc in range(DC):
                e.wait_ge(sem["CS"], cs0 + dc + 1)
                stt, spp = (dc == 0), (dc == DC - 1)
                for bi in range(4):
                    o = QB * bi
                    e.matmul(ps[bi][0:128, 0:QB], sb_onesq[:], xc[:, o:o + QB],
                             start=stt, stop=spp)
                    mm = e.matmul(ps[4 + bi][0:128, 0:QB], sb_onesq[:],
                                  x2[:, o:o + QB], start=stt, stop=spp)
                    if bi == 3:
                        mm.then_inc(sem["CT"], 1)
                ctx.inc("CT")

        ctx.phase({"sync": pS_sync, "scalar": pS_scal, "tensor": pS_tens},
                  order=["sync", "scalar", "tensor"])

        cvB, csB = ctx.cnt["CV"], ctx.cnt["CS"]

        def pB_v(e):
            dve_delay(e)
            for bi in range(4):
                o = QB * bi
                e.tensor_scalar_mul(mb[:, o:o + QB], ps[bi][0:128, 0:QB], 1.0 / D)
                e.tensor_scalar_mul(rb[:, o:o + QB], ps[4 + bi][0:128, 0:QB],
                                    1.0 / D)
            e.tensor_tensor(x2[:, 0:T], mb[:, 0:T], mb[:, 0:T], ALU.mult)
            e.tensor_tensor(rb[:, 0:T], rb[:, 0:T], x2[:, 0:T], ALU.subtract)
            e.tensor_scalar_add(rb[:, 0:T], rb[:, 0:T], EPS_LN)\
                .then_inc(sem["CV"], 1)
            ctx.inc("CV")
            e.wait_ge(sem["CS"], csB + 1)
            for bi in range(4):
                o = QB * bi
                e.reciprocal(rb[:, o:o + QB], rb[:, o:o + QB])
            e.tensor_copy(sb_dly[0:1, :], sb_ident[0:1, :])
            e.nop().then_inc(sem["CV"], 1)
            ctx.inc("CV")

        def pB_s(e):
            e.wait_ge(sem["CV"], cvB + 1)
            e.activation(rb[:, 0:T], rb[:, 0:T], ACTF.Sqrt).then_inc(sem["CS"], 1)
            ctx.inc("CS")

        ctx.phase({"vector": pB_v, "scalar": pB_s}, order=["vector", "scalar"])

    def ln_hilo(xsrc):
        with ExitStack() as st:
            xc = ctx.sbt(st, "lhx", [128, T])
            x2 = ctx.sbt(st, "lhx2", [128, T])
            mb = ctx.sbt(st, "lhmb", [128, T])
            rb = ctx.sbt(st, "lhrb", [128, T])
            hb = [ctx.sbt(st, "lhh", [128, T], BF16) for _ in range(2)]
            lb = [ctx.sbt(st, "lhl", [128, T], BF16) for _ in range(2)]
            ln_stats(xsrc, mb, rb, xc, x2)
            ds1, cv1 = ctx.cnt["DS"], ctx.cnt["CV"]

            def pL_sync(e):
                for dc in range(DC):
                    if dc >= 1:
                        e.wait_ge(sem["CV"], cv1 + dc)
                    ctx.ld(e, xc[:], xsrc[128 * dc:128 * (dc + 1), :])

            def pL_vec(e):
                for dc in range(DC):
                    i = dc % 2
                    e.wait_ge(sem["DS"], ds1 + 16 * (dc + 1))
                    e.tensor_tensor(x2[:], xc[:], mb[:], ALU.subtract)
                    e.tensor_tensor(x2[:], x2[:], rb[:], ALU.mult)
                    e.tensor_copy(hb[i][:], x2[:])
                    e.tensor_tensor(lb[i][:], x2[:], hb[i][:], ALU.subtract)\
                        .then_inc(sem["CV"], 1)
                    ctx.inc("CV")

            def pL_scal(e):
                for dc in range(DC):
                    e.wait_ge(sem["CV"], cv1 + dc + 1)
                    r_ = slice(128 * dc, 128 * (dc + 1))
                    ctx.ld(e, LNH[r_, :], hb[dc % 2][:], q="DA")
                    ctx.ld(e, LNL[r_, :], lb[dc % 2][:], q="DA")
                e.wait_ge(sem["DA"], ctx.cnt["DA"])

            ctx.phase({"sync": pL_sync, "vector": pL_vec, "scalar": pL_scal},
                      order=["sync", "vector", "scalar"])

    def attn(layer, xsrc, xdst):
        causal = (layer == 1)
        pn = ["pq1", "pk1", "pv1", "po1"] if causal else \
             ["pq0", "pk0", "pv0", "po0"]
        ln_hilo(xsrc)

        with ExitStack() as st:
            lnh_sb = ctx.sbt(st, "atlh", [128, DC * T], BF16)
            lnl_sb = ctx.sbt(st, "atll", [128, DC * T], BF16)
            wsh = ctx.sbt(st, "atwh", [128, DC * D], BF16)
            wsl = ctx.sbt(st, "atwl", [128, DC * D], BF16)
            osb = [ctx.sbt(st, "atos", [128, QB]) for _ in range(2)]
            ohb = [ctx.sbt(st, "atoh", [128, QB], BF16) for _ in range(2)]
            olb = [ctx.sbt(st, "atol", [128, QB], BF16) for _ in range(2)]
            vosb = [ctx.sbt(st, "atvo", [128, D]) for _ in range(2)]

            def pLd_sync(e):
                for dc in range(DC):
                    r_ = slice(128 * dc, 128 * (dc + 1))
                    ctx.ld(e, lnh_sb[:, T * dc:T * (dc + 1)], LNH[r_, :])
                    ctx.ld(e, lnl_sb[:, T * dc:T * (dc + 1)], LNL[r_, :])
                e.wait_ge(sem["DS"], ctx.cnt["DS"])

            ctx.phase({"sync": pLd_sync})

            def wload(e, nm):
                for dc in range(DC):
                    r_ = slice(128 * dc, 128 * (dc + 1))
                    ctx.ld(e, wsh[:, D * dc:D * (dc + 1)], pwq[nm + "h"][r_, :])
                    ctx.ld(e, wsl[:, D * dc:D * (dc + 1)], pwq[nm + "l"][r_, :])
                e.wait_ge(sem["DS"], ctx.cnt["DS"])

            def proj_fm(nm, kind):
                ctx.phase({"sync": lambda e: wload(e, nm)})
                ctP, cvP = ctx.cnt["CT"], ctx.cnt["CV"]
                sa0, sb0 = ctx.cnt["SA"], ctx.cnt["SB"]
                spc = 16 if kind == "q" else 32

                def pM_tens(e):
                    for tb in range(NQB):
                        if tb >= 1:
                            e.wait_ge(sem["CV"], cvP + tb * 8)
                        for dc in range(DC):
                            for mb_ in range(8):
                                lth = wsh[:, D * dc + 128 * mb_:
                                          D * dc + 128 * (mb_ + 1)]
                                ltl = wsl[:, D * dc + 128 * mb_:
                                          D * dc + 128 * (mb_ + 1)]
                                rh = lnh_sb[:, T * dc + QB * tb:
                                            T * dc + QB * (tb + 1)]
                                rl = lnl_sb[:, T * dc + QB * tb:
                                            T * dc + QB * (tb + 1)]
                                e.matmul(ps[mb_][:, 0:QB], lth, rh,
                                         start=(dc == 0), stop=False)
                                e.matmul(ps[mb_][:, 0:QB], lth, rl,
                                         start=False, stop=False)
                                mm = e.matmul(ps[mb_][:, 0:QB], ltl, rh,
                                              start=False, stop=(dc == DC - 1))
                                if dc == DC - 1:
                                    mm.then_inc(sem["CT"], 1)
                                    ctx.inc("CT")

                def pM_vec(e):
                    k = 0
                    for tb in range(NQB):
                        for mb_ in range(8):
                            e.wait_ge(sem["CT"], ctP + k + 1)
                            dve_delay(e)
                            i = k % 2
                            if k >= 2:
                                e.wait_ge(sem["SA" if i == 0 else "SB"],
                                          (sa0 if i == 0 else sb0)
                                          + spc * ((k - 2) // 2 + 1))
                            if kind == "q":
                                e.tensor_copy(osb[i][:], ps[mb_][:, 0:QB])\
                                    .then_inc(sem["CV"], 1)
                            else:
                                e.tensor_copy(ohb[i][:], ps[mb_][:, 0:QB])
                                e.tensor_tensor(olb[i][:], ps[mb_][:, 0:QB],
                                                ohb[i][:], ALU.subtract)\
                                    .then_inc(sem["CV"], 1)
                            ctx.inc("CV")
                            k += 1

                def pM_scal(e):
                    k = 0
                    for tb in range(NQB):
                        for mb_ in range(8):
                            e.wait_ge(sem["CV"], cvP + k + 1)
                            i = k % 2
                            qs = "SA" if i == 0 else "SB"
                            r_ = slice(128 * mb_, 128 * (mb_ + 1))
                            if kind == "q":
                                ctx.ld(e, Qt[r_, QB * tb:QB * (tb + 1)],
                                       osb[i][:], q=qs)
                            else:
                                ctx.ld(e, kpin[r_, QB * tb:QB * (tb + 1)],
                                       ohb[i][:], q=qs)
                                ctx.ld(e, kpin[r_, T + QB * tb:
                                               T + QB * (tb + 1)],
                                       olb[i][:], q=qs)
                            k += 1
                    e.wait_ge(sem["SA"], ctx.cnt["SA"])
                    e.wait_ge(sem["SB"], ctx.cnt["SB"])

                ctx.phase({"tensor": pM_tens, "vector": pM_vec,
                           "scalar": pM_scal},
                          order=["tensor", "vector", "scalar"])

            proj_fm(pn[0], "q")
            proj_fm(pn[1], "k")

            ctx.phase({"sync": lambda e: wload(e, pn[2])})
            ctV, cvV = ctx.cnt["CT"], ctx.cnt["CV"]
            sa0v, sb0v = ctx.cnt["SA"], ctx.cnt["SB"]

            def pV_tens(e):
                for kt in range(16):
                    if kt >= 4:
                        e.wait_ge(sem["CV"], cvV + (kt - 3) * 2)
                    bk = (kt % 4) * 2
                    for dc in range(DC):
                        lth = lnh_sb[:, T * dc + 128 * kt:
                                     T * dc + 128 * (kt + 1)]
                        ltl = lnl_sb[:, T * dc + 128 * kt:
                                     T * dc + 128 * (kt + 1)]
                        for mh in range(2):
                            rh = wsh[:, D * dc + 512 * mh:
                                     D * dc + 512 * (mh + 1)]
                            rl = wsl[:, D * dc + 512 * mh:
                                     D * dc + 512 * (mh + 1)]
                            e.matmul(ps[bk + mh][:, 0:512], lth, rh,
                                     start=(dc == 0), stop=False)
                            e.matmul(ps[bk + mh][:, 0:512], lth, rl,
                                     start=False, stop=False)
                            mm = e.matmul(ps[bk + mh][:, 0:512], ltl, rh,
                                          start=False, stop=(dc == DC - 1))
                            if dc == DC - 1 and mh == 1:
                                mm.then_inc(sem["CT"], 1)
                    ctx.inc("CT")

            def pV_vec(e):
                for kt in range(16):
                    e.wait_ge(sem["CT"], ctV + kt + 1)
                    dve_delay(e)
                    bk = (kt % 4) * 2
                    i = kt % 2
                    if kt >= 2:
                        e.wait_ge(sem["SA" if i == 0 else "SB"],
                                  (sa0v if i == 0 else sb0v)
                                  + 16 * 17 * ((kt - 2) // 2 + 1))
                    e.tensor_copy(vosb[i][:, 0:512], ps[bk][:, 0:512])
                    e.tensor_copy(vosb[i][:, 512:1024], ps[bk + 1][:, 0:512])\
                        .then_inc(sem["CV"], 2)
                    ctx.inc("CV", 2)

            def pV_scal(e):
                for kt in range(16):
                    e.wait_ge(sem["CV"], cvV + (kt + 1) * 2)
                    i = kt % 2
                    qs = "SA" if i == 0 else "SB"
                    for h in range(H):
                        ctx.ld(e, vain[128 * kt:128 * (kt + 1),
                                       65 * h:65 * h + 64],
                               vosb[i][:, 64 * h:64 * (h + 1)], q=qs,
                               slow=True)
                    ctx.ld(e, vain.rearrange("t (h c) -> t h c", c=65)
                           [128 * kt:128 * (kt + 1), :, 64:65],
                           sb_onesq[:, 0:H].unsqueeze(2), q=qs, slow=True)
                e.wait_ge(sem["SA"], ctx.cnt["SA"])
                e.wait_ge(sem["SB"], ctx.cnt["SB"])

            ctx.phase({"tensor": pV_tens, "vector": pV_vec, "scalar": pV_scal},
                      order=["tensor", "vector", "scalar"])

        def pGt_gps(e):
            e.collective_compute("AllGather", ALU.bypass, ins=[kpin],
                                 outs=[kpout], replica_groups=replica)\
                .then_inc(sem["CG"], 1)
            ctx.inc("CG")
            e.collective_compute("AllGather", ALU.bypass, ins=[vain],
                                 outs=[vaout], replica_groups=replica)\
                .then_inc(sem["CG"], 1)
            ctx.inc("CG")
            e.wait_ge(sem["CG"], ctx.cnt["CG"])

        ctx.phase({"gpsimd": pGt_gps})

        with ExitStack() as st:
            PT = ctx.sbt(st, "atpt", [128, KCN * QB])
            khh = ctx.sbt(st, "atkh", [64, S], BF16)
            khl = ctx.sbt(st, "atkl", [64, S], BF16)
            qf = ctx.sbt(st, "atqf", [64, QB])
            qh = ctx.sbt(st, "atqh", [64, QB], BF16)
            ql = ctx.sbt(st, "atql", [64, QB], BF16)
            vau = ctx.sbt(st, "atva", [128, KCN * 65])
            zmv = ctx.sbt(st, "atzm", [64, QB])
            dnb = ctx.sbt(st, "atdn", [64, QB])
            pstg = ctx.sbt(st, "atpg", [128, QB])
            zsb = ctx.sbt(st, "atz", [128, DC * QB])
            zhb = ctx.sbt(st, "atzh", [128, DC * QB], BF16)
            zlb = ctx.sbt(st, "atzl", [128, DC * QB], BF16)
            poh = [ctx.sbt(st, "atph", [128, D], BF16) for _ in range(2)]
            pol = [ctx.sbt(st, "atpl", [128, D], BF16) for _ in range(2)]
            xcb = ctx.sbt(st, "atxc", [128, QB])
            etm = ctx.sbt(st, "atet", [128, QB])
            lhc = ctx.sbt(st, "atlc", [128, QB], BF16)
            llc = ctx.sbt(st, "atl2", [128, QB], BF16)
            msb = ctx.sbt(st, "atms", [128, KCN * QB], BF16) if causal else None

            def pZz_vec(e):
                for j in range(4):
                    e.tensor_scalar_mul(pstg[:, 128 * j:128 * (j + 1)],
                                        sb_ident[:], 0.0)

            ctx.phase({"vector": pZz_vec})

            for qb in range(NQB):
                if causal:
                    def pMl_sync(e, qb=qb):
                        for kc in range(KCN):
                            ctx.ld(e, msb[:, QB * kc:QB * (kc + 1)],
                                   inp["lmask"][(qb * KCN + kc) * 128:
                                                (qb * KCN + kc + 1) * 128, :])
                        e.wait_ge(sem["DS"], ctx.cnt["DS"])

                    ctx.phase({"sync": pMl_sync})

                for h in range(H):
                    ct0, cs0, cv0, cg0 = (ctx.cnt[k]
                                          for k in ["CT", "CS", "CV", "CG"])
                    cv_rec = cv0 + 1 + (KCN if causal else 0) + 1
                    cv_fin = cv_rec + 1

                    def pA_sync(e, qb=qb, h=h):
                        ctx.ld(e, qf[:],
                               Qt[64 * h:64 * (h + 1), QB * qb:QB * (qb + 1)])
                        ctx.ld(e, khh[:, 0:T], kpout[64 * h:64 * (h + 1), 0:T])
                        ctx.ld(e, khh[:, T:S],
                               kpout[D + 64 * h:D + 64 * (h + 1), 0:T])
                        ctx.ld(e, khl[:, 0:T],
                               kpout[64 * h:64 * (h + 1), T:2 * T])
                        ctx.ld(e, khl[:, T:S],
                               kpout[D + 64 * h:D + 64 * (h + 1), T:2 * T])
                        for kc in range(KCN):
                            ctx.ld(e, vau[:, 65 * kc:65 * (kc + 1)],
                                   vaout[128 * kc:128 * (kc + 1),
                                         65 * h:65 * (h + 1)], slow=True)
                        e.wait_ge(sem["DS"], ctx.cnt["DS"])
                        e.nop().then_inc(sem["CG"], 1)
                        ctx.inc("CG")

                    def pA_vec(e, ct0=ct0, cs0=cs0, cg0=cg0, cv0=cv0,
                               cv_rec=cv_rec):
                        e.wait_ge(sem["CG"], cg0 + 1)
                        e.tensor_copy(qh[:], qf[:])
                        e.tensor_tensor(ql[:], qf[:], qh[:], ALU.subtract)\
                            .then_inc(sem["CV"], 1)
                        ctx.inc("CV")
                        if causal:
                            for kc in range(KCN):
                                e.wait_ge(sem["CS"], cs0 + kc + 1)
                                e.tensor_tensor(PT[:, QB * kc:QB * (kc + 1)],
                                                PT[:, QB * kc:QB * (kc + 1)],
                                                msb[:, QB * kc:QB * (kc + 1)],
                                                ALU.mult)\
                                    .then_inc(sem["CV"], 1)
                                ctx.inc("CV")
                        e.wait_ge(sem["CT"], ct0 + KCN + 1)
                        dve_delay(e)
                        if causal:
                            e.tensor_scalar_add(pstg[64:65, 0:QB],
                                                ps[2][64:65, 0:QB], 1e-8)
                            dve_delay(e)
                            e.reciprocal(pstg[64:65, 0:QB],
                                         pstg[64:65, 0:QB])\
                                .then_inc(sem["CV"], 1)
                        else:
                            e.reciprocal(pstg[64:65, 0:QB],
                                         ps[2][64:65, 0:QB])\
                                .then_inc(sem["CV"], 1)
                        ctx.inc("CV")
                        e.wait_ge(sem["CT"], ct0 + KCN + 2)
                        dve_delay(e)
                        e.tensor_copy(dnb[0:64, 0:QB], ps[3][0:64, 0:QB])
                        e.tensor_tensor(zmv[0:64, 0:QB], ps[2][0:64, 0:QB],
                                        dnb[0:64, 0:QB], ALU.mult)\
                            .then_inc(sem["CV"], 1)
                        ctx.inc("CV")

                    def pA_tens(e, ct0=ct0, cs0=cs0, cv0=cv0, cv_rec=cv_rec):
                        e.wait_ge(sem["CV"], cv0 + 1)
                        for kc in range(KCN):
                            if kc >= 2:
                                e.wait_ge(sem["CS"], cs0 + (kc - 1))
                            bk = kc % 2
                            lt_h = khh[:, 128 * kc:128 * (kc + 1)]
                            lt_l = khl[:, 128 * kc:128 * (kc + 1)]
                            e.matmul(ps[bk][:, 0:QB], lt_h, qh[:],
                                     start=True, stop=False)
                            e.matmul(ps[bk][:, 0:QB], lt_h, ql[:],
                                     start=False, stop=False)
                            e.matmul(ps[bk][:, 0:QB], lt_l, qh[:],
                                     start=False, stop=True)\
                                .then_inc(sem["CT"], 1)
                            ctx.inc("CT")
                        if causal:
                            e.wait_ge(sem["CV"], cv0 + 1 + KCN)
                        else:
                            e.wait_ge(sem["CS"], cs0 + KCN)
                        for kc in range(KCN):
                            mm = e.matmul(ps[2][0:65, 0:QB],
                                          vau[:, 65 * kc:65 * (kc + 1)],
                                          PT[:, QB * kc:QB * (kc + 1)],
                                          start=(kc == 0),
                                          stop=(kc == KCN - 1))
                            if kc == KCN - 1:
                                mm.then_inc(sem["CT"], 1)
                        ctx.inc("CT")
                        e.wait_ge(sem["CV"], cv_rec)
                        e.matmul(ps[3][0:64, 0:QB], sb_onesq[:, 0:64],
                                 pstg[:, 0:QB], start=True, stop=True)\
                            .then_inc(sem["CT"], 1)
                        ctx.inc("CT")

                    def pA_scal(e, ct0=ct0):
                        for kc in range(KCN):
                            e.wait_ge(sem["CT"], ct0 + kc + 1)
                            act_delay(e)
                            e.activation(PT[:, QB * kc:QB * (kc + 1)],
                                         ps[kc % 2][:, 0:QB], ACTF.Exp,
                                         scale=0.125)\
                                .then_inc(sem["CS"], 1)
                            ctx.inc("CS")

                    def pA_gps(e, h=h, cv_fin=cv_fin):
                        e.wait_ge(sem["CV"], cv_fin)
                        w = ctx.ld(e, zsb[64 * (h % 2):64 * (h % 2) + 64,
                                          QB * (h // 2):QB * (h // 2) + QB],
                                   zmv[0:64, 0:QB], q="DG")
                        e.wait_ge(sem["DG"], w)

                    ctx.phase({"sync": pA_sync, "vector": pA_vec,
                               "tensor": pA_tens, "scalar": pA_scal,
                               "gpsimd": pA_gps},
                              order=["sync", "vector", "tensor", "scalar",
                                     "gpsimd"])

                def pZs_vec(e):
                    for dc in range(DC):
                        cs_ = slice(QB * dc, QB * (dc + 1))
                        e.tensor_copy(zhb[:, cs_], zsb[:, cs_])
                        e.tensor_tensor(zlb[:, cs_], zsb[:, cs_], zhb[:, cs_],
                                        ALU.subtract)

                ctx.phase({"vector": pZs_vec})

                ctO = ctx.cnt["CT"]
                laO, lbO = ctx.cnt["LA"], ctx.cnt["LB"]

                def pO_sync(e, ctO=ctO):
                    for dc in range(DC):
                        i = dc % 2
                        qs = "LA" if i == 0 else "LB"
                        r_ = slice(128 * dc, 128 * (dc + 1))
                        if dc >= 2:
                            e.wait_ge(sem["CT"], ctO + (dc - 1))
                        ctx.ld(e, poh[i][:], pwq[pn[3] + "h"][r_, :], q=qs)
                        ctx.ld(e, pol[i][:], pwq[pn[3] + "l"][r_, :], q=qs)
                    e.wait_ge(sem["LA"], ctx.cnt["LA"])
                    e.wait_ge(sem["LB"], ctx.cnt["LB"])

                def pO_tens(e, ctO=ctO, laO=laO, lbO=lbO):
                    for dc in range(DC):
                        i = dc % 2
                        e.wait_ge(sem["LA" if i == 0 else "LB"],
                                  (laO if i == 0 else lbO)
                                  + 32 * (dc // 2 + 1))
                        zh_ = zhb[:, QB * dc:QB * (dc + 1)]
                        zl_ = zlb[:, QB * dc:QB * (dc + 1)]
                        for mb_ in range(8):
                            lth = poh[i][:, 128 * mb_:128 * (mb_ + 1)]
                            ltl = pol[i][:, 128 * mb_:128 * (mb_ + 1)]
                            e.matmul(ps[mb_][:, 0:QB], lth, zh_,
                                     start=(dc == 0), stop=False)
                            e.matmul(ps[mb_][:, 0:QB], lth, zl_,
                                     start=False, stop=False)
                            mm = e.matmul(ps[mb_][:, 0:QB], ltl, zh_,
                                          start=False, stop=(dc == DC - 1))
                            if mb_ == 7:
                                mm.then_inc(sem["CT"], 1)
                        ctx.inc("CT")

                ctx.phase({"sync": pO_sync, "tensor": pO_tens},
                          order=["sync", "tensor"])

                dsE, cvE = ctx.cnt["DS"], ctx.cnt["CV"]
                dgE = ctx.cnt["DG"]
                saE = ctx.cnt["SA"]

                def pE_sync(e, qb=qb, cvE=cvE):
                    if causal:
                        return
                    for mb_ in range(8):
                        if mb_ >= 1:
                            e.wait_ge(sem["CV"], cvE + mb_)
                        r_ = slice(128 * mb_, 128 * (mb_ + 1))
                        cs_ = slice(QB * qb, QB * (qb + 1))
                        ctx.ld(e, lhc[:], LNH[r_, cs_])
                        ctx.ld(e, llc[:], LNL[r_, cs_])

                def pE_gps(e, qb=qb, saE=saE, cvE=cvE):
                    for mb_ in range(8):
                        if mb_ >= 1:
                            e.wait_ge(sem["SA"], saE + 16 * mb_)
                        r_ = slice(128 * mb_, 128 * (mb_ + 1))
                        cs_ = slice(QB * qb, QB * (qb + 1))
                        ctx.ld(e, xcb[:], xsrc[r_, cs_], q="DG")
                        e.wait_ge(sem["CV"], cvE + mb_ + 1)
                        ctx.ld(e, xdst[r_, cs_], xcb[:], q="SA")
                    e.wait_ge(sem["SA"], ctx.cnt["SA"])

                def pE_vec(e, dsE=dsE, dgE=dgE):
                    dve_delay(e)
                    for mb_ in range(8):
                        e.wait_ge(sem["DG"], dgE + 16 * (mb_ + 1))
                        if causal:
                            e.tensor_tensor(xcb[:], xcb[:], ps[mb_][:, 0:QB],
                                            ALU.add).then_inc(sem["CV"], 1)
                        else:
                            e.wait_ge(sem["DS"], dsE + 32 * (mb_ + 1))
                            e.tensor_tensor(etm[:], lhc[:], llc[:], ALU.add)
                            e.tensor_tensor(etm[:], etm[:], ps[mb_][:, 0:QB],
                                            ALU.add)
                            e.tensor_tensor(xcb[:], xcb[:], etm[:], ALU.add)\
                                .then_inc(sem["CV"], 1)
                        ctx.inc("CV")

                ctx.phase({"sync": pE_sync, "gpsimd": pE_gps,
                           "vector": pE_vec},
                          order=["sync", "gpsimd", "vector"])

    def halo_gather():
        with ExitStack() as st:
            hb = ctx.sbt(st, "hgb", [128, 2])
            hr = ctx.sbt(st, "hgr", [1, D])

            def pHG_sync(e):
                for dc in range(DC):
                    r_ = slice(128 * dc, 128 * (dc + 1))
                    ctx.ld(e, hb[:, 0:1], Xs[r_, 0:1], slow=True)
                    w = ctx.ld(e, hb[:, 1:2], Xs[r_, T - 1:T], slow=True)
                    e.wait_ge(sem["DS"], w)
                    w = ctx.ld(e, bgin[r_, :], hb[:], q="DA")
                    e.wait_ge(sem["DA"], w)

            ctx.phase({"sync": pHG_sync})

            def pHG_gps(e):
                e.collective_compute("AllGather", ALU.bypass, ins=[bgin],
                                     outs=[bgout], replica_groups=replica)\
                    .then_inc(sem["CG"], 1)
                ctx.inc("CG")
                e.wait_ge(sem["CG"], ctx.cnt["CG"])

            ctx.phase({"gpsimd": pHG_gps})

            def pHG2_sync(e):
                w = ctx.ld(e, hr[:], bgout[0:D, 1:2].transpose([1, 0]),
                           slow=True)
                e.wait_ge(sem["DS"], w)
                w = ctx.ld(e, HALO3[0:1, :], hr[:], q="DA")
                e.wait_ge(sem["DA"], w)
                w = ctx.ld(e, hr[:], bgout[D:2 * D, 0:1].transpose([1, 0]),
                           slow=True)
                e.wait_ge(sem["DS"], w)
                w = ctx.ld(e, HALO3[1:2, :], hr[:], q="DA")
                e.wait_ge(sem["DA"], w)

            ctx.phase({"sync": pHG2_sync})

    def l1_ffn(xsrc, xdst):
        with ExitStack() as st:
            scr = {k: ctx.sbt(st, "f1" + k, [128, T]) for k in "atrvqus"}
            scr["m"] = ctx.sbt(st, "f1m", [128, 2 * (T // 16)])
            lqh = ctx.sbt(st, "f1lh", [128, DC * T], BF16)
            lql = ctx.sbt(st, "f1ll", [128, DC * T], BF16)

            with ExitStack() as st1:
                xc = ctx.sbt(st1, "f1x", [128, T])
                x2 = ctx.sbt(st1, "f1x2", [128, T])
                mb = ctx.sbt(st1, "f1mb", [128, T])
                rb = ctx.sbt(st1, "f1rb", [128, T])
                lnq = ctx.sbt(st1, "f1lq", [128, T])
                ln_stats(xsrc, mb, rb, xc, x2)

                ds1, cv1 = ctx.cnt["DS"], ctx.cnt["CV"]

                def pQ_sync(e):
                    for dc in range(DC):
                        if dc >= 1:
                            e.wait_ge(sem["CV"], cv1 + dc)
                        ctx.ld(e, xc[:], xsrc[128 * dc:128 * (dc + 1), :])

                def pQ_vec(e):
                    for dc in range(DC):
                        e.wait_ge(sem["DS"], ds1 + 16 * (dc + 1))
                        e.tensor_tensor(x2[:], xc[:], mb[:], ALU.subtract)
                        e.tensor_tensor(x2[:], x2[:], rb[:], ALU.mult)
                        fp4_qdq(e, x2, lnq, scr, T, dve_delay)
                        cs_ = slice(T * dc, T * (dc + 1))
                        e.tensor_copy(lqh[:, cs_], lnq[:])
                        e.tensor_tensor(lql[:, cs_], lnq[:], lqh[:, cs_],
                                        ALU.subtract).then_inc(sem["CV"], 1)
                        ctx.inc("CV")

                ctx.phase({"sync": pQ_sync, "vector": pQ_vec},
                          order=["sync", "vector"])

            with ExitStack() as st2:
                wgbh = [ctx.sbt(st2, "f1gh", [128, D], BF16) for _ in range(2)]
                wgbl = [ctx.sbt(st2, "f1gl", [128, D], BF16) for _ in range(2)]
                wubh = [ctx.sbt(st2, "f1uh", [128, D], BF16) for _ in range(2)]
                wubl = [ctx.sbt(st2, "f1ul", [128, D], BF16) for _ in range(2)]
                sgb = ctx.sbt(st2, "f1sg", [128, QB])
                hbf = ctx.sbt(st2, "f1hb", [128, T])
                hfq = ctx.sbt(st2, "f1hq", [128, T])
                hhb = [ctx.sbt(st2, "f1hh", [128, T], BF16) for _ in range(2)]
                hlb = [ctx.sbt(st2, "f1hl", [128, T], BF16) for _ in range(2)]

                ctG, cvG2, csG2 = ctx.cnt["CT"], ctx.cnt["CV"], ctx.cnt["CS"]
                laG, lbG = ctx.cnt["LA"], ctx.cnt["LB"]
                saG, sbG = ctx.cnt["SA"], ctx.cnt["SB"]

                def pG_sync(e):
                    for gc in range(GH):
                        i = gc % 2
                        qs = "LA" if i == 0 else "LB"
                        cg_ = slice(128 * gc, 128 * (gc + 1))
                        if gc >= 2:
                            e.wait_ge(sem["CT"], ctG + (gc - 1))
                        for dc in range(DC):
                            r_ = slice(128 * dc, 128 * (dc + 1))
                            w_ = slice(128 * dc, 128 * (dc + 1))
                            ctx.ld(e, wgbh[i][:, w_], wgqh[r_, cg_], q=qs)
                            ctx.ld(e, wgbl[i][:, w_], wgql[r_, cg_], q=qs)
                            ctx.ld(e, wubh[i][:, w_], wuqh[r_, cg_], q=qs)
                            ctx.ld(e, wubl[i][:, w_], wuql[r_, cg_], q=qs)
                    e.wait_ge(sem["LA"], ctx.cnt["LA"])
                    e.wait_ge(sem["LB"], ctx.cnt["LB"])

                def pG_tens(e):
                    for gc in range(GH):
                        i = gc % 2
                        e.wait_ge(sem["LA" if i == 0 else "LB"],
                                  (laG if i == 0 else lbG)
                                  + 16 * 32 * (gc // 2 + 1))
                        if gc >= 1:
                            e.wait_ge(sem["CV"], cvG2 + 5 * gc)
                        for dc in range(DC):
                            w_ = slice(128 * dc, 128 * (dc + 1))
                            for tb in range(NQB):
                                r_h = lqh[:, T * dc + QB * tb:
                                          T * dc + QB * (tb + 1)]
                                r_l = lql[:, T * dc + QB * tb:
                                          T * dc + QB * (tb + 1)]
                                for u_ in range(2):
                                    wh_ = (wgbh if u_ == 0 else wubh)[i][:, w_]
                                    wl_ = (wgbl if u_ == 0 else wubl)[i][:, w_]
                                    bk = tb + 4 * u_
                                    e.matmul(ps[bk][:, 0:QB], wh_, r_h,
                                             start=(dc == 0), stop=False)
                                    e.matmul(ps[bk][:, 0:QB], wh_, r_l,
                                             start=False, stop=False)
                                    mm = e.matmul(ps[bk][:, 0:QB], wl_, r_h,
                                                  start=False,
                                                  stop=(dc == DC - 1))
                                    if dc == DC - 1 and tb == 3 and u_ == 1:
                                        mm.then_inc(sem["CT"], 1)
                        ctx.inc("CT")

                def pG_scal(e):
                    for gc in range(GH):
                        e.wait_ge(sem["CT"], ctG + gc + 1)
                        for tb in range(NQB):
                            act_delay(e)
                            e.activation(sgb[:], ps[tb][:, 0:QB], ACTF.Silu)\
                                .then_inc(sem["CS"], 1)
                            ctx.inc("CS")
                            e.wait_ge(sem["CV"], cvG2 + 5 * gc + tb + 1)
                        e.wait_ge(sem["CV"], cvG2 + 5 * gc + 5)
                        i = gc % 2
                        qs = "SA" if i == 0 else "SB"
                        cg_ = slice(128 * gc, 128 * (gc + 1))
                        ctx.ld(e, hqh[cg_, :], hhb[i][:], q=qs)
                        ctx.ld(e, hql[cg_, :], hlb[i][:], q=qs)
                    e.wait_ge(sem["SA"], ctx.cnt["SA"])
                    e.wait_ge(sem["SB"], ctx.cnt["SB"])

                def pG_vec(e):
                    for gc in range(GH):
                        i = gc % 2
                        for tb in range(NQB):
                            e.wait_ge(sem["CS"], csG2 + 4 * gc + tb + 1)
                            dve_delay(e)
                            e.tensor_tensor(hbf[:, QB * tb:QB * (tb + 1)],
                                            sgb[:], ps[4 + tb][:, 0:QB],
                                            ALU.mult).then_inc(sem["CV"], 1)
                            ctx.inc("CV")
                        if gc >= 2:
                            e.wait_ge(sem["SA" if i == 0 else "SB"],
                                      (saG if i == 0 else sbG)
                                      + 32 * ((gc - 2) // 2 + 1))
                        fp4_qdq(e, hbf, hfq, scr, T, dve_delay)
                        e.tensor_copy(hhb[i][:], hfq[:])
                        e.tensor_tensor(hlb[i][:], hfq[:], hhb[i][:],
                                        ALU.subtract).then_inc(sem["CV"], 1)
                        ctx.inc("CV")

                ctx.phase({"sync": pG_sync, "tensor": pG_tens,
                           "scalar": pG_scal, "vector": pG_vec},
                          order=["sync", "tensor", "scalar", "vector"])

            with ExitStack() as st3:
                wdbh = [ctx.sbt(st3, "f1dh", [128, D], BF16) for _ in range(2)]
                wdbl = [ctx.sbt(st3, "f1dl", [128, D], BF16) for _ in range(2)]
                hqbh = [ctx.sbt(st3, "f1qh", [128, QB], BF16)
                        for _ in range(2)]
                hqbl = [ctx.sbt(st3, "f1ql", [128, QB], BF16)
                        for _ in range(2)]
                xcb = ctx.sbt(st3, "f1xc", [128, QB])

                for tb in range(NQB):
                    ctD = ctx.cnt["CT"]
                    laD, lbD = ctx.cnt["LA"], ctx.cnt["LB"]
                    cvD = ctx.cnt["CV"]
                    dgD = ctx.cnt["DG"]
                    saD = [ctx.cnt["SA"]]

                    def pD_sync(e, tb=tb, ctD=ctD):
                        for fc in range(GH):
                            i = fc % 2
                            qs = "LA" if i == 0 else "LB"
                            r_ = slice(128 * fc, 128 * (fc + 1))
                            if fc >= 2:
                                e.wait_ge(sem["CT"], ctD + (fc - 1))
                            ctx.ld(e, wdbh[i][:], wdqh[r_, :], q=qs)
                            ctx.ld(e, wdbl[i][:], wdql[r_, :], q=qs)
                            ctx.ld(e, hqbh[i][:],
                                   hqh[r_, QB * tb:QB * (tb + 1)], q=qs)
                            ctx.ld(e, hqbl[i][:],
                                   hql[r_, QB * tb:QB * (tb + 1)], q=qs)
                        e.wait_ge(sem["LA"], ctx.cnt["LA"])
                        e.wait_ge(sem["LB"], ctx.cnt["LB"])

                    def pD_tens(e, ctD=ctD, laD=laD, lbD=lbD):
                        for fc in range(GH):
                            i = fc % 2
                            e.wait_ge(sem["LA" if i == 0 else "LB"],
                                      (laD if i == 0 else lbD)
                                      + 64 * (fc // 2 + 1))
                            for mb_ in range(8):
                                lth = wdbh[i][:, 128 * mb_:128 * (mb_ + 1)]
                                ltl = wdbl[i][:, 128 * mb_:128 * (mb_ + 1)]
                                e.matmul(ps[mb_][:, 0:QB], lth, hqbh[i][:],
                                         start=(fc == 0), stop=False)
                                e.matmul(ps[mb_][:, 0:QB], lth, hqbl[i][:],
                                         start=False, stop=False)
                                mm = e.matmul(ps[mb_][:, 0:QB], ltl,
                                              hqbh[i][:], start=False,
                                              stop=(fc == GH - 1))
                                if mb_ == 7:
                                    mm.then_inc(sem["CT"], 1)
                            ctx.inc("CT")

                    def pD_gps(e, tb=tb, cvD=cvD, saD=saD):
                        for mb_ in range(8):
                            if mb_ >= 1:
                                e.wait_ge(sem["SA"], saD[0] + 16 * mb_)
                            r_ = slice(128 * mb_, 128 * (mb_ + 1))
                            cs_ = slice(QB * tb, QB * (tb + 1))
                            ctx.ld(e, xcb[:], xsrc[r_, cs_], q="DG")
                            e.wait_ge(sem["CV"], cvD + mb_ + 1)
                            ctx.ld(e, xdst[r_, cs_], xcb[:], q="SA")
                        e.wait_ge(sem["SA"], ctx.cnt["SA"])

                    def pD_vec(e, ctD=ctD, dgD=dgD):
                        e.wait_ge(sem["CT"], ctD + GH)
                        dve_delay(e)
                        for mb_ in range(8):
                            e.wait_ge(sem["DG"], dgD + 16 * (mb_ + 1))
                            e.scalar_tensor_tensor(xcb[:], ps[mb_][:, 0:QB],
                                                   0.5, xcb[:], ALU.mult,
                                                   ALU.add)\
                                .then_inc(sem["CV"], 1)
                            ctx.inc("CV")

                    ctx.phase({"sync": pD_sync, "tensor": pD_tens,
                               "gpsimd": pD_gps, "vector": pD_vec},
                              order=["sync", "tensor", "gpsimd", "vector"])

    if KPH >= 2:
        if os.environ.get("SKIPBN") != "1":
            bitnet_quant(inp["w1T"], w1q, D, FF0, 0)
            bitnet_quant(inp["w2T"], w2q, FF0, D, 2)
        if KPH >= 3:
            hilo_split(["pq0", "pk0", "pv0", "po0"])
        if KPH >= 6:
            hilo_split(["pq1", "pk1", "pv1", "po1"])
        if KPH >= 5:
            fp4_weights(inp["wgT"], wgqh, wgql, D, FF1)
            fp4_weights(inp["wuT"], wuqh, wuql, D, FF1)
            fp4_weights(inp["wdT"], wdqh, wdql, FF1, D)
        l0_ffn(inp["xT"], inp["xhalo"], False, Xs)
        if KPH >= 3:
            attn(0, Xs, Xs)
        if KPH >= 4:
            halo_gather()
            l0_ffn(Xs, HALO3, True, Xs)
        if KPH >= 5:
            l1_ffn(Xs, Xs)
        if KPH >= 6:
            attn(1, Xs, Xs)
        if KPH >= 7:
            l1_ffn(Xs, Xs)
    else:
        def pP_sync(e):
            w = ctx.ld(e, Xs, inp["xT"])
            e.wait_ge(sem["DS"], w)

        ctx.phase({"sync": pP_sync})

    with ExitStack() as st:
        ob = ctx.sbt(st, "ob", [128, T])

        def pZ_sync(e):
            for dc in range(DC):
                w = ctx.ld(e, ob[:], Xs[128 * dc:128 * (dc + 1), :])
                e.wait_ge(sem["DS"], w)
                w2_ = ctx.ld(e, yT[128 * dc:128 * (dc + 1), :], ob[:])
                e.wait_ge(sem["DS"], w2_)

        ctx.phase({"sync": pZ_sync})

    return nc, stack


def kernel(**inputs):
    in_maps = _prep_inputs(inputs)
    nc, stack = build_program()
    stack.close()
    res = run_bass_kernel_spmd(nc, in_maps, core_ids=list(range(NC)))
    out = np.empty((B, S, D), np.float32)
    for c in range(NC):
        b, half = c // 2, c % 2
        out[b, half * T:(half + 1) * T, :] = np.asarray(res.results[c]["yT"]).T
    return out


# version 03 (final)
# speedup vs baseline: 1.0401x; 1.0401x over previous
import os
from contextlib import ExitStack

import numpy as np
import ml_dtypes

import concourse.bass as bass
import concourse.bass_isa as bass_isa
import concourse.mybir as mybir
from concourse.bass_utils import run_bass_kernel_spmd

F32 = mybir.dt.float32
BF16 = mybir.dt.bfloat16
I32 = mybir.dt.int32
ALU = mybir.AluOpType
ACTF = mybir.ActivationFunctionType
AX = mybir.AxisListType.X

B, S, D, H, DH = 4, 4096, 1024, 16, 64
FF0, FF1 = 4096, 2816
T = 2048
NC = 8
DC = D // 128
QB = 512
NQB = T // QB
KCN = S // 128
FH = FF0 // 128
GH = FF1 // 128
WAVES = [(0, 6), (6, 12), (12, 16)]
EPS_LN = 1e-5
TP = T + 2
MAIN_BLOCKS = [(1, QB), (1 + QB, QB), (1 + 2 * QB, QB), (1 + 3 * QB, QB)]
KPH = int(os.environ.get("KPHASES", "7"))


def _prep_inputs(inputs):
    x = np.asarray(inputs["L0_attn_x"], dtype=np.float32)
    tr = lambda a: np.ascontiguousarray(np.asarray(a, np.float32).T)
    nat = lambda a: np.ascontiguousarray(np.asarray(a, np.float32))
    shared = dict(
        w1T=tr(inputs["L0_ffn_w1"]), w2T=tr(inputs["L0_ffn_w2"]),
        cw=nat(np.asarray(inputs["L0_ffn_conv_w"], np.float32)[:, 0, :]),
        cb=nat(np.asarray(inputs["L0_ffn_conv_b"], np.float32)[:, None]),
        pq0=nat(inputs["L0_attn_Wq"]), pk0=nat(inputs["L0_attn_Wk"]),
        pv0=nat(inputs["L0_attn_Wv"]), po0=nat(inputs["L0_attn_Wout"]),
        pq1=tr(inputs["L1_attn_wq"]), pk1=tr(inputs["L1_attn_wk"]),
        pv1=tr(inputs["L1_attn_wv"]), po1=tr(inputs["L1_attn_wo"]),
        wgT=tr(inputs["L1_ffn_w_gate"]), wuT=tr(inputs["L1_ffn_w_up"]),
        wdT=tr(inputs["L1_ffn_w_down"]),
        ident=np.eye(128, dtype=np.float32),
        onesq=np.ones((128, 128), dtype=np.float32),
        e2=np.repeat(np.eye(2, dtype=np.float32), 64, axis=1),
    )
    in_maps = []
    for c in range(NC):
        b, half = c // 2, c % 2
        r0 = half * T
        xT = np.ascontiguousarray(x[b, r0:r0 + T, :].T)
        halo = np.zeros((2, D), np.float32)
        if r0 - 1 >= 0:
            halo[0, :] = x[b, r0 - 1, :]
        if r0 + T < S:
            halo[1, :] = x[b, r0 + T, :]
        sel = np.zeros((128, 2), np.float32)
        sel[:, 0] = 1.0 if half == 1 else 0.0
        sel[:, 1] = 1.0 if half == 0 else 0.0
        kidx = np.arange(S).reshape(KCN, 128)
        qidx = r0 + np.arange(T).reshape(NQB, QB)
        m = (kidx[None, :, :, None] <= qidx[:, None, None, :])
        m = np.ascontiguousarray(
            m.reshape(NQB * KCN * 128, QB).astype(ml_dtypes.bfloat16))
        d = dict(shared)
        d.update(xT=xT, xhalo=halo, sel=sel, lmask=m)
        in_maps.append(d)
    return in_maps


SEMS = ["DS", "DA", "DG", "CV", "CS", "CT", "CG", "PH", "LA", "LB", "SA", "SB"]
ENGS = ["sync", "vector", "scalar", "tensor", "gpsimd"]


class _APCM:
    def __init__(self, cm):
        self.cm = cm

    def __enter__(self):
        return self.cm.__enter__().ap()

    def __exit__(self, *a):
        return self.cm.__exit__(*a)


class Ctx:
    def __init__(self, nc, stack):
        self.nc = nc
        self.sem = {n: stack.enter_context(nc.semaphore(n)) for n in SEMS}
        self.cnt = {n: 0 for n in SEMS}
        self.ph_target = 0
        self.uid = 0

    def name(self, base):
        self.uid += 1
        return f"{base}_{self.uid}"

    def sbt(self, st, name, shape, dt=F32):
        return st.enter_context(_APCM(self.nc.sbuf_tensor(self.name(name), shape, dt)))

    def inc(self, name, k=1):
        self.cnt[name] += k
        return self.cnt[name]

    def ld(self, eng, dst, src, q="DS", slow=False):
        eng.dma_start(out=dst, in_=src, allow_slow_non_contiguous=slow)\
            .then_inc(self.sem[q], 16)
        return self.inc(q, 16)

    def phase(self, emitters, order=None):
        order = list(order or ENGS)
        for nme in ENGS:
            if nme not in order:
                order.append(nme)
        with self.nc.Block() as blk:
            start = self.ph_target
            for nme in order:
                fn = emitters.get(nme)
                dec = getattr(blk, nme)

                def wrap(e, fn=fn, start=start):
                    e.wait_ge(self.sem["PH"], start)
                    if fn is not None:
                        fn(e)
                    e.nop().then_inc(self.sem["PH"], 1)

                dec(wrap)
            self.ph_target = start + len(ENGS)


def fp4_qdq(e, xin, xout, scr, w, dly=None):
    a, t, m, r, v, q, u, s = (scr[k] for k in "atmrvqus")
    w16, w32 = w // 16, w // 32
    e.tensor_scalar_mul(a[:, :w], xin[:, :w], -1.0)
    e.tensor_max(a[:, :w], a[:, :w], xin[:, :w])
    e.transpose(t[:, :w], a[:, :w])
    e.tensor_reduce(m[:, :w16],
                    t[:, :w].rearrange("p (a b c) -> p a b c", b=2, c=16),
                    AX, ALU.max)
    if dly is not None:
        dly(e)
    e.tensor_scalar(m[:, :w16], m[:, :w16], 1.0 / 6.0, 1e-12, ALU.mult, ALU.max)
    if dly is not None:
        dly(e)
    e.reciprocal(m[:, w16:2 * w16], m[:, :w16])
    if dly is not None:
        dly(e)
    e.tensor_copy(
        r[:, :w].rearrange("p (a b c) -> p a b c", b=2, c=16),
        m[:, w16:2 * w16].rearrange("p (a b) -> p a b", b=2).unsqueeze(3)
        .broadcast_to([128, w32, 2, 16]))
    e.transpose(v[:, :w], r[:, :w])
    e.tensor_copy(
        r[:, :w].rearrange("p (a b c) -> p a b c", b=2, c=16),
        m[:, :w16].rearrange("p (a b) -> p a b", b=2).unsqueeze(3)
        .broadcast_to([128, w32, 2, 16]))
    e.transpose(s[:, :w], r[:, :w])
    e.tensor_tensor(v[:, :w], a[:, :w], v[:, :w], ALU.mult)
    e.tensor_single_scalar(q[:, :w], v[:, :w], 0.25, ALU.is_gt)
    for c in (0.75, 1.25, 1.75):
        e.scalar_tensor_tensor(q[:, :w], v[:, :w], c, q[:, :w], ALU.is_gt, ALU.add)
    e.tensor_single_scalar(u[:, :w], v[:, :w], 2.5, ALU.is_gt)
    e.scalar_tensor_tensor(u[:, :w], v[:, :w], 3.5, u[:, :w], ALU.is_gt, ALU.add)
    e.scalar_tensor_tensor(q[:, :w], q[:, :w], 0.5, u[:, :w], ALU.mult, ALU.add)
    e.tensor_scalar(u[:, :w], v[:, :w], 5.0, 2.0, ALU.is_gt, ALU.mult)
    e.tensor_tensor(q[:, :w], q[:, :w], u[:, :w], ALU.add)
    e.tensor_scalar(u[:, :w], xin[:, :w], 0.0, None, ALU.is_ge)
    e.tensor_scalar(u[:, :w], u[:, :w], 2.0, 1.0, ALU.mult, ALU.subtract)
    e.tensor_tensor(q[:, :w], q[:, :w], u[:, :w], ALU.mult)
    e.tensor_tensor(xout[:, :w], q[:, :w], s[:, :w], ALU.mult)


def round_int_bf16(e, v, y0, y0f, rr, out, w):
    e.tensor_copy(y0[:, :w], v[:, :w])
    e.tensor_copy(y0f[:, :w], y0[:, :w])
    e.tensor_tensor(rr[:, :w], v[:, :w], y0f[:, :w], ALU.subtract)
    e.scalar_tensor_tensor(y0f[:, :w], rr[:, :w], 0.5, y0f[:, :w], ALU.is_gt, ALU.add)
    e.scalar_tensor_tensor(y0f[:, :w], rr[:, :w], -0.5, y0f[:, :w], ALU.is_ge, ALU.add)
    return e.tensor_scalar(out[:, :w], y0f[:, :w], 1.0, None, ALU.subtract)


def build_program():
    nc = bass.Bass("TRN2", target_bir_lowering=False, detect_race_conditions=False)
    stack = ExitStack()
    ctx = Ctx(nc, stack)
    sem = ctx.sem

    inp = {}
    for name, shape in [
        ("xT", [D, T]), ("xhalo", [2, D]), ("sel", [128, 2]), ("ident", [128, 128]),
        ("w1T", [D, FF0]), ("w2T", [FF0, D]), ("cw", [FF0, 3]), ("cb", [FF0, 1]),
        ("pq0", [D, D]), ("pk0", [D, D]), ("pv0", [D, D]), ("po0", [D, D]),
        ("pq1", [D, D]), ("pk1", [D, D]), ("pv1", [D, D]), ("po1", [D, D]),
        ("wgT", [D, FF1]), ("wuT", [D, FF1]), ("wdT", [FF1, D]),
        ("e2", [2, 128]), ("onesq", [128, 128]),
    ]:
        inp[name] = nc.dram_tensor(name, shape, F32, kind="ExternalInput").ap()
    inp["lmask"] = nc.dram_tensor("lmask", [NQB * KCN * 128, QB], BF16,
                                  kind="ExternalInput").ap()
    yT = nc.dram_tensor("yT", [D, T], F32, kind="ExternalOutput").ap()

    def dram(name, shape, dt=F32):
        return nc.dram_tensor(name, shape, dt, kind="Internal").ap()

    Xs = dram("Xs", [D, T])
    HALO3 = dram("HALO3", [2, D])
    LNH = dram("LNH", [D, T], BF16)
    LNL = dram("LNL", [D, T], BF16)
    HST = dram("HST", [FF0, QB])
    w1q = dram("w1q", [D, FF0], BF16)
    w2q = dram("w2q", [FF0, D], BF16)
    pwq = {}
    for nm in ["pq0", "pk0", "pv0", "po0", "pq1", "pk1", "pv1", "po1"]:
        pwq[nm + "h"] = dram(nm + "h", [D, D], BF16)
        pwq[nm + "l"] = dram(nm + "l", [D, D], BF16)
    wgqh = dram("wgqh", [D, FF1], BF16); wgql = dram("wgql", [D, FF1], BF16)
    wuqh = dram("wuqh", [D, FF1], BF16); wuql = dram("wuql", [D, FF1], BF16)
    wdqh = dram("wdqh", [FF1, D], BF16); wdql = dram("wdql", [FF1, D], BF16)
    kpin = dram("kpin", [D, 2 * T], BF16)
    kpout = dram("kpout", [2 * D, 2 * T], BF16)
    vain = dram("vain", [T, H * 65])
    vaout = dram("vaout", [S, H * 65])
    bgin = dram("bgin", [D, 2])
    bgout = dram("bgout", [2 * D, 2])
    Qt = dram("Qt", [D, T])
    hqh = dram("hqh", [FF1, T], BF16)
    hql = dram("hql", [FF1, T], BF16)

    A = lambda n, s_, d=F32: nc.alloc_sbuf_tensor(n, s_, d).ap()
    sb_ident = A("sb_ident", [128, 128])
    sb_sel = A("sb_sel", [128, 2])
    sb_ones1 = A("sb_ones1", [1, 128])
    sb_onesq = A("sb_onesq", [128, 128])
    sb_onesc = A("sb_onesc", [128, 1])
    sb_e2 = A("sb_e2", [2, 128])
    sb_scal = A("sb_scal", [128, 8])
    cwsb = A("cwsb", [128, FH * 3])
    cbsb = A("cbsb", [128, FH])

    ps = [nc.alloc_psum_tensor(f"ps{i}", [128, 512], F32).ap() for i in range(8)]
    sb_dly = A("sb_dly", [1, 128])
    asb_dly = A("asb_dly", [1, 128])

    def dve_delay(e):
        e.tensor_copy(sb_dly[0:1, :], sb_ident[0:1, :])
        e.tensor_copy(sb_dly[0:1, :], sb_ident[0:1, :])

    def act_delay(e):
        e.activation(asb_dly[0:1, :], sb_ident[0:1, :], ACTF.Copy)
        e.activation(asb_dly[0:1, :], sb_ident[0:1, :], ACTF.Copy)
    replica = [[2 * i, 2 * i + 1] for i in range(NC // 2)]

    def p0_sync(e):
        ctx.ld(e, sb_ident[:], inp["ident"])
        ctx.ld(e, sb_sel[:], inp["sel"])
        ctx.ld(e, sb_onesq[:], inp["onesq"])
        ctx.ld(e, sb_ones1[:], inp["onesq"][0:1, :])
        ctx.ld(e, sb_onesc[:], inp["onesq"][:, 0:1], slow=True)
        ctx.ld(e, cwsb.rearrange("p (a k) -> p a k", k=3),
               inp["cw"].rearrange("(a p) k -> a p k", p=128).transpose([1, 0, 2]))
        ctx.ld(e, cbsb[:],
               inp["cb"].rearrange("(a p) k -> a p k", p=128)
               .transpose([1, 0, 2]).rearrange("p a k -> p (a k)"), slow=True)
        w = ctx.ld(e, sb_e2[:], inp["e2"])
        e.wait_ge(sem["DS"], w)

    def p0_vec(e):
        pass

    ctx.phase({"sync": p0_sync, "vector": p0_vec})

    def bitnet_quant(src, dst, rows, cols, scol):
        nch = rows // 128
        with ExitStack() as st:
            wb = [ctx.sbt(st, "bnw", [128, cols]) for _ in range(2)]
            qb_ = [ctx.sbt(st, "bnq", [128, cols], BF16) for _ in range(2)]
            ab = ctx.sbt(st, "bna", [128, cols])
            sg = ctx.sbt(st, "bns", [128, cols])
            partb = ctx.sbt(st, "bnpb", [128, 32])
            bc = ctx.sbt(st, "bnbc", [128, 1])
            scl = ctx.sbt(st, "bnscl", [128, 1])

            ds0, cv0 = ctx.cnt["DS"], ctx.cnt["CV"]

            la0, lb0 = ctx.cnt["LA"], ctx.cnt["LB"]

            def p_sync(e):
                for k in range(nch):
                    if k >= 2:
                        e.wait_ge(sem["CV"], cv0 + (k - 1))
                    ctx.ld(e, wb[k % 2][:], src[128 * k:128 * (k + 1), :],
                           q="LA" if k % 2 == 0 else "LB")
                e.wait_ge(sem["LA"], ctx.cnt["LA"])
                e.wait_ge(sem["LB"], ctx.cnt["LB"])

            def p_vec(e):
                for k in range(nch):
                    e.wait_ge(sem["LA" if k % 2 == 0 else "LB"],
                              (la0 if k % 2 == 0 else lb0) + 16 * (k // 2 + 1))
                    e.reduce_sum(partb[:, k:k + 1], wb[k % 2][:], AX,
                                 apply_absolute_value=True)\
                        .then_inc(sem["CV"], 1)
                    ctx.inc("CV")

            ct0 = ctx.cnt["CT"]

            def p_tens(e):
                e.wait_ge(sem["CV"], cv0 + nch)
                e.matmul(ps[6][0:1, 0:nch], sb_onesc[:], partb[:, 0:nch],
                         start=True, stop=True).then_inc(sem["CT"], 1)
                ctx.inc("CT")
                e.wait_ge(sem["CV"], cv0 + nch + 1)
                e.matmul(ps[7][0:128, 0:1], sb_onesq[:], bc[:],
                         start=True, stop=True).then_inc(sem["CT"], 1)
                ctx.inc("CT")

            def p_vec_b(e):
                e.wait_ge(sem["CT"], ct0 + 1)
                dve_delay(e)
                e.tensor_scalar_mul(bc[:], sb_ident[:, 0:1], 0.0)
                e.reduce_sum(bc[0:1, 0:1], ps[6][0:1, 0:nch], AX)\
                    .then_inc(sem["CV"], 1)
                ctx.inc("CV")
                e.wait_ge(sem["CT"], ct0 + 2)
                dve_delay(e)
                e.tensor_scalar(scl[:], ps[7][0:128, 0:1], 1.0 / (rows * cols),
                                1e-5, ALU.mult, ALU.max)
                dve_delay(e)
                e.tensor_scalar_mul(sb_scal[:, scol:scol + 1], scl[:], 0.5)
                e.tensor_scalar_mul(sb_scal[:, scol + 1:scol + 2], scl[:],
                                    1.0 / 127.0)
                dve_delay(e)
                e.nop().then_inc(sem["CV"], 1)
                ctx.inc("CV")

            def _pv(e):
                p_vec(e)
                p_vec_b(e)

            ctx.phase({"sync": p_sync, "vector": _pv, "tensor": p_tens},
                      order=["sync", "vector", "tensor"])

            if os.environ.get("DBGBN") == "1" and scol == 0:
                def pbn_sync(e):
                    ctx.ld(e, Xs[640:768, 0:nch], partb[:, 0:nch], slow=True)
                    ctx.ld(e, Xs[640:768, 8:9], bc[:], slow=True)
                    ctx.ld(e, Xs[640:768, 9:10], scl[:], slow=True)
                    ctx.ld(e, Xs[640:768, 10:12], sb_scal[:, 0:2], slow=True)
                    e.wait_ge(sem["DS"], ctx.cnt["DS"])

                ctx.phase({"sync": pbn_sync})

            cv1 = ctx.cnt["CV"]
            la1, lb1 = ctx.cnt["LA"], ctx.cnt["LB"]

            def p_sync2(e):
                for k in range(nch):
                    if k >= 2:
                        e.wait_ge(sem["CV"], cv1 + (k - 1))
                    ctx.ld(e, wb[k % 2][:], src[128 * k:128 * (k + 1), :],
                           q="LA" if k % 2 == 0 else "LB")
                e.wait_ge(sem["LA"], ctx.cnt["LA"])
                e.wait_ge(sem["LB"], ctx.cnt["LB"])

            def p_vec2(e):
                for k in range(nch):
                    i = k % 2
                    e.wait_ge(sem["LA" if k % 2 == 0 else "LB"],
                              (la1 if k % 2 == 0 else lb1) + 16 * (k // 2 + 1))
                    e.tensor_scalar_mul(ab[:], wb[i][:], -1.0)
                    e.tensor_max(ab[:], ab[:], wb[i][:])
                    e.tensor_scalar(sg[:], wb[i][:], 0.0, None, ALU.is_ge)
                    e.tensor_scalar(sg[:], sg[:], 2.0, 1.0, ALU.mult, ALU.subtract)
                    e.tensor_single_scalar(ab[:], ab[:], sb_scal[:, scol:scol + 1],
                                           ALU.is_gt)
                    e.tensor_tensor(qb_[i][:], ab[:], sg[:], ALU.mult)\
                        .then_inc(sem["CV"], 1)
                    ctx.inc("CV")

            def p_scal2(e):
                for k in range(nch):
                    e.wait_ge(sem["CV"], cv1 + (k + 1))
                    ctx.ld(e, dst[128 * k:128 * (k + 1), :], qb_[k % 2][:], q="DA")
                e.wait_ge(sem["DA"], ctx.cnt["DA"])

            ctx.phase({"sync": p_sync2, "vector": p_vec2, "scalar": p_scal2})

    def hilo_split(names):
        with ExitStack() as st:
            wb = [ctx.sbt(st, "sp", [128, D]) for _ in range(2)]
            hb = [ctx.sbt(st, "sph", [128, D], BF16) for _ in range(2)]
            lb = [ctx.sbt(st, "spl", [128, D], BF16) for _ in range(2)]
            cv0 = ctx.cnt["CV"]
            la0, lb0 = ctx.cnt["LA"], ctx.cnt["LB"]

            def p_sync(e):
                k = 0
                for nm in names:
                    for dc in range(DC):
                        if k >= 2:
                            e.wait_ge(sem["CV"], cv0 + (k - 1))
                        ctx.ld(e, wb[k % 2][:], inp[nm][128 * dc:128 * (dc + 1), :],
                               q="LA" if k % 2 == 0 else "LB")
                        k += 1
                e.wait_ge(sem["LA"], ctx.cnt["LA"])
                e.wait_ge(sem["LB"], ctx.cnt["LB"])

            def p_vec(e):
                for k in range(len(names) * DC):
                    i = k % 2
                    e.wait_ge(sem["LA" if k % 2 == 0 else "LB"],
                              (la0 if k % 2 == 0 else lb0) + 16 * (k // 2 + 1))
                    e.tensor_copy(hb[i][:], wb[i][:])
                    e.tensor_tensor(lb[i][:], wb[i][:], hb[i][:], ALU.subtract)\
                        .then_inc(sem["CV"], 1)
                    ctx.inc("CV")

            def p_scal(e):
                k = 0
                for nm in names:
                    for dc in range(DC):
                        e.wait_ge(sem["CV"], cv0 + (k + 1))
                        r_ = slice(128 * dc, 128 * (dc + 1))
                        ctx.ld(e, pwq[nm + "h"][r_, :], hb[k % 2][:], q="DA")
                        ctx.ld(e, pwq[nm + "l"][r_, :], lb[k % 2][:], q="DA")
                        k += 1
                e.wait_ge(sem["DA"], ctx.cnt["DA"])

            ctx.phase({"sync": p_sync, "vector": p_vec, "scalar": p_scal})

    def fp4_weights(src, dsth, dstl, rows, cols):
        nch = rows // 128
        with ExitStack() as st:
            wb = [ctx.sbt(st, "f4w", [128, cols]) for _ in range(2)]
            hb = [ctx.sbt(st, "f4h", [128, cols], BF16) for _ in range(2)]
            lb = [ctx.sbt(st, "f4l", [128, cols], BF16) for _ in range(2)]
            xq = ctx.sbt(st, "f4xq", [128, cols])
            scr = {k: ctx.sbt(st, "f4" + k, [128, cols]) for k in "atrvqus"}
            scr["m"] = ctx.sbt(st, "f4m", [128, 2 * (cols // 16)])
            cv0 = ctx.cnt["CV"]
            la0, lb0 = ctx.cnt["LA"], ctx.cnt["LB"]

            def p_sync(e):
                for k in range(nch):
                    if k >= 2:
                        e.wait_ge(sem["CV"], cv0 + (k - 1))
                    ctx.ld(e, wb[k % 2][:], src[128 * k:128 * (k + 1), :],
                           q="LA" if k % 2 == 0 else "LB")
                e.wait_ge(sem["LA"], ctx.cnt["LA"])
                e.wait_ge(sem["LB"], ctx.cnt["LB"])

            def p_vec(e):
                for k in range(nch):
                    i = k % 2
                    e.wait_ge(sem["LA" if k % 2 == 0 else "LB"],
                              (la0 if k % 2 == 0 else lb0) + 16 * (k // 2 + 1))
                    fp4_qdq(e, wb[i], xq, scr, cols, dve_delay)
                    e.tensor_copy(hb[i][:], xq[:])
                    e.tensor_tensor(lb[i][:], xq[:], hb[i][:], ALU.subtract)\
                        .then_inc(sem["CV"], 1)
                    ctx.inc("CV")

            def p_scal(e):
                for k in range(nch):
                    e.wait_ge(sem["CV"], cv0 + (k + 1))
                    r_ = slice(128 * k, 128 * (k + 1))
                    ctx.ld(e, dsth[r_, :], hb[k % 2][:], q="DA")
                    ctx.ld(e, dstl[r_, :], lb[k % 2][:], q="DA")
                e.wait_ge(sem["DA"], ctx.cnt["DA"])

            ctx.phase({"sync": p_sync, "vector": p_vec, "scalar": p_scal})

    def l0_ffn(xsrc, halosrc, mask_halo, xdst):
        with ExitStack() as st:
            row = ctx.sbt(st, "ffrow", [1, 2 * TP])
            rsb = ctx.sbt(st, "ffrs", [128, TP])
            vv = ctx.sbt(st, "ffvv", [128, QB + 2])
            y0 = ctx.sbt(st, "ffy0", [128, QB + 2], I32)
            y0f = ctx.sbt(st, "ffy0f", [128, QB + 2])
            rr = ctx.sbt(st, "ffrr", [128, QB + 2])
            xq = ctx.sbt(st, "ffxq", [128, DC * TP], BF16)
            tr1 = ctx.sbt(st, "fftr", [128, 16])
            s16 = ctx.sbt(st, "s16", [16, 128])
            rseg = [ctx.sbt(st, "rseg", [128, 512]) for _ in range(2)]

            def pZr_vec(e):
                for i in range(2):
                    for j in range(4):
                        e.tensor_scalar_mul(rseg[i][:, 128 * j:128 * (j + 1)],
                                            sb_ident[:], 0.0)

            ctx.phase({"vector": pZr_vec})
            st2 = ExitStack()
            _mbst = st if os.environ.get("CMODE") == "mbouter" else st2
            xc = ctx.sbt(st2, "ffx", [128, TP])
            x2 = ctx.sbt(st2, "ffx2", [128, TP])
            lnc = ctx.sbt(st2, "ffln", [128, TP])
            mxa = ctx.sbt(st2, "ffmx", [128, TP])
            mb = ctx.sbt(_mbst, "ffmb", [128, TP])
            rb = ctx.sbt(_mbst, "ffrb", [128, TP])
            sxb = ctx.sbt(st2, "ffsx", [128, TP])

            ds0, cs0, ct0 = ctx.cnt["DS"], ctx.cnt["CS"], ctx.cnt["CT"]

            def pA_sync(e):
                for dc in range(DC):
                    if dc >= 1:
                        e.wait_ge(sem["CT"], ct0 + dc)
                    chunk_load(e, xc, xsrc, halosrc, dc)

            def pA_scal(e):
                for dc in range(DC):
                    e.wait_ge(sem["DS"], ds0 + 48 * (dc + 1))
                    if mask_halo:
                        e.activation(xc[:, 0:1], xc[:, 0:1], ACTF.Copy,
                                     scale=sb_sel[:, 0:1])
                        e.activation(xc[:, TP - 1:TP], xc[:, TP - 1:TP], ACTF.Copy,
                                     scale=sb_sel[:, 1:2])
                    e.activation(x2[:], xc[:], ACTF.Square).then_inc(sem["CS"], 1)
                    ctx.inc("CS")

            def pA_tens(e):
                for dc in range(DC):
                    e.wait_ge(sem["CS"], cs0 + dc + 1)
                    stt, spp = (dc == 0), (dc == DC - 1)
                    for bi, (o, w) in enumerate(MAIN_BLOCKS):
                        e.matmul(ps[bi][0:128, 0:w], sb_onesq[:], xc[:, o:o + w],
                                 start=stt, stop=spp)
                        e.matmul(ps[4 + bi][0:128, 0:w], sb_onesq[:],
                                 x2[:, o:o + w], start=stt, stop=spp)\
                            .then_inc(sem["CT"], 1) if bi == 3 else \
                            e.matmul(ps[4 + bi][0:128, 0:w], sb_onesq[:],
                                     x2[:, o:o + w], start=stt, stop=spp)
                    ctx.inc("CT")

            ctx.phase({"sync": pA_sync, "scalar": pA_scal, "tensor": pA_tens},
                      order=["sync", "scalar", "tensor"])

            cvB, csB = ctx.cnt["CV"], ctx.cnt["CS"]

            def pB_vec(e):
                for bi, (o, w) in enumerate(MAIN_BLOCKS):
                    e.tensor_scalar_mul(mb[:, o:o + w], ps[bi][0:128, 0:w], 1.0 / D)
                    e.tensor_scalar_mul(rb[:, o:o + w], ps[4 + bi][0:128, 0:w],
                                        1.0 / D)
                e.tensor_tensor(lnc[:, 1:T + 1], mb[:, 1:T + 1], mb[:, 1:T + 1],
                                ALU.mult)
                e.tensor_tensor(rb[:, 1:T + 1], rb[:, 1:T + 1], lnc[:, 1:T + 1],
                                ALU.subtract)
                e.tensor_scalar_add(rb[:, 1:T + 1], rb[:, 1:T + 1], EPS_LN)\
                    .then_inc(sem["CV"], 1)
                ctx.inc("CV")

            def pB_scal(e):
                e.wait_ge(sem["CV"], cvB + 1)
                e.activation(rb[:, 1:T + 1], rb[:, 1:T + 1], ACTF.Sqrt)\
                    .then_inc(sem["CS"], 1)
                ctx.inc("CS")

            def pB_vec2(e):
                e.wait_ge(sem["CS"], csB + 1)
                for o, w in MAIN_BLOCKS:
                    e.reciprocal(rb[:, o:o + w], rb[:, o:o + w])
                e.nop().then_inc(sem["CV"], 1)
                ctx.inc("CV")

            def _pB_v(e):
                pB_vec(e)
                pB_vec2(e)

            ctx.phase({"vector": _pB_v, "scalar": pB_scal},
                      order=["vector", "scalar"])

            ds2, cs2, ct2, cv2 = (ctx.cnt[k] for k in ["DS", "CS", "CT", "CV"])

            def pA2_sync(e):
                for dc in range(DC):
                    if dc >= 1:
                        e.wait_ge(sem["CT"], ct2 + dc)
                    r_ = slice(128 * dc, 128 * (dc + 1))
                    ctx.ld(e, xc[:, 0:1],
                           halosrc[0:1, r_].rearrange("a (p b) -> (a p) b", b=1))
                    ctx.ld(e, xc[:, TP - 1:TP],
                           halosrc[1:2, r_].rearrange("a (p b) -> (a p) b", b=1))

            def pA2_scal(e):
                for dc in range(DC):
                    e.wait_ge(sem["DS"], ds2 + 32 * (dc + 1))
                    if mask_halo:
                        e.activation(xc[:, 0:1], xc[:, 0:1], ACTF.Copy,
                                     scale=sb_sel[:, 0:1])
                        e.activation(xc[:, TP - 1:TP], xc[:, TP - 1:TP], ACTF.Copy,
                                     scale=sb_sel[:, 1:2])
                    e.activation(x2[:, 0:1], xc[:, 0:1], ACTF.Square)
                    e.activation(x2[:, TP - 1:TP], xc[:, TP - 1:TP], ACTF.Square)\
                        .then_inc(sem["CS"], 1)
                    ctx.inc("CS")

            def pA2_tens(e):
                for dc in range(DC):
                    e.wait_ge(sem["CS"], cs2 + dc + 1)
                    stt, spp = (dc == 0), (dc == DC - 1)
                    e.matmul(ps[0][0:128, 0:1], sb_onesq[:], xc[:, 0:1],
                             start=stt, stop=spp)
                    e.matmul(ps[1][0:128, 0:1], sb_onesq[:], xc[:, TP - 1:TP],
                             start=stt, stop=spp)
                    e.matmul(ps[2][0:128, 0:1], sb_onesq[:], x2[:, 0:1],
                             start=stt, stop=spp)
                    e.matmul(ps[3][0:128, 0:1], sb_onesq[:], x2[:, TP - 1:TP],
                             start=stt, stop=spp).then_inc(sem["CT"], 1)
                    ctx.inc("CT")

            ctx.phase({"sync": pA2_sync, "scalar": pA2_scal, "tensor": pA2_tens},
                      order=["sync", "scalar", "tensor"])

            cvB2, csB2 = ctx.cnt["CV"], ctx.cnt["CS"]

            def pB2_vec(e):
                for j, col in enumerate((0, TP - 1)):
                    e.tensor_scalar_mul(mb[:, col:col + 1], ps[j][0:128, 0:1],
                                        1.0 / D)
                    e.tensor_scalar_mul(rb[:, col:col + 1], ps[2 + j][0:128, 0:1],
                                        1.0 / D)
                    e.tensor_tensor(lnc[:, col:col + 1], mb[:, col:col + 1],
                                    mb[:, col:col + 1], ALU.mult)
                    e.tensor_tensor(rb[:, col:col + 1], rb[:, col:col + 1],
                                    lnc[:, col:col + 1], ALU.subtract)
                    e.tensor_scalar_add(rb[:, col:col + 1], rb[:, col:col + 1],
                                        EPS_LN)
                e.nop().then_inc(sem["CV"], 1)
                ctx.inc("CV")

            def pB2_scal(e):
                e.wait_ge(sem["CV"], cvB2 + 1)
                e.activation(rb[:, 0:1], rb[:, 0:1], ACTF.Sqrt)
                e.activation(rb[:, TP - 1:TP], rb[:, TP - 1:TP], ACTF.Sqrt)\
                    .then_inc(sem["CS"], 1)
                ctx.inc("CS")

            def pB2_vec2(e):
                e.wait_ge(sem["CS"], csB2 + 1)
                e.reciprocal(rb[:, 0:1], rb[:, 0:1])
                e.reciprocal(rb[:, TP - 1:TP], rb[:, TP - 1:TP])
                e.nop().then_inc(sem["CV"], 1)
                ctx.inc("CV")

            def _pB2_v(e):
                pB2_vec(e)
                pB2_vec2(e)

            ctx.phase({"vector": _pB2_v, "scalar": pB2_scal},
                      order=["vector", "scalar"])
            if os.environ.get("FFCUT") == "C":
                st2.close()
                return

            dsD, csD, cvD = ctx.cnt["DS"], ctx.cnt["CS"], ctx.cnt["CV"]

            def pD_sync(e):
                for dc in range(DC):
                    if dc >= 1:
                        e.wait_ge(sem["CV"], cvD + 2 * dc)
                    chunk_load(e, xc, xsrc, halosrc, dc)

            def pD_scal(e):
                for dc in range(DC):
                    e.wait_ge(sem["DS"], dsD + 48 * (dc + 1))
                    if mask_halo:
                        e.activation(xc[:, 0:1], xc[:, 0:1], ACTF.Copy,
                                     scale=sb_sel[:, 0:1])
                        e.activation(xc[:, TP - 1:TP], xc[:, TP - 1:TP], ACTF.Copy,
                                     scale=sb_sel[:, 1:2]).then_inc(sem["CS"], 1)
                    else:
                        e.nop().then_inc(sem["CS"], 1)
                    ctx.inc("CS")

            def pD_vec(e):
                for dc in range(DC):
                    e.wait_ge(sem["CS"], csD + dc + 1)
                    e.tensor_tensor(lnc[:], xc[:], mb[:], ALU.subtract)
                    e.tensor_tensor(lnc[:], lnc[:], rb[:], ALU.mult)\
                        .then_inc(sem["CV"], 1)
                    ctx.inc("CV")
                    e.tensor_scalar_mul(x2[:], lnc[:], -1.0)
                    e.tensor_max(lnc[:], lnc[:], x2[:])
                    if dc == 0:
                        e.tensor_copy(mxa[:], lnc[:]).then_inc(sem["CV"], 1)
                    else:
                        e.tensor_max(mxa[:], mxa[:], lnc[:]).then_inc(sem["CV"], 1)
                    ctx.inc("CV")

            ctx.phase({"sync": pD_sync, "scalar": pD_scal, "vector": pD_vec},
                      order=["sync", "scalar", "vector"])
            if os.environ.get("FFCUT") == "D":
                st2.close()
                return

            ctE, cvE, cgE = ctx.cnt["CT"], ctx.cnt["CV"], ctx.cnt["CG"]

            def pE_tens(e):
                for tb in range(16):
                    cseg = slice(1 + 128 * tb, 1 + 128 * (tb + 1))
                    e.transpose(ps[tb % 2][:, 0:128], mxa[:, cseg], sb_ident[:]).then_inc(sem["CT"], 1)
                    ctx.inc("CT")
                    if tb >= 1:
                        e.wait_ge(sem["CV"], cvE + tb)
                e.transpose(ps[3][0:1, 0:128], mxa[:, 0:1], sb_ident[:])
                e.transpose(ps[4][0:1, 0:128], mxa[:, TP - 1:TP], sb_ident[:]).then_inc(sem["CT"], 1)
                ctx.inc("CT")

            def pE_gps(e):
                e.nop().then_inc(sem["CG"], 1)
                ctx.inc("CG")

            def pE_vec(e):
                for tb in range(16):
                    e.wait_ge(sem["CT"], ctE + tb + 1)
                    dve_delay(e)
                    e.tensor_reduce(tr1[:, tb:tb + 1], ps[tb % 2][:, 0:128],
                                    AX, ALU.max).then_inc(sem["CV"], 1)
                    ctx.inc("CV")
                e.wait_ge(sem["CT"], ctE + 17)
                dve_delay(e)
                e.tensor_reduce(vv[0:1, 0:1], ps[3][0:1, 0:128], AX, ALU.max)
                e.tensor_reduce(vv[0:1, 2:3], ps[4][0:1, 0:128], AX, ALU.max)\
                    .then_inc(sem["CV"], 1)
                ctx.inc("CV")

            ctx.phase({"tensor": pE_tens, "gpsimd": pE_gps, "vector": pE_vec},
                      order=["tensor", "gpsimd", "vector"])
            if os.environ.get("FFCUT") == "E":
                st2.close()
                return

            ctE2, cvE2, cgE2, dsE2 = (ctx.cnt[k] for k in ["CT", "CV", "CG", "DS"])

            def pE2_tens(e):
                e.transpose(ps[2][0:16, 0:128], tr1[:], sb_ident[:]).then_inc(sem["CT"], 1)
                ctx.inc("CT")

            def pE2_vec(e):
                e.wait_ge(sem["CT"], ctE2 + 1)
                dve_delay(e)
                e.tensor_copy(s16[:], ps[2][0:16, 0:128]).then_inc(sem["CV"], 1)
                ctx.inc("CV")
                e.wait_ge(sem["DS"], dsE2 + 16 * 16)
                e.wait_ge(sem["CG"], cgE2)
                e.tensor_copy(row[0:1, TP + T:TP + T + 1], vv[0:1, 0:1])
                e.tensor_copy(row[0:1, TP + T + 1:TP + T + 2], vv[0:1, 2:3])
                e.tensor_scalar(row[0:1, TP:TP + T + 2], row[0:1, TP:TP + T + 2],
                                1e-5, None, ALU.max)
                e.tensor_single_scalar(row[0:1, 0:T + 2], row[0:1, TP:TP + T + 2],
                                       sb_scal[0:1, 1:2], ALU.mult)
                e.reciprocal(row[0:1, TP:TP + T], row[0:1, TP:TP + T])
                e.reciprocal(row[0:1, TP + T:TP + T + 2], row[0:1, TP + T:TP + T + 2])
                e.tensor_scalar_mul(row[0:1, TP:TP + T + 2],
                                    row[0:1, TP:TP + T + 2], 127.0)\
                    .then_inc(sem["CV"], 1)
                ctx.inc("CV")

            def pE2_sync(e):
                e.wait_ge(sem["CV"], cvE2 + 1)
                for tb in range(16):
                    ctx.ld(e, row[0:1, TP + 128 * tb:TP + 128 * (tb + 1)],
                           s16[tb:tb + 1, :])

            ctx.phase({"tensor": pE2_tens, "vector": pE2_vec, "sync": pE2_sync},
                      order=["tensor", "vector", "sync"])
            if os.environ.get("FFCUT") == "E2":
                st2.close()
                return


            ctF = ctx.cnt["CT"]
            cvF = ctx.cnt["CV"]
            segsF = []
            for so_, dst in [(TP, sxb), (0, rsb)]:
                for c0 in range(0, T + 2, 512):
                    segsF.append((so_ + c0, min(512, T + 2 - c0), dst, c0))

            def pF_vec(e):
                for k, (so_, w_, dst, c0) in enumerate(segsF):
                    if k >= 2:
                        e.wait_ge(sem["CT"], ctF + k - 1)
                    e.tensor_copy(rseg[k % 2][0:1, 0:w_], row[0:1, so_:so_ + w_])\
                        .then_inc(sem["CV"], 1)
                    ctx.inc("CV")
                    e.wait_ge(sem["CT"], ctF + k + 1)
                    dve_delay(e)
                    if c0 + w_ <= T:
                        e.tensor_copy(dst[:, 1 + c0:1 + c0 + w_],
                                      ps[k % 4][0:128, 0:w_])
                    else:
                        e.tensor_copy(dst[:, 0:1], ps[k % 4][0:128, 0:1])
                        e.tensor_copy(dst[:, TP - 1:TP], ps[k % 4][0:128, 1:2])
                    e.nop().then_inc(sem["CV"], 1)
                    ctx.inc("CV")

            def pF_tens(e):
                for k, (so_, w_, dst, c0) in enumerate(segsF):
                    e.wait_ge(sem["CV"], cvF + 2 * k + 1)
                    e.matmul(ps[k % 4][0:128, 0:w_], sb_onesq[:],
                             rseg[k % 2][:, 0:w_], start=True, stop=True).then_inc(sem["CT"], 1)
                    ctx.inc("CT")

            ctx.phase({"vector": pF_vec, "tensor": pF_tens},
                      order=["vector", "tensor"])
            if os.environ.get("FFCUT") == "F":
                st2.close()
                return

            dsG, csG, cvG = ctx.cnt["DS"], ctx.cnt["CS"], ctx.cnt["CV"]

            def pG_sync(e):
                for dc in range(DC):
                    if dc >= 1:
                        e.wait_ge(sem["CV"], cvG + dc)
                    chunk_load(e, xc, xsrc, halosrc, dc)

            def pG_scal(e):
                for dc in range(DC):
                    e.wait_ge(sem["DS"], dsG + 48 * (dc + 1))
                    if mask_halo:
                        e.activation(xc[:, 0:1], xc[:, 0:1], ACTF.Copy,
                                     scale=sb_sel[:, 0:1])
                        e.activation(xc[:, TP - 1:TP], xc[:, TP - 1:TP], ACTF.Copy,
                                     scale=sb_sel[:, 1:2]).then_inc(sem["CS"], 1)
                    else:
                        e.nop().then_inc(sem["CS"], 1)
                    ctx.inc("CS")

            def pG_vec(e):
                gb = [(0, QB + 1)] + [(1 + QB * i, QB) for i in range(1, 4)]
                gb.append((1 + 3 * QB + QB, 1))
                for dc in range(DC):
                    e.wait_ge(sem["CS"], csG + dc + 1)
                    e.tensor_tensor(lnc[:], xc[:], mb[:], ALU.subtract)
                    e.tensor_tensor(lnc[:], lnc[:], rb[:], ALU.mult)
                    last = None
                    for o, w in gb:
                        e.tensor_tensor(vv[:, 0:w], lnc[:, o:o + w],
                                        sxb[:, o:o + w], ALU.mult)
                        last = round_int_bf16(e, vv, y0, y0f, rr,
                                              xq[:, TP * dc + o:TP * dc + o + w],
                                              w)
                    last.then_inc(sem["CV"], 1)
                    ctx.inc("CV")

            ctx.phase({"sync": pG_sync, "scalar": pG_scal, "vector": pG_vec},
                      order=["sync", "scalar", "vector"])

            if os.environ.get("DBGDUMP") == "1":
                def pDBG_vec(e):
                    e.tensor_copy(lnc[:, 0:T], mb[:, 1:T + 1])
                    e.nop().then_inc(sem["CV"], 1)
                    ctx.inc("CV")
                    e.tensor_copy(lnc[:, 0:T], rb[:, 1:T + 1])\
                        .then_inc(sem["CV"], 1)
                    ctx.inc("CV")

                def pDBG_gps(e):
                    cv0_ = ctx.cnt["CV"] - 2
                    e.wait_ge(sem["CV"], cv0_ + 1)
                    pass

                def pDBG_sync(e):
                    ctx.ld(e, Xs[0:128, :], mb[:, 1:T + 1])
                    ctx.ld(e, Xs[128:256, :], rb[:, 1:T + 1])
                    ctx.ld(e, Xs[256:384, :], sxb[:, 1:T + 1])
                    ctx.ld(e, Xs[384:512, :], rsb[:, 1:T + 1])
                    w = ctx.ld(e, Xs[512:640, 0:T // 2],
                               xq.bitcast(F32)[:, 0:T // 2])
                    e.wait_ge(sem["DS"], w)

                ctx.phase({"sync": pDBG_sync})
                st2.close()
                return

            st2.close()
            if os.environ.get("FFCUT") == "G":
                return

            w1sb = ctx.sbt(st, "w1sb", [128, DC * FF0], BF16)
            w2sb = [ctx.sbt(st, "w2sb", [128, D], BF16) for _ in range(2)]
            hdv = ctx.sbt(st, "hdv", [128, QB + 2])
            cvo = ctx.sbt(st, "cvo", [128, QB])
            slb = [ctx.sbt(st, "slb", [128, QB]) for _ in range(2)]
            h2q = ctx.sbt(st, "h2q", [128, FH * QB], BF16)
            mx2 = ctx.sbt(st, "mx2", [128, QB])
            sx2 = ctx.sbt(st, "sx2", [128, QB])
            rs2 = ctx.sbt(st, "rs2", [128, QB])
            xcb = ctx.sbt(st, "xcb", [128, QB])
            srow = ctx.sbt(st, "srow", [1, 2 * QB])

            dsW = ctx.cnt["DS"]

            def pW_sync(e):
                for dc in range(DC):
                    ctx.ld(e, w1sb[:, FF0 * dc:FF0 * (dc + 1)],
                           w1q[128 * dc:128 * (dc + 1), :])
                e.wait_ge(sem["DS"], dsW + 16 * DC)

            ctx.phase({"sync": pW_sync})

            for tbk in range(NQB):
                c0 = QB * tbk

                ctH, cvH, csH, daH = (ctx.cnt[k] for k in ["CT", "CV", "CS", "DA"])

                def pH_tens(e, c0=c0, cvH=cvH):
                    for hc in range(FH):
                        pb = hc % 2
                        if hc >= 2:
                            e.wait_ge(sem["CV"], cvH + 2 * (hc - 1))
                        for dc in range(DC):
                            lt = w1sb[:, FF0 * dc + 128 * hc:FF0 * dc + 128 * (hc + 1)]
                            e.matmul(ps[pb][:, 0:QB],
                                     lt, xq[:, TP * dc + c0 + 1:TP * dc + c0 + 1 + QB],
                                     start=(dc == 0), stop=(dc == DC - 1))
                        for dc in range(DC):
                            lt = w1sb[:, FF0 * dc + 128 * hc:FF0 * dc + 128 * (hc + 1)]
                            e.matmul(ps[2 + pb][:, 0:1], lt,
                                     xq[:, TP * dc + c0:TP * dc + c0 + 1],
                                     start=(dc == 0), stop=(dc == DC - 1))
                        for dc in range(DC):
                            lt = w1sb[:, FF0 * dc + 128 * hc:FF0 * dc + 128 * (hc + 1)]
                            mm = e.matmul(
                                ps[4 + pb][:, 0:1], lt,
                                xq[:, TP * dc + c0 + QB + 1:TP * dc + c0 + QB + 2],
                                start=(dc == 0), stop=(dc == DC - 1))
                            if dc == DC - 1:
                                mm.then_inc(sem["CT"], 1)
                        ctx.inc("CT")

                def pH_vec(e, c0=c0, ctH=ctH, csH=csH):
                    for hc in range(FH):
                        pb = hc % 2
                        e.wait_ge(sem["CT"], ctH + hc + 1)
                        if hc >= 1:
                            e.wait_ge(sem["CS"], csH + hc)
                        dve_delay(e)
                        e.tensor_tensor(hdv[:, 1:QB + 1], ps[pb][:, 0:QB],
                                        rsb[:, c0 + 1:c0 + 1 + QB], ALU.mult)
                        e.tensor_tensor(hdv[:, 0:1], ps[2 + pb][:, 0:1],
                                        rsb[:, c0:c0 + 1], ALU.mult)
                        e.tensor_tensor(hdv[:, QB + 1:QB + 2], ps[4 + pb][:, 0:1],
                                        rsb[:, c0 + QB + 1:c0 + QB + 2], ALU.mult)
                        e.tensor_scalar(cvo[:], hdv[:, 1:QB + 1],
                                        cwsb[:, 3 * hc + 1:3 * hc + 2],
                                        cbsb[:, hc:hc + 1], ALU.mult, ALU.add)
                        e.scalar_tensor_tensor(cvo[:], hdv[:, 0:QB],
                                               cwsb[:, 3 * hc:3 * hc + 1], cvo[:],
                                               ALU.mult, ALU.add)
                        e.scalar_tensor_tensor(cvo[:], hdv[:, 2:QB + 2],
                                               cwsb[:, 3 * hc + 2:3 * hc + 3], cvo[:],
                                               ALU.mult, ALU.add)\
                            .then_inc(sem["CV"], 1)
                        ctx.inc("CV")
                        e.wait_ge(sem["CS"], csH + hc + 1)
                        e.tensor_scalar_mul(hdv[:, 0:QB], slb[pb][:], -1.0)
                        e.tensor_max(hdv[:, 0:QB], hdv[:, 0:QB], slb[pb][:])
                        if hc == 0:
                            e.tensor_copy(mx2[:], hdv[:, 0:QB]).then_inc(sem["CV"], 1)
                        else:
                            e.tensor_max(mx2[:], mx2[:], hdv[:, 0:QB])\
                                .then_inc(sem["CV"], 1)
                        ctx.inc("CV")

                sa0, sb0 = ctx.cnt["SA"], ctx.cnt["SB"]

                def pH_scal(e, cvH=cvH, sa0=sa0, sb0=sb0):
                    for hc in range(FH):
                        e.wait_ge(sem["CV"], cvH + 2 * hc + 1)
                        if hc >= 2:
                            p_ = hc % 2
                            e.wait_ge(sem["SA" if p_ == 0 else "SB"],
                                      (sa0 if p_ == 0 else sb0)
                                      + 16 * ((hc - 2) // 2 + 1))
                        e.activation(slb[hc % 2][:], cvo[:], ACTF.Silu)\
                            .then_inc(sem["CS"], 1)
                        ctx.inc("CS")

                def pH_gps(e, csH=csH):
                    for hc in range(FH):
                        e.wait_ge(sem["CS"], csH + hc + 1)
                        ctx.ld(e, HST[128 * hc:128 * (hc + 1), :], slb[hc % 2][:],
                               q="SA" if hc % 2 == 0 else "SB")
                    e.wait_ge(sem["SA"], ctx.cnt["SA"])
                    e.wait_ge(sem["SB"], ctx.cnt["SB"])

                ctx.phase({"tensor": pH_tens, "vector": pH_vec, "scalar": pH_scal,
                           "gpsimd": pH_gps},
                          order=["tensor", "vector", "scalar", "gpsimd"])
                if os.environ.get("FFCUT") == "H":
                    return
                if os.environ.get("DBGH") == "1" and tbk == 0:
                    def pDH_sync(e):
                        for j in range(16):
                            ctx.ld(e, Xs[128 * (j % 8):128 * (j % 8 + 1),
                                         512 * (j // 8):512 * (j // 8) + 512],
                                   HST[128 * j:128 * (j + 1), :])
                        e.wait_ge(sem["DS"], ctx.cnt["DS"])

                    ctx.phase({"sync": pDH_sync})
                    return

                ctI, cvI = ctx.cnt["CT"], ctx.cnt["CV"]

                dsI = ctx.cnt["DS"]
                _imode = os.environ.get("IMODE", "")

                def pI_tens(e, ctI=ctI, cvI=cvI):
                    if _imode in ("not", "none"):
                        for _ in range(5):
                            e.nop().then_inc(sem["CT"], 1)
                            ctx.inc("CT")
                        return
                    for qt in range(QB // 128):
                        if qt >= 2:
                            e.wait_ge(sem["CV"], cvI + qt - 1)
                        e.transpose(ps[qt % 2][:, 0:128],
                                    mx2[:, 128 * qt:128 * (qt + 1)], sb_ident[:]).then_inc(sem["CT"], 1)
                        ctx.inc("CT")
                    e.wait_ge(sem["CV"], cvI + 5)
                    e.transpose(ps[2][0:16, 0:128], tr1[:], sb_ident[:]).then_inc(sem["CT"], 1)
                    ctx.inc("CT")

                def pI_vec(e, ctI=ctI, dsI=dsI):
                    if _imode in ("not", "none"):
                        for _ in range(6):
                            e.nop().then_inc(sem["CV"], 1)
                            ctx.inc("CV")
                        e.wait_ge(sem["DS"], dsI + 16 * 4)
                        e.tensor_scalar(srow[0:1, 0:QB], srow[0:1, 0:QB], 1e-5,
                                        None, ALU.max)
                        e.reciprocal(srow[0:1, QB:2 * QB], srow[0:1, 0:QB])
                        e.tensor_scalar_mul(srow[0:1, QB:2 * QB],
                                            srow[0:1, QB:2 * QB], 127.0)
                        e.tensor_single_scalar(srow[0:1, 0:QB], srow[0:1, 0:QB],
                                               sb_scal[0:1, 3:4], ALU.mult)\
                            .then_inc(sem["CV"], 1)
                        ctx.inc("CV")
                        return
                    for qt in range(QB // 128):
                        e.wait_ge(sem["CT"], ctI + qt + 1)
                        dve_delay(e)
                        e.tensor_reduce(tr1[:, qt:qt + 1], ps[qt % 2][:, 0:128],
                                        AX, ALU.max).then_inc(sem["CV"], 1)
                        ctx.inc("CV")
                    e.nop().then_inc(sem["CV"], 1)
                    ctx.inc("CV")
                    e.wait_ge(sem["CT"], ctI + 5)
                    dve_delay(e)
                    e.tensor_copy(s16[0:4, :], ps[2][0:4, 0:128])\
                        .then_inc(sem["CV"], 1)
                    ctx.inc("CV")
                    e.wait_ge(sem["DS"], dsI + (0 if _imode in ("nos", "none") else 16 * 4))
                    if _imode not in ("nos", "none"):
                        e.tensor_scalar(srow[0:1, 0:QB], srow[0:1, 0:QB], 1e-5, None,
                                        ALU.max)
                        e.reciprocal(srow[0:1, QB:2 * QB], srow[0:1, 0:QB])
                        e.tensor_scalar_mul(srow[0:1, QB:2 * QB],
                                            srow[0:1, QB:2 * QB], 127.0)
                        e.tensor_single_scalar(srow[0:1, 0:QB], srow[0:1, 0:QB],
                                               sb_scal[0:1, 3:4], ALU.mult)\
                            .then_inc(sem["CV"], 1)
                    else:
                        e.nop().then_inc(sem["CV"], 1)
                    ctx.inc("CV")

                def pI_sync(e, cvI=cvI):
                    e.wait_ge(sem["CV"], cvI + 6)
                    if _imode not in ("nos", "none"):
                        for qt in range(4):
                            ctx.ld(e, srow[0:1, 128 * qt:128 * (qt + 1)],
                                   s16[qt:qt + 1, :])

                ctx.phase({"tensor": pI_tens, "vector": pI_vec, "sync": pI_sync},
                          order=["tensor", "vector", "sync"])
                if os.environ.get("FFCUT") == "I":
                    return

                ctI2, cvI2 = ctx.cnt["CT"], ctx.cnt["CV"]

                def pI2_vec(e, cvI2=cvI2, ctI2=ctI2):
                    e.tensor_copy(rseg[0][0:1, 0:QB], srow[0:1, QB:2 * QB])
                    e.tensor_copy(rseg[1][0:1, 0:QB], srow[0:1, 0:QB])\
                        .then_inc(sem["CV"], 1)
                    ctx.inc("CV")
                    e.wait_ge(sem["CT"], ctI2 + 1)
                    dve_delay(e)
                    e.tensor_copy(sx2[:], ps[0][0:128, 0:QB])
                    e.tensor_copy(rs2[:], ps[1][0:128, 0:QB]).then_inc(sem["CV"], 1)
                    ctx.inc("CV")

                def pI2_tens(e, cvI2=cvI2):
                    e.wait_ge(sem["CV"], cvI2 + 1)
                    e.matmul(ps[0][0:128, 0:QB], sb_onesq[:], rseg[0][:, 0:QB],
                             start=True, stop=True)
                    e.matmul(ps[1][0:128, 0:QB], sb_onesq[:], rseg[1][:, 0:QB],
                             start=True, stop=True).then_inc(sem["CT"], 1)
                    ctx.inc("CT")

                ctx.phase({"vector": pI2_vec, "tensor": pI2_tens},
                          order=["vector", "tensor"])
                if os.environ.get("FFCUT") == "I2":
                    return
                if os.environ.get("DBGI") == "1" and tbk == 0:
                    def pDI_sync(e):
                        ctx.ld(e, Xs[0:128, 0:QB], sx2[:])
                        ctx.ld(e, Xs[128:256, 0:QB], rs2[:])
                        ctx.ld(e, Xs[256:257, 0:2 * QB], srow[0:1, :], slow=True)
                        ctx.ld(e, Xs[384:512, 0:QB], mx2[:])
                        ctx.ld(e, Xs[512:640, 0:16], tr1[:, 0:16], slow=True)
                        e.wait_ge(sem["DS"], ctx.cnt["DS"])

                    ctx.phase({"sync": pDI_sync})
                    return

                cvJ = ctx.cnt["CV"]
                laJ, lbJ = ctx.cnt["LA"], ctx.cnt["LB"]

                def pJ_sync(e, cvJ=cvJ):
                    for hc in range(FH):
                        if hc >= 2:
                            e.wait_ge(sem["CV"], cvJ + (hc - 1))
                        ctx.ld(e, slb[hc % 2][:], HST[128 * hc:128 * (hc + 1), :],
                               q="LA" if hc % 2 == 0 else "LB")
                    e.wait_ge(sem["LA"], ctx.cnt["LA"])
                    e.wait_ge(sem["LB"], ctx.cnt["LB"])

                _jmode = os.environ.get("JMODE", "")

                def pJ_vec(e, laJ=laJ, lbJ=lbJ):
                    for hc in range(FH):
                        e.wait_ge(sem["LA" if hc % 2 == 0 else "LB"],
                                  (laJ if hc % 2 == 0 else lbJ)
                                  + 16 * (hc // 2 + 1))
                        e.tensor_tensor(vv[:, 0:QB], slb[hc % 2][:], sx2[:], ALU.mult)
                        if _jmode != "norc":
                            round_int_bf16(e, vv, y0, y0f, rr,
                                           h2q[:, QB * hc:QB * (hc + 1)], QB)\
                                .then_inc(sem["CV"], 1)
                        else:
                            e.tensor_copy(h2q[:, QB * hc:QB * (hc + 1)],
                                          vv[:, 0:QB]).then_inc(sem["CV"], 1)
                        ctx.inc("CV")

                ctx.phase({"sync": pJ_sync, "vector": pJ_vec},
                          order=["sync", "vector"])
                if os.environ.get("FFCUT") == "J":
                    return
                if os.environ.get("DBGJ") == "1" and tbk == 0:
                    def pDJ_sync(e):
                        ctx.ld(e, Xs[0:128, 0:4 * QB // 2],
                               h2q.bitcast(F32)[:, 0:4 * QB // 2])
                        e.wait_ge(sem["DS"], ctx.cnt["DS"])

                    ctx.phase({"sync": pDJ_sync})
                    return

                dsK, dgK, ctK, cvK = (ctx.cnt[k] for k in ["DS", "DG", "CT", "CV"])
                ctx_saK = [ctx.cnt["SA"]]

                laK, lbK = ctx.cnt["LA"], ctx.cnt["LB"]

                def pK_sync(e, ctK=ctK):
                    for hc in range(FH):
                        ctx.ld(e, w2sb[hc % 2][:], w2q[128 * hc:128 * (hc + 1), :],
                               q="LA" if hc % 2 == 0 else "LB")
                        if hc >= 1:
                            e.wait_ge(sem["CT"], ctK + hc)
                    e.wait_ge(sem["LA"], ctx.cnt["LA"])
                    e.wait_ge(sem["LB"], ctx.cnt["LB"])

                dgK2 = ctx.cnt["DG"]

                def pK_gps(e, dgK2=dgK2, cvK=cvK, c0=c0):
                    for dc in range(DC):
                        if dc >= 1:
                            e.wait_ge(sem["SA"], ctx_saK[0] + 16 * dc)
                        ctx.ld(e, xcb[:], xsrc[128 * dc:128 * (dc + 1), c0:c0 + QB],
                               q="DG")
                        e.wait_ge(sem["CV"], cvK + dc + 1)
                        ctx.ld(e, xdst[128 * dc:128 * (dc + 1), c0:c0 + QB],
                               xcb[:], q="SA")
                    e.wait_ge(sem["SA"], ctx.cnt["SA"])

                def pK_tens(e, laK=laK, lbK=lbK):
                    for hc in range(FH):
                        e.wait_ge(sem["LA" if hc % 2 == 0 else "LB"],
                                  (laK if hc % 2 == 0 else lbK)
                                  + 16 * (hc // 2 + 1))
                        for dc in range(DC):
                            mm = e.matmul(ps[dc][:, 0:QB],
                                          w2sb[hc % 2][:, 128 * dc:128 * (dc + 1)],
                                          h2q[:, QB * hc:QB * (hc + 1)],
                                          start=(hc == 0), stop=(hc == FH - 1))
                            if dc == DC - 1:
                                mm.then_inc(sem["CT"], 1)
                        ctx.inc("CT")

                def pK_vec(e, ctK=ctK, dgK2=dgK2, c0=c0):
                    e.wait_ge(sem["CT"], ctK + FH)
                    dve_delay(e)
                    for dc in range(DC):
                        e.wait_ge(sem["DG"], dgK2 + 16 * (dc + 1))
                        e.tensor_tensor(cvo[:], ps[dc][:, 0:QB], rs2[:], ALU.mult)
                        e.scalar_tensor_tensor(xcb[:], cvo[:], 0.5, xcb[:],
                                               ALU.mult, ALU.add)\
                            .then_inc(sem["CV"], 1)
                        ctx.inc("CV")

                ctx.phase({"sync": pK_sync, "tensor": pK_tens, "gpsimd": pK_gps,
                           "vector": pK_vec},
                          order=["sync", "tensor", "gpsimd", "vector"])

    def chunk_load(e, xc, xsrc, halosrc, dc):
        r_ = slice(128 * dc, 128 * (dc + 1))
        ctx.ld(e, xc[:, 1:T + 1], xsrc[r_, :])
        ctx.ld(e, xc[:, 0:1],
               halosrc[0:1, r_].rearrange("a (p b) -> (a p) b", b=1))
        ctx.ld(e, xc[:, TP - 1:TP],
               halosrc[1:2, r_].rearrange("a (p b) -> (a p) b", b=1))

    def ln_stats(xsrc, mb, rb, xc, x2):
        ds0, cs0, ct0 = ctx.cnt["DS"], ctx.cnt["CS"], ctx.cnt["CT"]

        def pS_sync(e):
            for dc in range(DC):
                if dc >= 1:
                    e.wait_ge(sem["CT"], ct0 + dc)
                ctx.ld(e, xc[:], xsrc[128 * dc:128 * (dc + 1), :])

        def pS_scal(e):
            for dc in range(DC):
                e.wait_ge(sem["DS"], ds0 + 16 * (dc + 1))
                e.activation(x2[:], xc[:], ACTF.Square).then_inc(sem["CS"], 1)
                ctx.inc("CS")

        def pS_tens(e):
            for dc in range(DC):
                e.wait_ge(sem["CS"], cs0 + dc + 1)
                stt, spp = (dc == 0), (dc == DC - 1)
                for bi in range(4):
                    o = QB * bi
                    e.matmul(ps[bi][0:128, 0:QB], sb_onesq[:], xc[:, o:o + QB],
                             start=stt, stop=spp)
                    mm = e.matmul(ps[4 + bi][0:128, 0:QB], sb_onesq[:],
                                  x2[:, o:o + QB], start=stt, stop=spp)
                    if bi == 3:
                        mm.then_inc(sem["CT"], 1)
                ctx.inc("CT")

        ctx.phase({"sync": pS_sync, "scalar": pS_scal, "tensor": pS_tens},
                  order=["sync", "scalar", "tensor"])

        cvB, csB = ctx.cnt["CV"], ctx.cnt["CS"]

        def pB_v(e):
            dve_delay(e)
            for bi in range(4):
                o = QB * bi
                e.tensor_scalar_mul(mb[:, o:o + QB], ps[bi][0:128, 0:QB], 1.0 / D)
                e.tensor_scalar_mul(rb[:, o:o + QB], ps[4 + bi][0:128, 0:QB],
                                    1.0 / D)
            e.tensor_tensor(x2[:, 0:T], mb[:, 0:T], mb[:, 0:T], ALU.mult)
            e.tensor_tensor(rb[:, 0:T], rb[:, 0:T], x2[:, 0:T], ALU.subtract)
            e.tensor_scalar_add(rb[:, 0:T], rb[:, 0:T], EPS_LN)\
                .then_inc(sem["CV"], 1)
            ctx.inc("CV")
            e.wait_ge(sem["CS"], csB + 1)
            for bi in range(4):
                o = QB * bi
                e.reciprocal(rb[:, o:o + QB], rb[:, o:o + QB])
            e.tensor_copy(sb_dly[0:1, :], sb_ident[0:1, :])
            e.nop().then_inc(sem["CV"], 1)
            ctx.inc("CV")

        def pB_s(e):
            e.wait_ge(sem["CV"], cvB + 1)
            e.activation(rb[:, 0:T], rb[:, 0:T], ACTF.Sqrt).then_inc(sem["CS"], 1)
            ctx.inc("CS")

        ctx.phase({"vector": pB_v, "scalar": pB_s}, order=["vector", "scalar"])

    def ln_hilo(xsrc):
        with ExitStack() as st:
            xc = ctx.sbt(st, "lhx", [128, T])
            x2 = ctx.sbt(st, "lhx2", [128, T])
            mb = ctx.sbt(st, "lhmb", [128, T])
            rb = ctx.sbt(st, "lhrb", [128, T])
            hb = [ctx.sbt(st, "lhh", [128, T], BF16) for _ in range(2)]
            lb = [ctx.sbt(st, "lhl", [128, T], BF16) for _ in range(2)]
            ln_stats(xsrc, mb, rb, xc, x2)
            ds1, cv1 = ctx.cnt["DS"], ctx.cnt["CV"]

            def pL_sync(e):
                for dc in range(DC):
                    if dc >= 1:
                        e.wait_ge(sem["CV"], cv1 + dc)
                    ctx.ld(e, xc[:], xsrc[128 * dc:128 * (dc + 1), :])

            def pL_vec(e):
                for dc in range(DC):
                    i = dc % 2
                    e.wait_ge(sem["DS"], ds1 + 16 * (dc + 1))
                    e.tensor_tensor(x2[:], xc[:], mb[:], ALU.subtract)
                    e.tensor_tensor(x2[:], x2[:], rb[:], ALU.mult)
                    e.tensor_copy(hb[i][:], x2[:])
                    e.tensor_tensor(lb[i][:], x2[:], hb[i][:], ALU.subtract)\
                        .then_inc(sem["CV"], 1)
                    ctx.inc("CV")

            def pL_scal(e):
                for dc in range(DC):
                    e.wait_ge(sem["CV"], cv1 + dc + 1)
                    r_ = slice(128 * dc, 128 * (dc + 1))
                    ctx.ld(e, LNH[r_, :], hb[dc % 2][:], q="DA")
                    ctx.ld(e, LNL[r_, :], lb[dc % 2][:], q="DA")
                e.wait_ge(sem["DA"], ctx.cnt["DA"])

            ctx.phase({"sync": pL_sync, "vector": pL_vec, "scalar": pL_scal},
                      order=["sync", "vector", "scalar"])

    def attn(layer, xsrc, xdst):
        causal = (layer == 1)
        pn = ["pq1", "pk1", "pv1", "po1"] if causal else \
             ["pq0", "pk0", "pv0", "po0"]
        ln_hilo(xsrc)

        with ExitStack() as st:
            lnh_sb = ctx.sbt(st, "atlh", [128, DC * T], BF16)
            lnl_sb = ctx.sbt(st, "atll", [128, DC * T], BF16)
            wsh = ctx.sbt(st, "atwh", [128, DC * D], BF16)
            wsl = ctx.sbt(st, "atwl", [128, DC * D], BF16)
            osb = [ctx.sbt(st, "atos", [128, QB]) for _ in range(2)]
            ohb = [ctx.sbt(st, "atoh", [128, QB], BF16) for _ in range(2)]
            olb = [ctx.sbt(st, "atol", [128, QB], BF16) for _ in range(2)]
            vosb = [ctx.sbt(st, "atvo", [128, D]) for _ in range(2)]

            def pLd_sync(e):
                for dc in range(DC):
                    r_ = slice(128 * dc, 128 * (dc + 1))
                    ctx.ld(e, lnh_sb[:, T * dc:T * (dc + 1)], LNH[r_, :])
                    ctx.ld(e, lnl_sb[:, T * dc:T * (dc + 1)], LNL[r_, :])
                e.wait_ge(sem["DS"], ctx.cnt["DS"])

            ctx.phase({"sync": pLd_sync})

            def wload(e, nm):
                for dc in range(DC):
                    r_ = slice(128 * dc, 128 * (dc + 1))
                    ctx.ld(e, wsh[:, D * dc:D * (dc + 1)], pwq[nm + "h"][r_, :])
                    ctx.ld(e, wsl[:, D * dc:D * (dc + 1)], pwq[nm + "l"][r_, :])
                e.wait_ge(sem["DS"], ctx.cnt["DS"])

            def proj_fm(nm, kind):
                ctx.phase({"sync": lambda e: wload(e, nm)})
                ctP, cvP = ctx.cnt["CT"], ctx.cnt["CV"]
                sa0, sb0 = ctx.cnt["SA"], ctx.cnt["SB"]
                spc = 16 if kind == "q" else 32

                def pM_tens(e):
                    for tb in range(NQB):
                        if tb >= 1:
                            e.wait_ge(sem["CV"], cvP + tb * 8)
                        for dc in range(DC):
                            for mb_ in range(8):
                                lth = wsh[:, D * dc + 128 * mb_:
                                          D * dc + 128 * (mb_ + 1)]
                                ltl = wsl[:, D * dc + 128 * mb_:
                                          D * dc + 128 * (mb_ + 1)]
                                rh = lnh_sb[:, T * dc + QB * tb:
                                            T * dc + QB * (tb + 1)]
                                rl = lnl_sb[:, T * dc + QB * tb:
                                            T * dc + QB * (tb + 1)]
                                e.matmul(ps[mb_][:, 0:QB], lth, rh,
                                         start=(dc == 0), stop=False)
                                e.matmul(ps[mb_][:, 0:QB], lth, rl,
                                         start=False, stop=False)
                                mm = e.matmul(ps[mb_][:, 0:QB], ltl, rh,
                                              start=False, stop=(dc == DC - 1))
                                if dc == DC - 1:
                                    mm.then_inc(sem["CT"], 1)
                                    ctx.inc("CT")

                def pM_vec(e):
                    k = 0
                    for tb in range(NQB):
                        for mb_ in range(8):
                            e.wait_ge(sem["CT"], ctP + k + 1)
                            dve_delay(e)
                            i = k % 2
                            if k >= 2:
                                e.wait_ge(sem["SA" if i == 0 else "SB"],
                                          (sa0 if i == 0 else sb0)
                                          + spc * ((k - 2) // 2 + 1))
                            if kind == "q":
                                e.tensor_copy(osb[i][:], ps[mb_][:, 0:QB])\
                                    .then_inc(sem["CV"], 1)
                            else:
                                e.tensor_copy(ohb[i][:], ps[mb_][:, 0:QB])
                                e.tensor_tensor(olb[i][:], ps[mb_][:, 0:QB],
                                                ohb[i][:], ALU.subtract)\
                                    .then_inc(sem["CV"], 1)
                            ctx.inc("CV")
                            k += 1

                def pM_scal(e):
                    k = 0
                    for tb in range(NQB):
                        for mb_ in range(8):
                            e.wait_ge(sem["CV"], cvP + k + 1)
                            i = k % 2
                            qs = "SA" if i == 0 else "SB"
                            r_ = slice(128 * mb_, 128 * (mb_ + 1))
                            if kind == "q":
                                ctx.ld(e, Qt[r_, QB * tb:QB * (tb + 1)],
                                       osb[i][:], q=qs)
                            else:
                                ctx.ld(e, kpin[r_, QB * tb:QB * (tb + 1)],
                                       ohb[i][:], q=qs)
                                ctx.ld(e, kpin[r_, T + QB * tb:
                                               T + QB * (tb + 1)],
                                       olb[i][:], q=qs)
                            k += 1
                    e.wait_ge(sem["SA"], ctx.cnt["SA"])
                    e.wait_ge(sem["SB"], ctx.cnt["SB"])

                ctx.phase({"tensor": pM_tens, "vector": pM_vec,
                           "scalar": pM_scal},
                          order=["tensor", "vector", "scalar"])

            proj_fm(pn[0], "q")
            proj_fm(pn[1], "k")

            ctx.phase({"sync": lambda e: wload(e, pn[2])})
            ctV, cvV = ctx.cnt["CT"], ctx.cnt["CV"]
            sa0v, sb0v = ctx.cnt["SA"], ctx.cnt["SB"]

            def pV_tens(e):
                for kt in range(16):
                    if kt >= 4:
                        e.wait_ge(sem["CV"], cvV + (kt - 3) * 2)
                    bk = (kt % 4) * 2
                    for dc in range(DC):
                        lth = lnh_sb[:, T * dc + 128 * kt:
                                     T * dc + 128 * (kt + 1)]
                        ltl = lnl_sb[:, T * dc + 128 * kt:
                                     T * dc + 128 * (kt + 1)]
                        for mh in range(2):
                            rh = wsh[:, D * dc + 512 * mh:
                                     D * dc + 512 * (mh + 1)]
                            rl = wsl[:, D * dc + 512 * mh:
                                     D * dc + 512 * (mh + 1)]
                            e.matmul(ps[bk + mh][:, 0:512], lth, rh,
                                     start=(dc == 0), stop=False)
                            e.matmul(ps[bk + mh][:, 0:512], lth, rl,
                                     start=False, stop=False)
                            mm = e.matmul(ps[bk + mh][:, 0:512], ltl, rh,
                                          start=False, stop=(dc == DC - 1))
                            if dc == DC - 1 and mh == 1:
                                mm.then_inc(sem["CT"], 1)
                    ctx.inc("CT")

            def pV_vec(e):
                for kt in range(16):
                    e.wait_ge(sem["CT"], ctV + kt + 1)
                    dve_delay(e)
                    bk = (kt % 4) * 2
                    i = kt % 2
                    if kt >= 2:
                        e.wait_ge(sem["SA" if i == 0 else "SB"],
                                  (sa0v if i == 0 else sb0v)
                                  + 16 * 17 * ((kt - 2) // 2 + 1))
                    e.tensor_copy(vosb[i][:, 0:512], ps[bk][:, 0:512])
                    e.tensor_copy(vosb[i][:, 512:1024], ps[bk + 1][:, 0:512])\
                        .then_inc(sem["CV"], 2)
                    ctx.inc("CV", 2)

            def pV_scal(e):
                for kt in range(16):
                    e.wait_ge(sem["CV"], cvV + (kt + 1) * 2)
                    i = kt % 2
                    qs = "SA" if i == 0 else "SB"
                    for h in range(H):
                        ctx.ld(e, vain[128 * kt:128 * (kt + 1),
                                       65 * h:65 * h + 64],
                               vosb[i][:, 64 * h:64 * (h + 1)], q=qs,
                               slow=True)
                    ctx.ld(e, vain.rearrange("t (h c) -> t h c", c=65)
                           [128 * kt:128 * (kt + 1), :, 64:65],
                           sb_onesq[:, 0:H].unsqueeze(2), q=qs, slow=True)
                e.wait_ge(sem["SA"], ctx.cnt["SA"])
                e.wait_ge(sem["SB"], ctx.cnt["SB"])

            ctx.phase({"tensor": pV_tens, "vector": pV_vec, "scalar": pV_scal},
                      order=["tensor", "vector", "scalar"])

        def pGt_gps(e):
            e.collective_compute("AllGather", ALU.bypass, ins=[kpin],
                                 outs=[kpout], replica_groups=replica)\
                .then_inc(sem["CG"], 1)
            ctx.inc("CG")
            e.collective_compute("AllGather", ALU.bypass, ins=[vain],
                                 outs=[vaout], replica_groups=replica)\
                .then_inc(sem["CG"], 1)
            ctx.inc("CG")
            e.wait_ge(sem["CG"], ctx.cnt["CG"])

        ctx.phase({"gpsimd": pGt_gps})

        with ExitStack() as st:
            PT = ctx.sbt(st, "atpt", [128, KCN * QB])
            khh = ctx.sbt(st, "atkh", [64, S], BF16)
            khl = ctx.sbt(st, "atkl", [64, S], BF16)
            qf = ctx.sbt(st, "atqf", [64, QB])
            qh = ctx.sbt(st, "atqh", [64, QB], BF16)
            ql = ctx.sbt(st, "atql", [64, QB], BF16)
            vau = ctx.sbt(st, "atva", [128, KCN * 65])
            zmv = ctx.sbt(st, "atzm", [64, QB])
            dnb = ctx.sbt(st, "atdn", [64, QB])
            pstg = ctx.sbt(st, "atpg", [128, QB])
            zsb = ctx.sbt(st, "atz", [128, DC * QB])
            zhb = ctx.sbt(st, "atzh", [128, DC * QB], BF16)
            zlb = ctx.sbt(st, "atzl", [128, DC * QB], BF16)
            poh = [ctx.sbt(st, "atph", [128, D], BF16) for _ in range(2)]
            pol = [ctx.sbt(st, "atpl", [128, D], BF16) for _ in range(2)]
            xcb = ctx.sbt(st, "atxc", [128, QB])
            etm = ctx.sbt(st, "atet", [128, QB])
            lhc = ctx.sbt(st, "atlc", [128, QB], BF16)
            llc = ctx.sbt(st, "atl2", [128, QB], BF16)
            msb = ctx.sbt(st, "atms", [128, KCN * QB], BF16) if causal else None

            def pZz_vec(e):
                for j in range(4):
                    e.tensor_scalar_mul(pstg[:, 128 * j:128 * (j + 1)],
                                        sb_ident[:], 0.0)

            ctx.phase({"vector": pZz_vec})

            for qb in range(NQB):
                if causal:
                    def pMl_sync(e, qb=qb):
                        for kc in range(KCN):
                            ctx.ld(e, msb[:, QB * kc:QB * (kc + 1)],
                                   inp["lmask"][(qb * KCN + kc) * 128:
                                                (qb * KCN + kc + 1) * 128, :])
                        e.wait_ge(sem["DS"], ctx.cnt["DS"])

                    ctx.phase({"sync": pMl_sync})

                for h in range(H):
                    ct0, cs0, cv0, cg0 = (ctx.cnt[k]
                                          for k in ["CT", "CS", "CV", "CG"])
                    cv_rec = cv0 + 1 + (KCN if causal else 0) + 1
                    cv_fin = cv_rec + 1

                    def pA_sync(e, qb=qb, h=h):
                        ctx.ld(e, qf[:],
                               Qt[64 * h:64 * (h + 1), QB * qb:QB * (qb + 1)])
                        ctx.ld(e, khh[:, 0:T], kpout[64 * h:64 * (h + 1), 0:T])
                        ctx.ld(e, khh[:, T:S],
                               kpout[D + 64 * h:D + 64 * (h + 1), 0:T])
                        ctx.ld(e, khl[:, 0:T],
                               kpout[64 * h:64 * (h + 1), T:2 * T])
                        ctx.ld(e, khl[:, T:S],
                               kpout[D + 64 * h:D + 64 * (h + 1), T:2 * T])
                        for kc in range(KCN):
                            ctx.ld(e, vau[:, 65 * kc:65 * (kc + 1)],
                                   vaout[128 * kc:128 * (kc + 1),
                                         65 * h:65 * (h + 1)], slow=True)
                        e.wait_ge(sem["DS"], ctx.cnt["DS"])
                        e.nop().then_inc(sem["CG"], 1)
                        ctx.inc("CG")

                    def pA_vec(e, ct0=ct0, cs0=cs0, cg0=cg0, cv0=cv0,
                               cv_rec=cv_rec):
                        e.wait_ge(sem["CG"], cg0 + 1)
                        e.tensor_copy(qh[:], qf[:])
                        e.tensor_tensor(ql[:], qf[:], qh[:], ALU.subtract)\
                            .then_inc(sem["CV"], 1)
                        ctx.inc("CV")
                        if causal:
                            for kc in range(KCN):
                                e.wait_ge(sem["CS"], cs0 + kc + 1)
                                e.tensor_tensor(PT[:, QB * kc:QB * (kc + 1)],
                                                PT[:, QB * kc:QB * (kc + 1)],
                                                msb[:, QB * kc:QB * (kc + 1)],
                                                ALU.mult)\
                                    .then_inc(sem["CV"], 1)
                                ctx.inc("CV")
                        e.wait_ge(sem["CT"], ct0 + KCN + 1)
                        dve_delay(e)
                        if causal:
                            e.tensor_scalar_add(pstg[64:65, 0:QB],
                                                ps[2][64:65, 0:QB], 1e-8)
                            dve_delay(e)
                            e.reciprocal(pstg[64:65, 0:QB],
                                         pstg[64:65, 0:QB])\
                                .then_inc(sem["CV"], 1)
                        else:
                            e.reciprocal(pstg[64:65, 0:QB],
                                         ps[2][64:65, 0:QB])\
                                .then_inc(sem["CV"], 1)
                        ctx.inc("CV")
                        e.wait_ge(sem["CT"], ct0 + KCN + 2)
                        dve_delay(e)
                        e.tensor_copy(dnb[0:64, 0:QB], ps[3][0:64, 0:QB])
                        e.tensor_tensor(zmv[0:64, 0:QB], ps[2][0:64, 0:QB],
                                        dnb[0:64, 0:QB], ALU.mult)\
                            .then_inc(sem["CV"], 1)
                        ctx.inc("CV")

                    def pA_tens(e, ct0=ct0, cs0=cs0, cv0=cv0, cv_rec=cv_rec):
                        e.wait_ge(sem["CV"], cv0 + 1)
                        for kc in range(KCN):
                            if kc >= 2:
                                e.wait_ge(sem["CS"], cs0 + (kc - 1))
                            bk = kc % 2
                            lt_h = khh[:, 128 * kc:128 * (kc + 1)]
                            lt_l = khl[:, 128 * kc:128 * (kc + 1)]
                            e.matmul(ps[bk][:, 0:QB], lt_h, qh[:],
                                     start=True, stop=False)
                            e.matmul(ps[bk][:, 0:QB], lt_h, ql[:],
                                     start=False, stop=False)
                            e.matmul(ps[bk][:, 0:QB], lt_l, qh[:],
                                     start=False, stop=True)\
                                .then_inc(sem["CT"], 1)
                            ctx.inc("CT")
                        if causal:
                            e.wait_ge(sem["CV"], cv0 + 1 + KCN)
                        else:
                            e.wait_ge(sem["CS"], cs0 + KCN)
                        for kc in range(KCN):
                            mm = e.matmul(ps[2][0:65, 0:QB],
                                          vau[:, 65 * kc:65 * (kc + 1)],
                                          PT[:, QB * kc:QB * (kc + 1)],
                                          start=(kc == 0),
                                          stop=(kc == KCN - 1))
                            if kc == KCN - 1:
                                mm.then_inc(sem["CT"], 1)
                        ctx.inc("CT")
                        e.wait_ge(sem["CV"], cv_rec)
                        e.matmul(ps[3][0:64, 0:QB], sb_onesq[:, 0:64],
                                 pstg[:, 0:QB], start=True, stop=True)\
                            .then_inc(sem["CT"], 1)
                        ctx.inc("CT")

                    def pA_scal(e, ct0=ct0):
                        for kc in range(KCN):
                            e.wait_ge(sem["CT"], ct0 + kc + 1)
                            act_delay(e)
                            e.activation(PT[:, QB * kc:QB * (kc + 1)],
                                         ps[kc % 2][:, 0:QB], ACTF.Exp,
                                         scale=0.125)\
                                .then_inc(sem["CS"], 1)
                            ctx.inc("CS")

                    def pA_gps(e, h=h, cv_fin=cv_fin):
                        e.wait_ge(sem["CV"], cv_fin)
                        w = ctx.ld(e, zsb[64 * (h % 2):64 * (h % 2) + 64,
                                          QB * (h // 2):QB * (h // 2) + QB],
                                   zmv[0:64, 0:QB], q="DG")
                        e.wait_ge(sem["DG"], w)

                    ctx.phase({"sync": pA_sync, "vector": pA_vec,
                               "tensor": pA_tens, "scalar": pA_scal,
                               "gpsimd": pA_gps},
                              order=["sync", "vector", "tensor", "scalar",
                                     "gpsimd"])

                def pZs_vec(e):
                    for dc in range(DC):
                        cs_ = slice(QB * dc, QB * (dc + 1))
                        e.tensor_copy(zhb[:, cs_], zsb[:, cs_])
                        e.tensor_tensor(zlb[:, cs_], zsb[:, cs_], zhb[:, cs_],
                                        ALU.subtract)

                ctx.phase({"vector": pZs_vec})

                ctO = ctx.cnt["CT"]
                laO, lbO = ctx.cnt["LA"], ctx.cnt["LB"]

                def pO_sync(e, ctO=ctO):
                    for dc in range(DC):
                        i = dc % 2
                        qs = "LA" if i == 0 else "LB"
                        r_ = slice(128 * dc, 128 * (dc + 1))
                        if dc >= 2:
                            e.wait_ge(sem["CT"], ctO + (dc - 1))
                        ctx.ld(e, poh[i][:], pwq[pn[3] + "h"][r_, :], q=qs)
                        ctx.ld(e, pol[i][:], pwq[pn[3] + "l"][r_, :], q=qs)
                    e.wait_ge(sem["LA"], ctx.cnt["LA"])
                    e.wait_ge(sem["LB"], ctx.cnt["LB"])

                def pO_tens(e, ctO=ctO, laO=laO, lbO=lbO):
                    for dc in range(DC):
                        i = dc % 2
                        e.wait_ge(sem["LA" if i == 0 else "LB"],
                                  (laO if i == 0 else lbO)
                                  + 32 * (dc // 2 + 1))
                        zh_ = zhb[:, QB * dc:QB * (dc + 1)]
                        zl_ = zlb[:, QB * dc:QB * (dc + 1)]
                        for mb_ in range(8):
                            lth = poh[i][:, 128 * mb_:128 * (mb_ + 1)]
                            ltl = pol[i][:, 128 * mb_:128 * (mb_ + 1)]
                            e.matmul(ps[mb_][:, 0:QB], lth, zh_,
                                     start=(dc == 0), stop=False)
                            e.matmul(ps[mb_][:, 0:QB], lth, zl_,
                                     start=False, stop=False)
                            mm = e.matmul(ps[mb_][:, 0:QB], ltl, zh_,
                                          start=False, stop=(dc == DC - 1))
                            if mb_ == 7:
                                mm.then_inc(sem["CT"], 1)
                        ctx.inc("CT")

                ctx.phase({"sync": pO_sync, "tensor": pO_tens},
                          order=["sync", "tensor"])

                dsE, cvE = ctx.cnt["DS"], ctx.cnt["CV"]
                dgE = ctx.cnt["DG"]
                saE = ctx.cnt["SA"]

                def pE_sync(e, qb=qb, cvE=cvE):
                    if causal:
                        return
                    for mb_ in range(8):
                        if mb_ >= 1:
                            e.wait_ge(sem["CV"], cvE + mb_)
                        r_ = slice(128 * mb_, 128 * (mb_ + 1))
                        cs_ = slice(QB * qb, QB * (qb + 1))
                        ctx.ld(e, lhc[:], LNH[r_, cs_])
                        ctx.ld(e, llc[:], LNL[r_, cs_])

                def pE_gps(e, qb=qb, saE=saE, cvE=cvE):
                    for mb_ in range(8):
                        if mb_ >= 1:
                            e.wait_ge(sem["SA"], saE + 16 * mb_)
                        r_ = slice(128 * mb_, 128 * (mb_ + 1))
                        cs_ = slice(QB * qb, QB * (qb + 1))
                        ctx.ld(e, xcb[:], xsrc[r_, cs_], q="DG")
                        e.wait_ge(sem["CV"], cvE + mb_ + 1)
                        ctx.ld(e, xdst[r_, cs_], xcb[:], q="SA")
                    e.wait_ge(sem["SA"], ctx.cnt["SA"])

                def pE_vec(e, dsE=dsE, dgE=dgE):
                    dve_delay(e)
                    for mb_ in range(8):
                        e.wait_ge(sem["DG"], dgE + 16 * (mb_ + 1))
                        if causal:
                            e.tensor_tensor(xcb[:], xcb[:], ps[mb_][:, 0:QB],
                                            ALU.add).then_inc(sem["CV"], 1)
                        else:
                            e.wait_ge(sem["DS"], dsE + 32 * (mb_ + 1))
                            e.tensor_tensor(etm[:], lhc[:], llc[:], ALU.add)
                            e.tensor_tensor(etm[:], etm[:], ps[mb_][:, 0:QB],
                                            ALU.add)
                            e.tensor_tensor(xcb[:], xcb[:], etm[:], ALU.add)\
                                .then_inc(sem["CV"], 1)
                        ctx.inc("CV")

                ctx.phase({"sync": pE_sync, "gpsimd": pE_gps,
                           "vector": pE_vec},
                          order=["sync", "gpsimd", "vector"])

    def halo_gather():
        with ExitStack() as st:
            hb = ctx.sbt(st, "hgb", [128, 2])
            hr = ctx.sbt(st, "hgr", [1, D])

            def pHG_sync(e):
                for dc in range(DC):
                    r_ = slice(128 * dc, 128 * (dc + 1))
                    ctx.ld(e, hb[:, 0:1], Xs[r_, 0:1], slow=True)
                    w = ctx.ld(e, hb[:, 1:2], Xs[r_, T - 1:T], slow=True)
                    e.wait_ge(sem["DS"], w)
                    w = ctx.ld(e, bgin[r_, :], hb[:], q="DA")
                    e.wait_ge(sem["DA"], w)

            ctx.phase({"sync": pHG_sync})

            def pHG_gps(e):
                e.collective_compute("AllGather", ALU.bypass, ins=[bgin],
                                     outs=[bgout], replica_groups=replica)\
                    .then_inc(sem["CG"], 1)
                ctx.inc("CG")
                e.wait_ge(sem["CG"], ctx.cnt["CG"])

            ctx.phase({"gpsimd": pHG_gps})

            def pHG2_sync(e):
                w = ctx.ld(e, hr[:], bgout[0:D, 1:2].transpose([1, 0]),
                           slow=True)
                e.wait_ge(sem["DS"], w)
                w = ctx.ld(e, HALO3[0:1, :], hr[:], q="DA")
                e.wait_ge(sem["DA"], w)
                w = ctx.ld(e, hr[:], bgout[D:2 * D, 0:1].transpose([1, 0]),
                           slow=True)
                e.wait_ge(sem["DS"], w)
                w = ctx.ld(e, HALO3[1:2, :], hr[:], q="DA")
                e.wait_ge(sem["DA"], w)

            ctx.phase({"sync": pHG2_sync})

    def l1_ffn(xsrc, xdst):
        with ExitStack() as st:
            scr = {k: ctx.sbt(st, "f1" + k, [128, T]) for k in "atrvqus"}
            scr["m"] = ctx.sbt(st, "f1m", [128, 2 * (T // 16)])
            lqh = ctx.sbt(st, "f1lh", [128, DC * T], BF16)
            lql = ctx.sbt(st, "f1ll", [128, DC * T], BF16)

            with ExitStack() as st1:
                xc = ctx.sbt(st1, "f1x", [128, T])
                x2 = ctx.sbt(st1, "f1x2", [128, T])
                mb = ctx.sbt(st1, "f1mb", [128, T])
                rb = ctx.sbt(st1, "f1rb", [128, T])
                lnq = ctx.sbt(st1, "f1lq", [128, T])
                ln_stats(xsrc, mb, rb, xc, x2)

                ds1, cv1 = ctx.cnt["DS"], ctx.cnt["CV"]

                def pQ_sync(e):
                    for dc in range(DC):
                        if dc >= 1:
                            e.wait_ge(sem["CV"], cv1 + dc)
                        ctx.ld(e, xc[:], xsrc[128 * dc:128 * (dc + 1), :])

                def pQ_vec(e):
                    for dc in range(DC):
                        e.wait_ge(sem["DS"], ds1 + 16 * (dc + 1))
                        e.tensor_tensor(x2[:], xc[:], mb[:], ALU.subtract)
                        e.tensor_tensor(x2[:], x2[:], rb[:], ALU.mult)
                        fp4_qdq(e, x2, lnq, scr, T, dve_delay)
                        cs_ = slice(T * dc, T * (dc + 1))
                        e.tensor_copy(lqh[:, cs_], lnq[:])
                        e.tensor_tensor(lql[:, cs_], lnq[:], lqh[:, cs_],
                                        ALU.subtract).then_inc(sem["CV"], 1)
                        ctx.inc("CV")

                ctx.phase({"sync": pQ_sync, "vector": pQ_vec},
                          order=["sync", "vector"])

            with ExitStack() as st2:
                wgbh = [ctx.sbt(st2, "f1gh", [128, D], BF16) for _ in range(2)]
                wgbl = [ctx.sbt(st2, "f1gl", [128, D], BF16) for _ in range(2)]
                wubh = [ctx.sbt(st2, "f1uh", [128, D], BF16) for _ in range(2)]
                wubl = [ctx.sbt(st2, "f1ul", [128, D], BF16) for _ in range(2)]
                sgb = ctx.sbt(st2, "f1sg", [128, QB])
                hbf = ctx.sbt(st2, "f1hb", [128, T])
                hfq = ctx.sbt(st2, "f1hq", [128, T])
                hhb = [ctx.sbt(st2, "f1hh", [128, T], BF16) for _ in range(2)]
                hlb = [ctx.sbt(st2, "f1hl", [128, T], BF16) for _ in range(2)]

                ctG, cvG2, csG2 = ctx.cnt["CT"], ctx.cnt["CV"], ctx.cnt["CS"]
                laG, lbG = ctx.cnt["LA"], ctx.cnt["LB"]
                saG, sbG = ctx.cnt["SA"], ctx.cnt["SB"]

                def pG_sync(e):
                    for gc in range(GH):
                        i = gc % 2
                        qs = "LA" if i == 0 else "LB"
                        cg_ = slice(128 * gc, 128 * (gc + 1))
                        if gc >= 2:
                            e.wait_ge(sem["CT"], ctG + (gc - 1))
                        for dc in range(DC):
                            r_ = slice(128 * dc, 128 * (dc + 1))
                            w_ = slice(128 * dc, 128 * (dc + 1))
                            ctx.ld(e, wgbh[i][:, w_], wgqh[r_, cg_], q=qs)
                            ctx.ld(e, wgbl[i][:, w_], wgql[r_, cg_], q=qs)
                            ctx.ld(e, wubh[i][:, w_], wuqh[r_, cg_], q=qs)
                            ctx.ld(e, wubl[i][:, w_], wuql[r_, cg_], q=qs)
                    e.wait_ge(sem["LA"], ctx.cnt["LA"])
                    e.wait_ge(sem["LB"], ctx.cnt["LB"])

                def pG_tens(e):
                    for gc in range(GH):
                        i = gc % 2
                        e.wait_ge(sem["LA" if i == 0 else "LB"],
                                  (laG if i == 0 else lbG)
                                  + 16 * 32 * (gc // 2 + 1))
                        if gc >= 1:
                            e.wait_ge(sem["CV"], cvG2 + 5 * (gc - 1) + 4)
                        for dc in range(DC):
                            w_ = slice(128 * dc, 128 * (dc + 1))
                            for tb in range(NQB):
                                r_h = lqh[:, T * dc + QB * tb:
                                          T * dc + QB * (tb + 1)]
                                r_l = lql[:, T * dc + QB * tb:
                                          T * dc + QB * (tb + 1)]
                                for u_ in range(2):
                                    wh_ = (wgbh if u_ == 0 else wubh)[i][:, w_]
                                    wl_ = (wgbl if u_ == 0 else wubl)[i][:, w_]
                                    bk = tb + 4 * u_
                                    e.matmul(ps[bk][:, 0:QB], wh_, r_h,
                                             start=(dc == 0), stop=False)
                                    e.matmul(ps[bk][:, 0:QB], wh_, r_l,
                                             start=False, stop=False)
                                    mm = e.matmul(ps[bk][:, 0:QB], wl_, r_h,
                                                  start=False,
                                                  stop=(dc == DC - 1))
                                    if dc == DC - 1 and tb == 3 and u_ == 1:
                                        mm.then_inc(sem["CT"], 1)
                        ctx.inc("CT")

                def pG_scal(e):
                    for gc in range(GH):
                        e.wait_ge(sem["CT"], ctG + gc + 1)
                        for tb in range(NQB):
                            act_delay(e)
                            e.activation(sgb[:], ps[tb][:, 0:QB], ACTF.Silu)\
                                .then_inc(sem["CS"], 1)
                            ctx.inc("CS")
                            e.wait_ge(sem["CV"], cvG2 + 5 * gc + tb + 1)
                        e.wait_ge(sem["CV"], cvG2 + 5 * gc + 5)
                        i = gc % 2
                        qs = "SA" if i == 0 else "SB"
                        cg_ = slice(128 * gc, 128 * (gc + 1))
                        ctx.ld(e, hqh[cg_, :], hhb[i][:], q=qs)
                        ctx.ld(e, hql[cg_, :], hlb[i][:], q=qs)
                    e.wait_ge(sem["SA"], ctx.cnt["SA"])
                    e.wait_ge(sem["SB"], ctx.cnt["SB"])

                def pG_vec(e):
                    for gc in range(GH):
                        i = gc % 2
                        for tb in range(NQB):
                            e.wait_ge(sem["CS"], csG2 + 4 * gc + tb + 1)
                            dve_delay(e)
                            e.tensor_tensor(hbf[:, QB * tb:QB * (tb + 1)],
                                            sgb[:], ps[4 + tb][:, 0:QB],
                                            ALU.mult).then_inc(sem["CV"], 1)
                            ctx.inc("CV")
                        if gc >= 2:
                            e.wait_ge(sem["SA" if i == 0 else "SB"],
                                      (saG if i == 0 else sbG)
                                      + 32 * ((gc - 2) // 2 + 1))
                        fp4_qdq(e, hbf, hfq, scr, T, dve_delay)
                        e.tensor_copy(hhb[i][:], hfq[:])
                        e.tensor_tensor(hlb[i][:], hfq[:], hhb[i][:],
                                        ALU.subtract).then_inc(sem["CV"], 1)
                        ctx.inc("CV")

                ctx.phase({"sync": pG_sync, "tensor": pG_tens,
                           "scalar": pG_scal, "vector": pG_vec},
                          order=["sync", "tensor", "scalar", "vector"])

            with ExitStack() as st3:
                wdbh = [ctx.sbt(st3, "f1dh", [128, D], BF16) for _ in range(2)]
                wdbl = [ctx.sbt(st3, "f1dl", [128, D], BF16) for _ in range(2)]
                hqbh = [ctx.sbt(st3, "f1qh", [128, QB], BF16)
                        for _ in range(2)]
                hqbl = [ctx.sbt(st3, "f1ql", [128, QB], BF16)
                        for _ in range(2)]
                xcb = ctx.sbt(st3, "f1xc", [128, QB])

                for tb in range(NQB):
                    ctD = ctx.cnt["CT"]
                    laD, lbD = ctx.cnt["LA"], ctx.cnt["LB"]
                    cvD = ctx.cnt["CV"]
                    dgD = ctx.cnt["DG"]
                    saD = [ctx.cnt["SA"]]

                    def pD_sync(e, tb=tb, ctD=ctD):
                        for fc in range(GH):
                            i = fc % 2
                            qs = "LA" if i == 0 else "LB"
                            r_ = slice(128 * fc, 128 * (fc + 1))
                            if fc >= 2:
                                e.wait_ge(sem["CT"], ctD + (fc - 1))
                            ctx.ld(e, wdbh[i][:], wdqh[r_, :], q=qs)
                            ctx.ld(e, wdbl[i][:], wdql[r_, :], q=qs)
                            ctx.ld(e, hqbh[i][:],
                                   hqh[r_, QB * tb:QB * (tb + 1)], q=qs)
                            ctx.ld(e, hqbl[i][:],
                                   hql[r_, QB * tb:QB * (tb + 1)], q=qs)
                        e.wait_ge(sem["LA"], ctx.cnt["LA"])
                        e.wait_ge(sem["LB"], ctx.cnt["LB"])

                    def pD_tens(e, ctD=ctD, laD=laD, lbD=lbD):
                        for fc in range(GH):
                            i = fc % 2
                            e.wait_ge(sem["LA" if i == 0 else "LB"],
                                      (laD if i == 0 else lbD)
                                      + 64 * (fc // 2 + 1))
                            for mb_ in range(8):
                                lth = wdbh[i][:, 128 * mb_:128 * (mb_ + 1)]
                                ltl = wdbl[i][:, 128 * mb_:128 * (mb_ + 1)]
                                e.matmul(ps[mb_][:, 0:QB], lth, hqbh[i][:],
                                         start=(fc == 0), stop=False)
                                e.matmul(ps[mb_][:, 0:QB], lth, hqbl[i][:],
                                         start=False, stop=False)
                                mm = e.matmul(ps[mb_][:, 0:QB], ltl,
                                              hqbh[i][:], start=False,
                                              stop=(fc == GH - 1))
                                if mb_ == 7:
                                    mm.then_inc(sem["CT"], 1)
                            ctx.inc("CT")

                    def pD_gps(e, tb=tb, cvD=cvD, saD=saD):
                        for mb_ in range(8):
                            if mb_ >= 1:
                                e.wait_ge(sem["SA"], saD[0] + 16 * mb_)
                            r_ = slice(128 * mb_, 128 * (mb_ + 1))
                            cs_ = slice(QB * tb, QB * (tb + 1))
                            ctx.ld(e, xcb[:], xsrc[r_, cs_], q="DG")
                            e.wait_ge(sem["CV"], cvD + mb_ + 1)
                            ctx.ld(e, xdst[r_, cs_], xcb[:], q="SA")
                        e.wait_ge(sem["SA"], ctx.cnt["SA"])

                    def pD_vec(e, ctD=ctD, dgD=dgD):
                        e.wait_ge(sem["CT"], ctD + GH)
                        dve_delay(e)
                        for mb_ in range(8):
                            e.wait_ge(sem["DG"], dgD + 16 * (mb_ + 1))
                            e.scalar_tensor_tensor(xcb[:], ps[mb_][:, 0:QB],
                                                   0.5, xcb[:], ALU.mult,
                                                   ALU.add)\
                                .then_inc(sem["CV"], 1)
                            ctx.inc("CV")

                    ctx.phase({"sync": pD_sync, "tensor": pD_tens,
                               "gpsimd": pD_gps, "vector": pD_vec},
                              order=["sync", "tensor", "gpsimd", "vector"])

    if KPH >= 2:
        if os.environ.get("SKIPBN") != "1":
            bitnet_quant(inp["w1T"], w1q, D, FF0, 0)
            bitnet_quant(inp["w2T"], w2q, FF0, D, 2)
        if KPH >= 3:
            hilo_split(["pq0", "pk0", "pv0", "po0"])
        if KPH >= 6:
            hilo_split(["pq1", "pk1", "pv1", "po1"])
        if KPH >= 5:
            fp4_weights(inp["wgT"], wgqh, wgql, D, FF1)
            fp4_weights(inp["wuT"], wuqh, wuql, D, FF1)
            fp4_weights(inp["wdT"], wdqh, wdql, FF1, D)
        l0_ffn(inp["xT"], inp["xhalo"], False, Xs)
        if KPH >= 3:
            attn(0, Xs, Xs)
        if KPH >= 4:
            halo_gather()
            l0_ffn(Xs, HALO3, True, Xs)
        if KPH >= 5:
            l1_ffn(Xs, Xs)
        if KPH >= 6:
            attn(1, Xs, Xs)
        if KPH >= 7:
            l1_ffn(Xs, Xs)
    else:
        def pP_sync(e):
            w = ctx.ld(e, Xs, inp["xT"])
            e.wait_ge(sem["DS"], w)

        ctx.phase({"sync": pP_sync})

    with ExitStack() as st:
        ob = ctx.sbt(st, "ob", [128, T])

        def pZ_sync(e):
            for dc in range(DC):
                w = ctx.ld(e, ob[:], Xs[128 * dc:128 * (dc + 1), :])
                e.wait_ge(sem["DS"], w)
                w2_ = ctx.ld(e, yT[128 * dc:128 * (dc + 1), :], ob[:])
                e.wait_ge(sem["DS"], w2_)

        ctx.phase({"sync": pZ_sync})

    return nc, stack


def kernel(**inputs):
    in_maps = _prep_inputs(inputs)
    nc, stack = build_program()
    stack.close()
    res = run_bass_kernel_spmd(nc, in_maps, core_ids=list(range(NC)))
    out = np.empty((B, S, D), np.float32)
    for c in range(NC):
        b, half = c // 2, c % 2
        out[b, half * T:(half + 1) * T, :] = np.asarray(res.results[c]["yT"]).T
    return out
